# version 06
# baseline (speedup 1.0000x reference)
import math
import numpy as np
import concourse.bass as bass
import concourse.mybir as mybir
from concourse.bass_utils import run_bass_kernel_spmd

F32 = mybir.dt.float32
BF16 = mybir.dt.bfloat16
F8 = mybir.dt.float8e4
AF = mybir.ActivationFunctionType
ALU = mybir.AluOpType

NB = 2
T = 2048
D = 1024
NCH = 8
PAD = 8
TP = T + PAD
CHK = T // 128
H0 = 16
DH0 = 64
WIN = 512
HF = 8
DHF = 64
HID = 2730
HIDP = 2816
NHC = HIDP // 128
LN_EPS = 1e-5
RMS_EPS = 1.1920929e-07
ENGS = ("tensor", "vector", "scalar", "gpsimd", "sync")
L0A_LEVEL = 99
L0A_NSTEP = 99
L1_LEVEL = 99
L1_NGRP = 2
L1_NB = 2
L1_NCC = 99


def xcol(b, ch, t0=0):
    return (b * NCH + ch) * T + t0


def ncol(b, ch, t0=0):
    return (b * NCH + ch) * TP + PAD + t0


class Sem:
    def __init__(self, nc, name):
        self.h = nc.alloc_semaphore(name)
        self.v = 0
        self.name = name


class Ops:

    def __init__(self, nc):
        self.nc = nc
        self.q = {e: [] for e in ENGS}
        self.sems = {}

    def sem(self, name):
        if name not in self.sems:
            self.sems[name] = Sem(self.nc, name)
        return self.sems[name]

    def on(self, eng, fn):
        self.q[eng].append(fn)

    def wait(self, eng, sem, val):
        self.q[eng].append(lambda e: e.wait_ge(sem.h, val))

    def sig(self, eng, sem, fn, n=1):
        sem.v += n
        val = sem.v

        def run(e):
            ins = fn(e)
            assert ins is not None, "sig() closure must return the instruction"
            ins.then_inc(sem.h, n)
        self.q[eng].append(run)
        return val

    def flush(self, block):
        for eng in ENGS:
            if not self.q[eng]:
                continue
            ops = self.q[eng]
            self.q[eng] = []

            def section(e, ops=ops):
                for fn in ops:
                    fn(e)
            getattr(block, eng)(section)

    def barrier(self, tag):
        s = self.sem("bar_" + tag)
        engs = ("tensor", "vector", "scalar", "gpsimd")
        base = s.v
        for eng in engs:
            self.sig(eng, s, lambda e: e.drain())
        for eng in engs + ("sync",):
            self.wait(eng, s, base + len(engs))


class Ctx:
    def __init__(self, nc):
        self.nc = nc
        self.ops = Ops(nc)
        self.t = {}
        for nm in ("psA", "psB", "psC", "psD", "psF", "psH"):
            self.ps(nm, [128, 512], F32)
        self.ps("psG", [128, 1024], BF16)
        self.ps("psI", [128, 1024], BF16)

    def sb(self, name, shape, dtype):
        h = self.nc.alloc_sbuf_tensor(name, shape, dtype)
        self.t[name] = h
        return h

    def ps(self, name, shape, dtype=F32):
        h = self.nc.alloc_psum_tensor(name, shape, dtype)
        self.t[name] = h
        return h

    def addr_of(self, name):
        import concourse.mybir as mb
        for a in self.nc.m.functions[0].allocations:
            if isinstance(a, mb.MemoryLocationSet):
                for ml in a.memorylocations:
                    if ml.name == name or ml.name.startswith(name + "_"):
                        return ml.addr
        raise KeyError(name)

    def alias(self, name, shape, dtype, host, off_bytes=0):
        base = self.addr_of(self.t[host].name if host in self.t else host)
        h = self.nc.alloc_sbuf_tensor_at(name, shape, dtype,
                                         offset=base + off_bytes)
        self.t[name] = h
        return h


def emit_load(c, dram):
    nc, ops = c.nc, c.ops
    x_sb = c.sb("x_sb", [128, NB * NCH * T], BF16)
    xn_sb = c.sb("xn_sb", [128, NB * NCH * TP], BF16)
    s_in = ops.sem("s_in")

    ident = c.sb("ident", [128, 128], BF16)
    identf = c.sb("identf", [128, 128], F32)
    iota_pr = c.sb("iota_pr", [128, 128], F32)
    iota_p1 = c.sb("iota_p1", [128, 1], F32)
    iota_rev = c.sb("iota_rev", [128, 1], F32)
    m_low = c.sb("m_low", [128, 128], BF16)
    m_upp = c.sb("m_upp", [128, 128], BF16)
    m_til = c.sb("m_til", [128, 128], BF16)
    m_tiu = c.sb("m_tiu", [128, 128], BF16)
    ones_col = c.sb("ones_col", [128, 1], BF16)
    ones_row = c.sb("ones_row", [1, 128], BF16)
    ones_rowf = c.sb("ones_rowf", [1, 128], F32)

    def g_init(g):
        n = 0
        for tile, const in ((ones_col, 1.0), (ones_row, 1.0), (ones_rowf, 1.0)):
            g.memset(tile[:], const).then_inc(s_pre.h, 1)
            n += 1
        g.iota(iota_pr[:], pattern=[[-1, 128]], base=0, channel_multiplier=1,
               allow_small_or_imprecise_dtypes=True).then_inc(s_pre.h, 1)
        g.iota(iota_p1[:], pattern=[[0, 1]], base=1, channel_multiplier=1,
               allow_small_or_imprecise_dtypes=True).then_inc(s_pre.h, 1)
        g.iota(iota_rev[:], pattern=[[0, 1]], base=127, channel_multiplier=-1,
               allow_small_or_imprecise_dtypes=True).then_inc(s_pre.h, 1)
        g.memset(xn_sb[:], 0.0).then_inc(s_pre.h, 1)

    def v_masks(v):
        v.tensor_scalar(out=ident[:], in0=iota_pr[:], scalar1=0.0, scalar2=None,
                        op0=ALU.is_equal)
        v.tensor_scalar(out=identf[:], in0=iota_pr[:], scalar1=0.0, scalar2=None,
                        op0=ALU.is_equal)
        v.tensor_scalar(out=m_low[:], in0=iota_pr[:], scalar1=0.0, scalar2=None,
                        op0=ALU.is_gt)
        v.tensor_scalar(out=m_upp[:], in0=iota_pr[:], scalar1=0.0, scalar2=None,
                        op0=ALU.is_lt)
        v.tensor_scalar(out=m_til[:], in0=iota_pr[:], scalar1=0.0, scalar2=None,
                        op0=ALU.is_ge)
        return v.tensor_scalar(out=m_tiu[:], in0=iota_pr[:], scalar1=0.0,
                               scalar2=None, op0=ALU.is_le)

    s_pre = ops.sem("s_pre")
    s_pre.v += 7
    ops.on("gpsimd", g_init)
    ops.wait("vector", s_pre, s_pre.v)
    ops.sig("vector", s_pre, v_masks)
    for eng in ("tensor", "scalar", "sync", "gpsimd", "vector"):
        ops.wait(eng, s_pre, s_pre.v)

    xstg = c.sb("xstg", [128, 2 * 1024], F32)
    ps_t0, ps_t1 = c.t["psA"], c.t["psB"]
    s_tr = ops.sem("s_tr")
    s_tev = ops.sem("s_tev")
    s_ld = [ops.sem("s_ld0"), ops.sem("s_ld1")]
    x_in = dram["x_in"]
    items = [(b, tc) for b in range(NB) for tc in range(CHK)]
    for idx, (b, tc) in enumerate(items):
        buf = idx % 2
        if idx >= 2:
            ops.wait("sync", s_tr, idx - 1)
        ops.sig("sync", s_ld[buf],
                lambda e, b=b, tc=tc, buf=buf: e.dma_start(
                    out=xstg[:, buf * 1024:(buf + 1) * 1024],
                    in_=x_in[b, tc * 128:(tc + 1) * 128, :]), n=16)
        ops.wait("tensor", s_ld[buf], s_ld[buf].v)
        if idx >= 1:
            ops.wait("tensor", s_tev, idx - 1 + 1)

        def mm_tr(t, buf=buf):
            last = None
            for ch in range(NCH):
                ps = (ps_t0 if ch % 2 == 0 else ps_t1)
                last = t.transpose(ps[:, (ch // 2) * 128:(ch // 2) * 128 + 128],
                                   xstg[:, buf * 1024 + ch * 128:buf * 1024 + ch * 128 + 128],
                                   identf[:])
            return last
        ops.sig("tensor", s_tr, mm_tr)
        ops.wait("vector", s_tr, idx + 1)

        def v_ev(v, b=b, tc=tc):
            last = None
            for ch in range(NCH):
                ps = (ps_t0 if ch % 2 == 0 else ps_t1)
                last = v.tensor_copy(
                    x_sb[:, xcol(b, ch, tc * 128):xcol(b, ch, tc * 128) + 128],
                    ps[:, (ch // 2) * 128:(ch // 2) * 128 + 128])
            return last
        ops.sig("vector", s_tev, v_ev)
    for eng in ("tensor", "vector", "scalar", "gpsimd"):
        ops.wait(eng, s_tev, s_tev.v)


def emit_top_ln(c, tag):
    nc, ops = c.nc, c.ops
    x_sb, xn_sb = c.t["x_sb"], c.t["xn_sb"]
    ones_col, ones_rowf = c.t["ones_col"], c.t["ones_rowf"]
    ones_row = c.t["ones_row"]

    if "ln_xsq" not in c.t:
        c.sb("ln_xsq", [128, NCH * 512], BF16)
    xsq = c.t["ln_xsq"]
    if "ln_stats" not in c.t:
        c.sb("ln_stats", [1, 1024], BF16)
    stats = c.t["ln_stats"]
    ps_s, ps_q = c.t["psA"], c.t["psB"]
    ps_mu, ps_rs = c.t["psD"], c.t["psF"]

    s_sq = ops.sem(tag + "sq")
    s_mm = ops.sem(tag + "mm")
    s_st = ops.sem(tag + "st")
    s_rs = ops.sem(tag + "rs")
    s_ev = ops.sem(tag + "ev")

    TBL = min(512, T)
    items = [(b, tb) for b in range(NB) for tb in range(T // TBL)]
    for idx, (b, tb) in enumerate(items):
        if idx >= 1:
            ops.wait("scalar", s_mm, 2 * (idx - 1) + 1)
        xin3 = x_sb[:].rearrange("p (n t) -> p n t", t=T)[
            :, b * NCH:(b + 1) * NCH, tb * TBL:(tb + 1) * TBL]
        ops.sig("scalar", s_sq,
                lambda s, xin3=xin3: s.activation(
                    out=xsq[:].rearrange("p (n t) -> p n t", t=512)[:, :, 0:TBL],
                    in_=xin3, func=AF.Square))
        ops.wait("tensor", s_sq, idx + 1)
        if idx >= 1:
            ops.wait("tensor", s_st, idx)

        def mm_sums(t, b=b, tb=tb):
            for chx in range(NCH):
                xin = x_sb[:, xcol(b, chx, tb * TBL):xcol(b, chx, tb * TBL) + TBL]
                t.matmul(ps_s[0:1, 0:TBL], ones_col[:], xin,
                         start=(chx == 0), stop=(chx == NCH - 1))
                last = t.matmul(ps_q[0:1, 0:TBL], ones_col[:],
                                xsq[:, chx * 512:chx * 512 + TBL],
                                start=(chx == 0), stop=(chx == NCH - 1))
            return last
        ops.sig("tensor", s_mm, mm_sums)
        ops.wait("vector", s_mm, 2 * idx + 1)

        def v_stats(v):
            v.tensor_scalar(out=stats[0:1, 0:TBL], in0=ps_s[0:1, 0:TBL],
                            scalar1=-1.0 / D, scalar2=None, op0=ALU.mult)
            v.drain()
            v.tensor_tensor(out=stats[0:1, 512:512 + TBL], in0=stats[0:1, 0:TBL],
                            in1=stats[0:1, 0:TBL], op=ALU.mult)
            v.drain()
            v.scalar_tensor_tensor(out=stats[0:1, 512:512 + TBL], in0=ps_q[0:1, 0:TBL],
                                   scalar=1.0 / D, in1=stats[0:1, 512:512 + TBL],
                                   op0=ALU.mult, op1=ALU.subtract)
            v.drain()
            v.tensor_scalar(out=stats[0:1, 512:512 + TBL], in0=stats[0:1, 512:512 + TBL],
                            scalar1=float(LN_EPS), scalar2=None, op0=ALU.add)
            v.drain()
            with nc.allow_low_precision(reason="bf16 LN stats"):
                return v.reciprocal(stats[0:1, 512:512 + TBL],
                                    stats[0:1, 512:512 + TBL])
        ops.sig("vector", s_st, v_stats)
        ops.wait("scalar", s_st, idx + 1)
        def a_sqrt(s):
            with nc.allow_low_precision(reason="bf16 LN stats"):
                return s.sqrt(stats[0:1, 512:512 + TBL], stats[0:1, 512:512 + TBL])
        ops.sig("scalar", s_rs, a_sqrt)
        ops.wait("tensor", s_rs, idx + 1)
        if idx >= 1:
            ops.wait("tensor", s_ev, idx)
        def mm_bcast(t):
            t.matmul(ps_mu[:, 0:TBL], ones_row[:], stats[0:1, 0:TBL],
                     start=True, stop=True)
            return t.matmul(ps_rs[:, 0:TBL], ones_row[:],
                            stats[0:1, 512:512 + TBL], start=True, stop=True)
        ops.sig("tensor", s_mm, mm_bcast)
        ops.wait("vector", s_mm, 2 * idx + 2)

        def v_norm(v, b=b, tb=tb):
            xin3 = x_sb[:].rearrange("p (n t) -> p n t", t=T)[
                :, b * NCH:(b + 1) * NCH, tb * TBL:(tb + 1) * TBL]
            xout3 = xn_sb[:].rearrange("p (n t) -> p n t", t=TP)[
                :, b * NCH:(b + 1) * NCH, PAD + tb * TBL:PAD + (tb + 1) * TBL]
            mub = ps_mu[:, None, 0:TBL].broadcast_to([128, NCH, TBL])
            rsb = ps_rs[:, None, 0:TBL].broadcast_to([128, NCH, TBL])
            v.tensor_tensor(out=xout3, in0=xin3, in1=mub, op=ALU.add)
            v.drain()
            return v.tensor_tensor(out=xout3, in0=xout3, in1=rsb, op=ALU.mult)
        ops.sig("vector", s_ev, v_norm)


def make_tables():
    half = np.arange(0, DHF, 2, dtype=np.float64)
    inv = 1.0 / 10000.0 ** (half / DHF)
    t = np.arange(T, dtype=np.float64)
    fr = t[:, None] * inv[None, :]
    base = (half + 0.4 * DHF) / (1.4 * DHF)
    pw = (t - T // 2) / float(WIN // 2)
    sc = base[None, :] ** pw[:, None]
    return (np.cos(fr).astype(np.float32), np.sin(fr).astype(np.float32),
            sc.astype(np.float32))


def declare_inputs(nc):
    dram = {}

    def din(name, shape, dtype=F32):
        dram[name] = nc.dram_tensor(name, list(shape), dtype,
                                    kind="ExternalInput").ap()
    din("x_in", (NB, T, D))
    din("W0", (D, D))
    din("k_norm_w", (DH0,))
    din("o_norm_w", (D,))
    din("o_norm_b", (D,))
    din("eta_raw", (H0,))
    din("lam_logit", (H0,))
    din("wqkv", (2, D, 1536))
    din("q_scale", (2, DHF))
    din("k_scale", (2, DHF))
    din("wo", (2, 512, D))
    din("ln_g", (2, D))
    din("ln_b", (2, D))
    din("w1", (2, D, 5460))
    din("b1", (2, 5460))
    din("w2", (2, HID, D))
    din("b2", (2, D))
    din("Wq1", (D, D))
    din("Wk1", (D, D))
    din("Wv1", (D, D))
    din("Wo1", (D, D))
    din("Wf1", (D, D))
    din("bf1", (D,))
    din("t_cs", (T, 32))
    din("t_sn", (T, 32))
    din("t_scale", (T, 32))
    dram["s_w1"] = nc.dram_tensor("s_w1", [2, 44, 128, 1024], BF16,
                                  kind="Internal").ap()
    dram["s_w2"] = nc.dram_tensor("s_w2", [2, 22, 2, 128, 512], BF16,
                                  kind="Internal").ap()
    dram["y"] = nc.dram_tensor("y", [NB, T, D], F32, kind="ExternalOutput").ap()
    return dram


def build_program(stages=None):
    nc = bass.Bass("TRN2", target_bir_lowering=False)
    c = Ctx(nc)
    dram = declare_inputs(nc)
    mode = "full" if stages is None else stages[0]
    with nc.Block() as block:
        emit_load(c, dram)
        emit_top_ln(c, "ln0")
        if mode in ("l0a", "l0", "full"):
            emit_l0_attn(c, dram)
            if mode != "l0a":
                emit_w_scratch(c, dram)
        if mode in ("l0", "full"):
            emit_l0_ffn(c, dram)
        if mode == "full":
            emit_top_ln(c, "ln1")
            emit_l1(c, dram)
        emit_store(c, dram, debug=(mode == "ln"))
        c.ops.flush(block)
    return nc


def emit_store(c, dram, debug=False):
    ops = c.ops
    ident = c.t["ident"]
    y = dram["y"]
    ystg = c.t["xstg"]
    ps_g = c.t["psG"]
    ps_y0 = ps_g[:, 0:512]
    ps_y1 = ps_g[:, 512:1024]
    s_ytr = ops.sem("s_ytr")
    s_yev = ops.sem("s_yev")
    s_outs = [ops.sem("s_out0"), ops.sem("s_out1")]
    ops.barrier("store")
    src_t = c.t["xn_sb"] if debug else c.t["x_sb"]

    def scol(b, ch, t0):
        return ncol(b, ch, t0) if debug else xcol(b, ch, t0)

    items = [(b, tc) for b in range(NB) for tc in range(CHK)]
    for idx, (b, tc) in enumerate(items):
        buf = idx % 2
        if idx >= 1:
            ops.wait("tensor", s_yev, idx - 1 + 1)

        def mm_tr(t, b=b, tc=tc):
            last = None
            for ch in range(NCH):
                ps = ps_y0 if ch % 2 == 0 else ps_y1
                last = t.transpose(ps[:, (ch // 2) * 128:(ch // 2) * 128 + 128],
                                   src_t[:, scol(b, ch, tc * 128):scol(b, ch, tc * 128) + 128],
                                   ident[:])
            return last
        ops.sig("tensor", s_ytr, mm_tr)
        if idx >= 2:
            ops.wait("vector", s_outs[buf], s_outs[buf].v)
        ops.wait("vector", s_ytr, idx + 1)

        def v_ev(v, buf=buf):
            last = None
            for ch in range(NCH):
                ps = ps_y0 if ch % 2 == 0 else ps_y1
                last = v.tensor_copy(
                    ystg[:, buf * 1024 + ch * 128:buf * 1024 + (ch + 1) * 128],
                    ps[:, (ch // 2) * 128:(ch // 2) * 128 + 128])
            return last
        ops.sig("vector", s_yev, v_ev)
        ops.wait("sync", s_yev, idx + 1)
        ops.sig("sync", s_outs[buf],
                lambda e, b=b, tc=tc, buf=buf: e.dma_start(
                    out=y[b, tc * 128:(tc + 1) * 128, :],
                    in_=ystg[:, buf * 1024:(buf + 1) * 1024]), n=16)
    ops.wait("sync", s_outs[0], s_outs[0].v)
    ops.wait("sync", s_outs[1], s_outs[1].v)


def emit_l0_attn(c, dram):
    nc, ops = c.nc, c.ops
    x_sb, xn_sb = c.t["x_sb"], c.t["xn_sb"]
    ident, identf = c.t["ident"], c.t["identf"]
    iota_pr, iota_p1, iota_rev = c.t["iota_pr"], c.t["iota_p1"], c.t["iota_rev"]
    m_low = c.t["m_low"]
    ones_rowf = c.t["ones_rowf"]
    psA, psB, psC, psD = c.t["psA"], c.t["psB"], c.t["psC"], c.t["psD"]
    psF, psG, psH, psI = c.t["psF"], c.t["psG"], c.t["psH"], c.t["psI"]
    xsq = c.t["ln_xsq"]

    WT0 = c.sb("WT0", [128, NCH * 1024], BF16)
    knw2 = c.sb("knw2", [128, 1], F32)
    onw = c.sb("onw", [128, NCH], F32)
    onb = c.sb("onb", [128, NCH], F32)
    ones2 = c.sb("ones2", [128, 2], BF16)
    ident2 = c.sb("ident2", [2, 2], BF16)
    sel2 = c.sb("sel2", [2, 128], F32)
    iotaLM = c.sb("iotaLM", [128, 128], F32)
    iotaUM = c.sb("iotaUM", [128, 128], F32)
    prow = c.sb("prow", [1, 64], F32)
    pbc = c.sb("pbc", [128, 48], F32)
    negb_bc = c.sb("negb_bc", [128, 16], F32)
    lamC_bc = c.sb("lamC_bc", [128, 16], F32)
    gneg = c.sb("gneg", [128, 16], F32)
    sdec = c.sb("sdec", [128, 16], F32)
    dpair = c.sb("dpair", [128, 32 * 128], BF16)
    vneg = c.sb("vneg", [128, 1024], BF16)
    kt = c.sb("kt", [128, 1024], BF16)
    r2i = c.alias("r2i", [2, 1024], F32, "xstg", 0)
    rstd2 = c.alias("rstd2", [2, 1024], F32, "xstg", 4096)
    krow = c.sb("krow", [2, 1024], BF16)
    kbt = c.sb("kbt", [128, 16], F32)
    osum = c.sb("osum", [128, 1024], BF16)
    lnvec = c.sb("lnvec", [128, 4], F32)
    nh = c.sb("nh", [128, 2 * 256], BF16)
    nhT = c.sb("nhT", [128, 2 * 256], BF16)
    sqp = c.sb("sqp", [128, 2 * 5 * 512], BF16)
    rbuf = c.sb("rbuf", [128, 2 * 2 * 128], BF16)
    kdec = c.sb("kdec", [128, 2 * 128], BF16)
    S0 = c.sb("S0", [128, NB * 8 * 64], BF16)

    def sq_tile(px, j):
        o = (px * 9 + j) * 256
        return sqp[:, o:o + 256]

    s_pr = ops.sem("a_pr")
    s_T = ops.sem("a_T")
    s_V = ops.sem("a_V")
    s_A = ops.sem("a_A")
    s_ld = ops.sem("a_ld")
    s_gd = ops.sem("a_gd")

    s_pd = ops.sem("a_pd")

    def g_dma(g):
        g.dma_start(out=prow[0:1, 0:16], in_=dram["lam_logit"][None, :]).then_inc(s_pd.h, 16)
        g.dma_start(out=prow[0:1, 16:32], in_=dram["eta_raw"][None, :]).then_inc(s_pd.h, 16)
        g.dma_start(out=knw2[0:64, 0:1], in_=dram["k_norm_w"][:, None]).then_inc(s_pd.h, 16)
        g.dma_start(out=knw2[64:128, 0:1], in_=dram["k_norm_w"][:, None]).then_inc(s_pd.h, 16)
    s_pd.v += 64
    ops.on("gpsimd", g_dma)

    def g_tiles(g):
        g.iota(iotaLM[:], pattern=[[-1, 128]], base=0, channel_multiplier=1,
               allow_small_or_imprecise_dtypes=True).then_inc(s_pr.h, 1)
        g.iota(iotaUM[:], pattern=[[1, 128]], base=0, channel_multiplier=-1,
               allow_small_or_imprecise_dtypes=True).then_inc(s_pr.h, 1)
        g.memset(ones2[:], 0.0).then_inc(s_pr.h, 1)
        g.memset(sel2[:], 1.0).then_inc(s_pr.h, 1)
        g.memset(ident2[:], 1.0).then_inc(s_pr.h, 1)
    s_pr.v += 5
    ops.on("gpsimd", g_tiles)
    ops.wait("gpsimd", s_pr, s_pr.v)

    def g_tiles2(g):
        g.affine_select(out=iotaLM[:], in_=iotaLM[:], compare_op=ALU.is_gt,
                        fill=1e30, base=0, pattern=[[-1, 128]],
                        channel_multiplier=1).then_inc(s_pr.h, 1)
        g.affine_select(out=iotaUM[:], in_=iotaUM[:], compare_op=ALU.is_gt,
                        fill=1e30, base=0, pattern=[[1, 128]],
                        channel_multiplier=-1).then_inc(s_pr.h, 1)
        g.memset(ones2[0:64, 0:1], 1.0).then_inc(s_pr.h, 1)
        g.memset(ones2[64:128, 1:2], 1.0).then_inc(s_pr.h, 1)
        g.affine_select(out=sel2[:], in_=sel2[:], compare_op=ALU.is_ge,
                        fill=0.0, base=0, pattern=[[1, 128]],
                        channel_multiplier=-64).then_inc(s_pr.h, 1)
        g.affine_select(out=ident2[:], in_=ident2[:], compare_op=ALU.not_equal,
                        fill=0.0, base=0, pattern=[[-1, 2]],
                        channel_multiplier=1).then_inc(s_pr.h, 1)
        g.memset(S0[:], 0.0).then_inc(s_pr.h, 1)
    s_pr.v += 7
    ops.on("gpsimd", g_tiles2)
    ops.wait("gpsimd", s_pr, s_pr.v)

    def g_tiles3(g):
        g.affine_select(out=sel2[:], in_=sel2[:], compare_op=ALU.is_ge,
                        fill=0.0, base=63, pattern=[[-1, 128]],
                        channel_multiplier=64).then_inc(s_pr.h, 1)
    s_pr.v += 1
    ops.on("gpsimd", g_tiles3)

    ops.wait("scalar", s_pd, 64)

    def a_prep(a):
        a.activation(out=prow[0:1, 32:48], in_=prow[0:1, 0:16], func=AF.Sigmoid)
        a.drain()
        a.activation(out=prow[0:1, 48:64], in_=prow[0:1, 16:32], func=AF.Tanh)
        a.drain()
        return a.activation(out=prow[0:1, 0:16], in_=prow[0:1, 32:48], func=AF.Ln)
    ops.sig("scalar", s_A, a_prep)
    ops.wait("vector", s_A, s_A.v)

    def v_beta(v):
        v.tensor_scalar(out=prow[0:1, 32:48], in0=prow[0:1, 32:48], scalar1=-1.0,
                        scalar2=None, op0=ALU.add)
        v.drain()
        v.tensor_tensor(out=prow[0:1, 16:32], in0=prow[0:1, 48:64],
                        in1=prow[0:1, 32:48], op=ALU.mult)
        v.drain()
        return v.tensor_scalar(out=prow[0:1, 16:32], in0=prow[0:1, 16:32],
                               scalar1=1.0 / 64.0, scalar2=None, op0=ALU.mult)
    ops.sig("vector", s_V, v_beta)
    ops.wait("tensor", s_V, s_V.v)
    ops.sig("tensor", s_T,
            lambda t: t.matmul(psD[:, 0:32], ones_rowf[:], prow[0:1, 0:32],
                               start=True, stop=True))
    ops.wait("vector", s_T, s_T.v)

    def v_pbc(v):
        v.tensor_copy(pbc[:, 0:32], psD[:, 0:32])
        v.drain()
        return v.tensor_scalar(out=negb_bc[:], in0=pbc[:, 16:32], scalar1=-1.0,
                               scalar2=None, op0=ALU.mult)
    ops.sig("vector", s_V, v_pbc)
    ops.wait("scalar", s_V, s_V.v)

    def a_exp(a):
        a.activation(out=lamC_bc[:], in_=pbc[:, 0:16], func=AF.Exp, scale=128.0)
        for h in range(16):
            a.activation(out=gneg[:, h:h + 1], in_=iota_p1[:], func=AF.Exp,
                         scale=pbc[:, h:h + 1])
        last = None
        for h in range(16):
            last = a.activation(out=sdec[:, h:h + 1], in_=iota_rev[:], func=AF.Exp,
                                scale=pbc[:, h:h + 1])
        return last
    ops.sig("scalar", s_A, a_exp)
    ops.wait("vector", s_A, s_A.v)

    def v_gs(v):
        v.tensor_scalar(out=gneg[:], in0=gneg[:], scalar1=-1.0, scalar2=None,
                        op0=ALU.mult)
        return v.tensor_tensor(out=sdec[:], in0=sdec[:], in1=pbc[:, 16:32],
                               op=ALU.mult)
    ops.sig("vector", s_V, v_gs)
    ops.wait("scalar", s_pr, s_pr.v)

    dtmp = c.t["ln_xsq"]
    for h in range(16):
        for which, src in ((0, iotaLM), (1, iotaUM)):
            slot = (2 * h + which) % 4
            dsl = dtmp[:, 2048 + slot * 128:2048 + (slot + 1) * 128]
            if 2 * h + which >= 4:
                ops.wait("scalar", s_V, s_V.v - 3)
            ops.sig("scalar", s_A,
                    lambda a, src=src, h=h, dsl=dsl: a.activation(
                        out=dsl, in_=src[:], func=AF.Exp, scale=pbc[:, h:h + 1]))
            ops.wait("vector", s_A, s_A.v)
            idx = 4 * (h // 2) + (h % 2) + 2 * which
            ops.sig("vector", s_V,
                    lambda v, h=h, idx=idx, dsl=dsl: v.tensor_scalar(
                        out=dpair[:, idx * 128:(idx + 1) * 128],
                        in0=dsl, scalar1=negb_bc[:, h:h + 1], scalar2=None,
                        op0=ALU.mult))

    cstg = c.sb("cstg", [128, 128], F32)
    for name, dst in (("o_norm_w", onw), ("o_norm_b", onb)):
        ops.wait("sync", s_T, s_T.v)
        ops.sig("sync", s_ld,
                lambda e, name=name: e.dma_start(
                    out=cstg[0:8, 0:128],
                    in_=dram[name].rearrange("(a b) -> a b", b=128)), n=16)
        ops.wait("tensor", s_ld, s_ld.v)
        ops.wait("tensor", s_V, s_V.v)
        ops.sig("tensor", s_T,
                lambda t: t.transpose(psD[:, 0:8], cstg[0:8, 0:128], identf[0:8, 0:8]))
        ops.wait("vector", s_T, s_T.v)
        ops.sig("vector", s_V,
                lambda v, dst=dst: v.tensor_copy(dst[:], psD[:, 0:8]))

    ops.barrier("l0attn_wprep")
    wld = c.t["ln_xsq"]
    wtb_vals = []
    s_gw = [ops.sem("a_gw0"), ops.sem("a_gw1")]
    for oc in range(NCH):
        buf = oc % 2
        if oc >= 2:
            ops.wait("gpsimd", s_T, wtb_vals[oc - 2])
        ops.sig("gpsimd", s_gw[buf],
                lambda g, oc=oc, buf=buf: g.dma_start(
                    out=wld[:, buf * 1024:(buf + 1) * 1024],
                    in_=dram["W0"][oc * 128:(oc + 1) * 128, :]), n=16)
        ops.wait("tensor", s_gw[buf], s_gw[buf].v)

        def mm_wt_a(t, oc=oc, buf=buf):
            last = None
            for dc in range(4):
                last = t.transpose(psG[:, dc * 128:dc * 128 + 128],
                                   wld[:, buf * 1024 + dc * 128:buf * 1024 + dc * 128 + 128],
                                   ident[:])
            return last

        def mm_wt_b(t, oc=oc, buf=buf):
            last = None
            for dc in range(4, 8):
                last = t.transpose(psG[:, dc * 128:dc * 128 + 128],
                                   wld[:, buf * 1024 + dc * 128:buf * 1024 + dc * 128 + 128],
                                   ident[:])
            return last
        if oc >= 1:
            ops.wait("tensor", s_V, s_V.v)
        vta = ops.sig("tensor", s_T, mm_wt_a)
        ops.wait("vector", s_T, vta)

        def ev_a(v, oc=oc):
            last = None
            for dc in range(4):
                last = v.tensor_scalar(
                    out=WT0[:, dc * 1024 + oc * 128:dc * 1024 + oc * 128 + 128],
                    in0=psG[:, dc * 128:dc * 128 + 128], scalar1=-1.0, scalar2=None,
                    op0=ALU.mult)
            return last
        ops.sig("vector", s_V, ev_a)
        ops.wait("tensor", s_V, s_V.v)
        vtb = ops.sig("tensor", s_T, mm_wt_b)
        wtb_vals.append(vtb)
        ops.wait("vector", s_T, vtb)

        def ev_b(v, oc=oc):
            last = None
            for dc in range(4, 8):
                last = v.tensor_scalar(
                    out=WT0[:, dc * 1024 + oc * 128:dc * 1024 + oc * 128 + 128],
                    in0=psG[:, dc * 128:dc * 128 + 128], scalar1=-1.0, scalar2=None,
                    op0=ALU.mult)
            return last
        ops.sig("vector", s_V, ev_b)

    ops.barrier("l0attn_prep")

    if L0A_LEVEL >= 1:
        state_ev = {}
        for b in range(NB):
            for cc in range(CHK):
                emit_l0_chunk(c, b, cc, state_ev)

    ops.barrier("l0attn_done")


def emit_l0_chunk(c, b, cc, state_ev):
    nc, ops = c.nc, c.ops
    x_sb, xn_sb = c.t["x_sb"], c.t["xn_sb"]
    psA, psB, psC, psD = c.t["psA"], c.t["psB"], c.t["psC"], c.t["psD"]
    psF, psG, psH, psI = c.t["psF"], c.t["psG"], c.t["psH"], c.t["psI"]
    WT0, knw2, onw, onb = c.t["WT0"], c.t["knw2"], c.t["onw"], c.t["onb"]
    ones2, ident2, sel2 = c.t["ones2"], c.t["ident2"], c.t["sel2"]
    pbc, negb_bc, lamC_bc = c.t["pbc"], c.t["negb_bc"], c.t["lamC_bc"]
    gneg, sdec, dpair = c.t["gneg"], c.t["sdec"], c.t["dpair"]
    vneg, kt, r2i, rstd2 = c.t["vneg"], c.t["kt"], c.t["r2i"], c.t["rstd2"]
    krow, kbt, osum, lnvec = c.t["krow"], c.t["kbt"], c.t["osum"], c.t["lnvec"]
    nh, nhT, sqp, rbuf, kdec = c.t["nh"], c.t["nhT"], c.t["sqp"], c.t["rbuf"], c.t["kdec"]
    S0 = c.t["S0"]
    ident = c.t["ident"]
    xsq = c.t["ln_xsq"]
    s_T, s_V, s_A = ops.sem("a_T"), ops.sem("a_V"), ops.sem("a_A")

    def sq_tile(px, j):
        o = (px * 9 + j) * 256
        return c.t["sqp"][:, o:o + 256]

    xn3 = xn_sb[:].rearrange("p (n t) -> p n t", t=TP)[
        :, b * NCH:(b + 1) * NCH, PAD + cc * 128:PAD + cc * 128 + 128]

    ops.wait("tensor", s_V, s_V.v)
    ops.wait("tensor", s_A, s_A.v)

    def mm_static(t):
        last = None
        for half, ps in ((0, psA), (1, psB)):
            for dc in range(NCH):
                last = t.matmul(
                    ps[:, 0:512],
                    xn_sb[:, ncol(b, dc, cc * 128):ncol(b, dc, cc * 128) + 128],
                    WT0[:, dc * 1024 + half * 512:dc * 1024 + (half + 1) * 512],
                    start=(dc == 0), stop=(dc == NCH - 1))
        return last
    vs = ops.sig("tensor", s_T, mm_static)
    ops.wait("scalar", s_T, vs)

    def a_vneg(a):
        a.activation(out=vneg[:, 0:512], in_=psA[:, 0:512], func=AF.Copy)
        return a.activation(out=vneg[:, 512:1024], in_=psB[:, 0:512], func=AF.Copy)
    va_v = ops.sig("scalar", s_A, a_vneg)

    def a_sq(a):
        a.activation(out=xsq[:, 0:1024].rearrange("p (n t) -> p n t", t=128),
                     in_=xn3, func=AF.Square)
        return a.activation(out=xsq[:, 1024:2048].rearrange("p (n t) -> p n t", t=128),
                            in_=xn3, func=AF.Square, scale=knw2[:, 0:1])
    va_sq = ops.sig("scalar", s_A, a_sq)
    ops.wait("tensor", s_A, va_sq)
    ops.wait("tensor", s_V, s_V.v)

    def mm_hs(t):
        last = None
        for dc in range(NCH):
            ps = psC if dc < 4 else psD
            col = (dc % 4) * 128
            last = t.matmul(ps[0:2, col:col + 128], ones2[:],
                            xsq[:, dc * 128:dc * 128 + 128], start=True, stop=True)
        for dc in range(NCH):
            ps = psF if dc < 4 else psH
            col = (dc % 4) * 128
            last = t.matmul(ps[0:2, col:col + 128], ones2[:],
                            xsq[:, 1024 + dc * 128:1024 + dc * 128 + 128],
                            start=True, stop=True)
        return last
    vhs = ops.sig("tensor", s_T, mm_hs)
    ops.wait("vector", s_T, vhs)

    def v_rstd(v):
        v.tensor_scalar(out=r2i[0:2, 0:512], in0=psC[0:2, 0:512],
                        scalar1=1.0 / 64.0, scalar2=float(RMS_EPS),
                        op0=ALU.mult, op1=ALU.add)
        v.tensor_scalar(out=r2i[0:2, 512:1024], in0=psD[0:2, 0:512],
                        scalar1=1.0 / 64.0, scalar2=float(RMS_EPS),
                        op0=ALU.mult, op1=ALU.add)
        v.drain()
        v.reciprocal(r2i[0:2, 0:512], r2i[0:2, 0:512])
        v.reciprocal(r2i[0:2, 512:1024], r2i[0:2, 512:1024])
        v.drain()
        v.tensor_tensor(out=krow[0:2, 0:512], in0=psF[0:2, 0:512],
                        in1=r2i[0:2, 0:512], op=ALU.mult)
        return v.tensor_tensor(out=krow[0:2, 512:1024], in0=psH[0:2, 0:512],
                               in1=r2i[0:2, 512:1024], op=ALU.mult)
    vr = ops.sig("vector", s_V, v_rstd)
    ops.wait("scalar", s_V, vr)
    ops.sig("scalar", s_A,
            lambda a: a.activation(out=rstd2[0:2, 0:1024], in_=r2i[0:2, 0:1024],
                                   func=AF.Sqrt))
    ops.wait("tensor", s_A, s_A.v)

    def mm_rb(t):
        last = None
        for dc in range(NCH):
            ps = psA if dc < 4 else psB
            col = (dc % 4) * 128
            last = t.matmul(ps[:, col:col + 128], sel2[:],
                            rstd2[0:2, dc * 128:dc * 128 + 128],
                            start=True, stop=True)
        for hp in range(8):
            last = t.matmul(psC[:, 2 * hp:2 * hp + 2],
                            krow[0:2, hp * 128:hp * 128 + 128], ident2[:],
                            start=True, stop=True)
        return last
    vrb = ops.sig("tensor", s_T, mm_rb)
    ops.wait("vector", s_T, vrb)

    def v_kfin(v):
        rb3a = psA[:, None, 0:512].rearrange("p a (n t) -> p (a n) t", t=128)
        rb3b = psB[:, None, 0:512].rearrange("p a (n t) -> p (a n) t", t=128)
        xn3a = xn_sb[:].rearrange("p (n t) -> p n t", t=TP)[
            :, b * NCH:b * NCH + 4, PAD + cc * 128:PAD + cc * 128 + 128]
        xn3b = xn_sb[:].rearrange("p (n t) -> p n t", t=TP)[
            :, b * NCH + 4:b * NCH + 8, PAD + cc * 128:PAD + cc * 128 + 128]
        kt3a = kt[:, 0:512].rearrange("p (n t) -> p n t", t=128)
        kt3b = kt[:, 512:1024].rearrange("p (n t) -> p n t", t=128)
        v.scalar_tensor_tensor(out=kt3a, in0=xn3a, scalar=knw2[:, 0:1], in1=rb3a,
                               op0=ALU.mult, op1=ALU.mult)
        v.scalar_tensor_tensor(out=kt3b, in0=xn3b, scalar=knw2[:, 0:1], in1=rb3b,
                               op0=ALU.mult, op1=ALU.mult)
        v.drain()
        v.tensor_tensor(out=kbt[:], in0=psC[:, 0:16], in1=pbc[:, 16:32], op=ALU.mult)
        v.drain()
        return v.tensor_scalar(out=kbt[:], in0=kbt[:], scalar1=-1.0, scalar2=None,
                               op0=ALU.add)
    vkf = ops.sig("vector", s_V, v_kfin)

    if L0A_LEVEL >= 2:
        emit_scan_pairs(c, b, cc, state_ev, vkf, va_v)

    if L0A_LEVEL >= 3:
        emit_l0_outln(c, b, cc)





def dl_pair(c, pr):
    return c.t["dpair"][:, (4 * pr) * 128:(4 * pr + 2) * 128]


def du_pair(c, pr):
    return c.t["dpair"][:, (4 * pr + 2) * 128:(4 * pr + 4) * 128]


def emit_scan_pairs(c, b, cc, state_ev, vkf, va_v):
    ops = c.ops
    psA, psB, psC, psD = c.t["psA"], c.t["psB"], c.t["psC"], c.t["psD"]
    psF, psG, psH, psI = c.t["psF"], c.t["psG"], c.t["psH"], c.t["psI"]
    kt, kdec, rbuf, nh, nhT = c.t["kt"], c.t["kdec"], c.t["rbuf"], c.t["nh"], c.t["nhT"]
    sqp, S0, osum = c.t["sqp"], c.t["S0"], c.t["osum"]
    vneg, gneg, sdec, lamC_bc, kbt = (c.t["vneg"], c.t["gneg"], c.t["sdec"],
                                      c.t["lamC_bc"], c.t["kbt"])
    ident = c.t["ident"]
    s_T, s_V, s_A = ops.sem("a_T"), ops.sem("a_V"), ops.sem("a_A")
    BANKS = {0: (psA, psB, psC), 1: (psD, psF, psH)}

    def sql(px, lev):
        o = px * 2560 + (lev - 1) * 512
        return sqp[:, o:o + 512]

    val = {}

    def emit_step(pr, step):
        px = pr % 2
        bankE, bankO, alt = BANKS[px]
        he, ho = 2 * pr, 2 * pr + 1
        kte = kt[0:64, pr * 128:(pr + 1) * 128]
        kto = kt[64:128, pr * 128:(pr + 1) * 128]
        scol = (b * 8 + pr) * 64
        Spair = S0[:, scol:scol + 64]
        slot0 = rbuf[:, px * 256:px * 256 + 128]
        slot1 = rbuf[:, px * 256 + 128:px * 256 + 256]
        kd = kdec[:, px * 128:(px + 1) * 128]

        if step == "AP":
            ops.wait("tensor", s_V, max(vkf, state_ev.get((b, pr), 0),
                                        val.get((pr - 2, "SE"), 0)))
            ops.wait("tensor", s_A, max(va_v, val.get((pr - 2, "OS"), 0)))

            def mm(t):
                t.matmul(bankE[:, 0:128], kte, kte, start=True, stop=True)
                t.matmul(bankE[:, 256:320], kte, Spair[0:64, :], start=True, stop=True)
                t.matmul(bankO[:, 0:128], kto, kto, start=True, stop=True)
                return t.matmul(bankO[:, 256:320], kto, Spair[64:128, :],
                                start=True, stop=True)
            val[(pr, "AP")] = ops.sig("tensor", s_T, mm)
        elif step == "EV1":
            ops.wait("vector", s_T, val[(pr, "AP")])
            dp = c.t["dpair"]

            def ev(v):
                o = 4 * pr * 128
                v.tensor_tensor(out=nh[:, px * 256:px * 256 + 128],
                                in0=bankE[:, 0:128], in1=dp[:, o:o + 128],
                                op=ALU.mult)
                v.tensor_tensor(out=nh[:, px * 256 + 128:px * 256 + 256],
                                in0=bankO[:, 0:128], in1=dp[:, o + 128:o + 256],
                                op=ALU.mult)
                v.tensor_tensor(out=nhT[:, px * 256:px * 256 + 128],
                                in0=bankE[:, 0:128], in1=dp[:, o + 256:o + 384],
                                op=ALU.mult)
                v.tensor_tensor(out=nhT[:, px * 256 + 128:px * 256 + 256],
                                in0=bankO[:, 0:128], in1=dp[:, o + 384:o + 512],
                                op=ALU.mult)
                v.scalar_tensor_tensor(out=slot0[:, 0:64], in0=bankE[:, 256:320],
                                       scalar=gneg[:, he:he + 1],
                                       in1=vneg[:, he * 64:he * 64 + 64],
                                       op0=ALU.mult, op1=ALU.add)
                return v.scalar_tensor_tensor(out=slot0[:, 64:128],
                                              in0=bankO[:, 256:320],
                                              scalar=gneg[:, ho:ho + 1],
                                              in1=vneg[:, ho * 64:ho * 64 + 64],
                                              op0=ALU.mult, op1=ALU.add)
            val[(pr, "EV1")] = ops.sig("vector", s_V, ev)
        elif step.startswith("L"):
            lev = int(step[1:])
            psX = alt if lev % 2 == 1 else bankE
            if lev == 1:
                ops.wait("tensor", s_V, val[(pr, "EV1")])
                pe = nh[:, px * 256:px * 256 + 128]
                po = nh[:, px * 256 + 128:px * 256 + 256]
                be = nhT[:, px * 256:px * 256 + 128]
                bo = nhT[:, px * 256 + 128:px * 256 + 256]
            else:
                ops.wait("tensor", s_A, val[(pr, f"E{lev-1}")])
                prev = sql(px, lev - 1)
                pe, po = prev[:, 0:128], prev[:, 128:256]
                be, bo = prev[:, 256:384], prev[:, 384:512]

            def mm(t, pe=pe, po=po, be=be, bo=bo, lev=lev, psX=psX):
                if lev < 5:
                    t.matmul(psX[:, 0:128], be, pe, start=True, stop=True)
                    t.matmul(psX[:, 128:256], bo, po, start=True, stop=True)
                t.matmul(psX[:, 256:384], pe, be, start=True, stop=True)
                return t.matmul(psX[:, 384:512], po, bo, start=True, stop=True)
            val[(pr, step)] = ops.sig("tensor", s_T, mm)
        elif step.startswith("E"):
            lev = int(step[1:])
            psX = alt if lev % 2 == 1 else bankE
            ops.wait("scalar", s_T, val[(pr, f"L{lev}")])
            dst = sql(px, lev)
            if lev < 5:
                ops.sig("scalar", s_A,
                        lambda a, dst=dst, psX=psX: a.activation(
                            out=dst[:, 0:512], in_=psX[:, 0:512], func=AF.Copy))
            else:
                ops.sig("scalar", s_A,
                        lambda a, dst=dst, psX=psX: a.activation(
                            out=dst[:, 256:512], in_=psX[:, 256:512], func=AF.Copy))
            val[(pr, step)] = ops.sem("a_A").v
        elif step.startswith("AE"):
            ai = int(step[2:])
            psX = alt if ai % 2 == 0 else bankE
            ops.wait("scalar" if ai == 0 else "vector", s_T, val[(pr, f"A{ai}")])
            if ai == 0:
                ops.sig("scalar", s_A,
                        lambda a, psX=psX: a.activation(out=slot1[:, 0:128],
                                                        in_=psX[:, 0:128],
                                                        func=AF.Copy))
                val[(pr, "AE0")] = ops.sem("a_A").v
            else:
                def ev(v, psX=psX):
                    return v.scalar_tensor_tensor(out=slot0[:, 0:128],
                                                  in0=psX[:, 0:128], scalar=1.0,
                                                  in1=slot0[:, 0:128],
                                                  op0=ALU.mult, op1=ALU.add)
                val[(pr, f"AE{ai}")] = ops.sig("vector", s_V, ev)
        elif step.startswith("A") and step != "AP":
            ai = int(step[1:])
            psX = alt if ai % 2 == 0 else bankE
            if ai == 0 or ai == 1:
                bt = sql(px, 5)[:, 256:512]
            elif ai < 8:
                lev = 7 - ai
                lev = 7 - ai
                if lev == 0:
                    bt = nhT[:, px * 256:px * 256 + 256]
                else:
                    bt = sql(px, lev)[:, 256:512]
            rhs = slot0 if ai != 1 else slot1
            if ai == 0:
                ops.wait("tensor", s_A, val[(pr, "E5")])
            else:
                ops.wait("tensor", s_V if ai != 1 else s_A, val[(pr, f"AE{ai-1}")])

            def mm(t, bt=bt, rhs=rhs, psX=psX):
                t.matmul(psX[:, 0:64], bt[:, 0:128], rhs[:, 0:64],
                         start=True, stop=True)
                return t.matmul(psX[:, 64:128], bt[:, 128:256], rhs[:, 64:128],
                                start=True, stop=True)
            val[(pr, step)] = ops.sig("tensor", s_T, mm)
        elif step == "TR":
            ops.wait("tensor", s_V, max(vkf, val.get((pr - 2, "KD"), 0)))

            def mm(t):
                t.transpose(psG[:, px * 128:px * 128 + 64], kte, ident[0:64, 0:64])
                return t.transpose(psI[:, px * 128:px * 128 + 64], kto,
                                   ident[64:128, 64:128])
            val[(pr, "TR")] = ops.sig("tensor", s_T, mm)
        elif step == "KD":
            ops.wait("vector", s_T, val[(pr, "TR")])

            def ev(v):
                v.tensor_scalar(out=kd[:, 0:64], in0=psG[:, px * 128:px * 128 + 64],
                                scalar1=sdec[:, he:he + 1], scalar2=None, op0=ALU.mult)
                return v.tensor_scalar(out=kd[:, 64:128],
                                       in0=psI[:, px * 128:px * 128 + 64],
                                       scalar1=sdec[:, ho:ho + 1], scalar2=None,
                                       op0=ALU.mult)
            val[(pr, "KD")] = ops.sig("vector", s_V, ev)
        elif step == "ST":
            ops.wait("tensor", s_V, val[(pr, "KD")])

            def mm(t):
                t.matmul(bankE[0:64, 384:448], kd[:, 0:64], slot0[:, 0:64],
                         start=True, stop=True)
                return t.matmul(bankO[64:128, 384:448], kd[:, 64:128],
                                slot0[:, 64:128], start=True, stop=True,
                                tile_position=(0, 64))
            val[(pr, "ST")] = ops.sig("tensor", s_T, mm)
        elif step == "SE":
            ops.wait("vector", s_T, val[(pr, "ST")])

            def ev(v):
                v.scalar_tensor_tensor(out=Spair[0:64, :], in0=Spair[0:64, :],
                                       scalar=lamC_bc[0:64, he:he + 1],
                                       in1=bankE[0:64, 384:448],
                                       op0=ALU.mult, op1=ALU.add)
                return v.scalar_tensor_tensor(out=Spair[64:128, :],
                                              in0=Spair[64:128, :],
                                              scalar=lamC_bc[64:128, ho:ho + 1],
                                              in1=bankO[64:128, 384:448],
                                              op0=ALU.mult, op1=ALU.add)
            val[(pr, "SE")] = ops.sig("vector", s_V, ev)
            state_ev[(b, pr)] = val[(pr, "SE")]
        elif step == "OS":
            ops.wait("scalar", s_V, val[(pr, "AE7")])

            def ev(a):
                a.activation(out=osum[:, he * 64:he * 64 + 64], in_=slot0[:, 0:64],
                             func=AF.Copy, scale=kbt[:, he:he + 1])
                return a.activation(out=osum[:, ho * 64:ho * 64 + 64],
                                    in_=slot0[:, 64:128], func=AF.Copy,
                                    scale=kbt[:, ho:ho + 1])
            val[(pr, "OS")] = ops.sig("scalar", s_A, ev)

    STEPS = (["AP", "EV1", "L1", "E1", "L2", "E2", "L3", "E3", "L4", "E4",
              "L5", "E5", "A0", "AE0", "A1", "AE1", "A2", "AE2", "A3", "AE3",
              "A4", "AE4", "A5", "AE5", "A6", "AE6", "A7", "AE7",
              "TR", "KD", "ST", "SE", "OS"])[:L0A_NSTEP]
    for g in range(4):
        for step in STEPS:
            for pr in (2 * g, 2 * g + 1):
                emit_step(pr, step)


def emit_l0_outln(c, b, cc):
    ops = c.ops
    osum, lnvec = c.t["osum"], c.t["lnvec"]
    onw, onb = c.t["onw"], c.t["onb"]
    x_sb, ident, psG = c.t["x_sb"], c.t["ident"], c.t["psG"]
    xsq = c.t["ln_xsq"]
    afftmp = xsq[:, 2048:3072]
    s_T, s_V, s_A = ops.sem("a_T"), ops.sem("a_V"), ops.sem("a_A")
    vOS = ops.sem("a_A").v
    vSE = ops.sem("a_V").v

    ops.wait("vector", s_A, vOS)
    ops.sig("vector", ops.sem("a_V"),
            lambda v: v.reduce_sum(lnvec[:, 2:3], osum[:], axis=mybir.AxisListType.X))
    ops.wait("scalar", s_A, vOS)
    ops.sig("scalar", ops.sem("a_A"),
            lambda a: a.activation(out=xsq[:, 3072:4096], in_=osum[:],
                                   func=AF.Square, accum_out=lnvec[:, 3:4]))
    ops.wait("vector", ops.sem("a_A"), ops.sem("a_A").v)

    def v_stats(v):
        v.drain()
        v.tensor_scalar(out=lnvec[:, 0:1], in0=lnvec[:, 2:3], scalar1=-1.0 / D,
                        scalar2=None, op0=ALU.mult)
        v.drain()
        v.tensor_tensor(out=lnvec[:, 2:3], in0=lnvec[:, 0:1], in1=lnvec[:, 0:1],
                        op=ALU.mult)
        v.drain()
        v.scalar_tensor_tensor(out=lnvec[:, 1:2], in0=lnvec[:, 3:4],
                               scalar=1.0 / D, in1=lnvec[:, 2:3],
                               op0=ALU.mult, op1=ALU.subtract)
        v.drain()
        v.tensor_scalar(out=lnvec[:, 1:2], in0=lnvec[:, 1:2],
                        scalar1=float(LN_EPS), scalar2=None, op0=ALU.add)
        v.drain()
        return v.reciprocal(lnvec[:, 1:2], lnvec[:, 1:2])
    vst = ops.sig("vector", ops.sem("a_V"), v_stats)
    ops.wait("scalar", ops.sem("a_V"), vst)
    vsq = ops.sig("scalar", ops.sem("a_A"),
                  lambda a: a.sqrt(lnvec[:, 1:2], lnvec[:, 1:2]))
    ops.wait("vector", ops.sem("a_A"), vsq)
    def v_norm(v):
        v.drain()
        return v.tensor_scalar(out=osum[:], in0=osum[:], scalar1=lnvec[:, 0:1],
                               scalar2=lnvec[:, 1:2], op0=ALU.add, op1=ALU.mult)
    vn = ops.sig("vector", ops.sem("a_V"), v_norm)
    ops.wait("tensor", ops.sem("a_V"), vn)

    def mm_tr(t):
        last = None
        for dc in range(NCH):
            last = t.transpose(psG[:, dc * 128:dc * 128 + 128],
                               osum[:, dc * 128:dc * 128 + 128], ident[:])
        return last
    vtr = ops.sig("tensor", ops.sem("a_T"), mm_tr)
    ops.wait("scalar", ops.sem("a_T"), vtr)

    def a_aff(a):
        last = None
        for dc in range(NCH):
            last = a.activation(out=afftmp[:, dc * 128:dc * 128 + 128],
                                in_=psG[:, dc * 128:dc * 128 + 128],
                                func=AF.Identity,
                                scale=onw[:, dc:dc + 1], bias=onb[:, dc:dc + 1])
        return last
    vaf = ops.sig("scalar", ops.sem("a_A"), a_aff)
    ops.wait("vector", ops.sem("a_A"), vaf)

    def v_add(v):
        last = None
        for dc in range(NCH):
            xs = x_sb[:, xcol(b, dc, cc * 128):xcol(b, dc, cc * 128) + 128]
            last = v.tensor_tensor(out=xs, in0=xs,
                                   in1=afftmp[:, dc * 128:dc * 128 + 128],
                                   op=ALU.add)
        return last
    ops.sig("vector", ops.sem("a_V"), v_add)


def emit_w_scratch(c, dram):
    nc, ops = c.nc, c.ops
    cvt = c.sb("cvt", [128, 4 * 1024], BF16)
    s_ci = [ops.sem(f"cv_i{k}") for k in range(4)]
    s_co = [ops.sem(f"cv_o{k}") for k in range(4)]
    s_cm = ops.sem("cv_m")
    sw1, sw2 = dram["s_w1"], dram["s_w2"]
    items = []
    for l in range(2):
        for hc in range(22):
            for part in range(2):
                items.append(("w1", l, hc, part))
    for idx, (kind, l, hc, x) in enumerate(items):
        slot = idx % 4
        ops.wait("gpsimd", s_co[slot], s_co[slot].v)
        if kind == "w1":
            asz = min(128, HID - hc * 128)
            u = 2 * hc + x
            if asz < 128:
                ops.sig("gpsimd", s_cm, lambda g, slot=slot: g.memset(
                    cvt[:, slot * 1024:(slot + 1) * 1024], 0.0))
                ops.wait("gpsimd", s_cm, s_cm.v)

            def g_in(g, l=l, hc=hc, part=x, asz=asz, slot=slot):
                cb = part * HID + hc * 128
                src = dram["w1"][l, :, cb:cb + asz].rearrange(
                    "(n p) c -> p n c", p=128)
                dst3 = cvt[:, slot * 1024:(slot + 1) * 1024].rearrange(
                    "p (n c) -> p n c", c=128)
                return g.dma_start(out=dst3[:, :, 0:asz] if asz < 128 else dst3,
                                   in_=src)
            ops.sig("gpsimd", s_ci[slot], g_in, n=16)
            ops.wait("gpsimd", s_ci[slot], s_ci[slot].v)
            ops.sig("gpsimd", s_co[slot],
                    lambda g, l=l, u=u, slot=slot: g.dma_start(
                        out=sw1[l, u],
                        in_=cvt[:, slot * 1024:(slot + 1) * 1024]), n=16)
        else:
            dgrp = x
            rsz = min(128, HID - hc * 128)
            tile = cvt[:, slot * 1024:slot * 1024 + 512]
            if rsz < 128:
                ops.sig("gpsimd", s_cm,
                        lambda g, tile=tile: g.memset(tile, 0.0))
                ops.wait("gpsimd", s_cm, s_cm.v)

            def g_in(g, l=l, hc=hc, dgrp=dgrp, rsz=rsz, tile=tile):
                src = dram["w2"][l, hc * 128:hc * 128 + rsz,
                                 dgrp * 512:(dgrp + 1) * 512]
                return g.dma_start(out=tile[0:rsz, :], in_=src)
            ops.sig("gpsimd", s_ci[slot], g_in, n=16)
            ops.wait("gpsimd", s_ci[slot], s_ci[slot].v)
            ops.sig("gpsimd", s_co[slot],
                    lambda g, l=l, hc=hc, dgrp=dgrp, tile=tile: g.dma_start(
                        out=sw2[l, hc, dgrp], in_=tile), n=16)
    c.cv_done = [(sem, sem.v) for sem in s_co]


def emit_l0_ffn(c, dram):
    nc, ops = c.nc, c.ops
    ops.barrier("ffn_start")
    x_sb, xn_sb = c.t["x_sb"], c.t["xn_sb"]

    base = c.addr_of("WT0")
    end = c.addr_of("S0") + 2048
    cur = [base]

    def arena(name, shape, dtype):
        import concourse.mybir as mb
        per = 1
        for d in shape[1:]:
            per *= d
        per *= mb.dt.size(dtype)
        off = (cur[0] + 31) // 32 * 32
        assert off + per <= end, f"arena overflow {name}: {off+per-base} > {end-base}"
        cur[0] = off + per
        h = nc.alloc_sbuf_tensor_at(name, shape, dtype, offset=off)
        c.t[name] = h
        return h

    arena("wqkv_sb", [128, NCH * 1536], BF16)
    arena("TA", [128, 4 * CHK * 64], BF16)
    arena("wo_sb", [128, 4 * 1024], BF16)
    arena("qnb", [128, 1024], BF16)
    arena("ropet", [128, 1024], BF16)
    arena("attnT", [128, 1280], BF16)
    arena("rowq", [1, 128], F32)
    arena("qkbc", [128, 128], F32)
    arena("fstat", [128, 24], F32)
    arena("lgb", [128, 2 * NCH], F32)
    arena("b1p", [128, 48], F32)
    arena("b2p", [128, NCH], F32)
    arena("lcbuf", [128, NCH], F32)
    c.t["tbase"] = nc.alloc_sbuf_tensor_at(
        "tbase", [128, 4 * CHK * 32], F32, offset=c.addr_of("wo_sb"))

    for sem, vv in c.cv_done:
        for eng in ENGS:
            ops.wait(eng, sem, vv)
    cvb = c.addr_of("cvt")
    c.t["krT"] = nc.alloc_sbuf_tensor_at("krT", [128, 5 * 512], BF16,
                                         offset=cvb)
    c.t["ffnpad"] = nc.alloc_sbuf_tensor_at("ffnpad", [128, 1536], BF16,
                                            offset=cvb + 5120)
    c.t["cstg2"] = c.t["cstg"]

    lnx = c.addr_of("ln_xsq")
    xst = c.addr_of("xstg")

    def at(name, shape, dtype, addr):
        h = nc.alloc_sbuf_tensor_at(name, shape, dtype, offset=addr)
        c.t[name] = h
        return h
    at("qT", [128, 512], BF16, lnx + 0)
    at("sqscr", [128, 1024], BF16, lnx + 1024)
    at("vring", [128, 5 * 512], BF16, lnx + 3072)
    at("obuf", [128, 2048], BF16, xst + 0)
    at("attnb", [128, 1280], BF16, xst + 4096)

    krb = c.addr_of("krT")
    at("hid", [128, 8 * 512], BF16, krb + 0)
    at("vtmp", [128, 1024], F32, krb + 0)
    at("gelt", [128, 512], BF16, c.addr_of("qnb"))
    at("lnout", [128, 8 * 512], BF16, lnx + 0)
    at("w1blk", [128, 3 * 1024], BF16, xst + 0)
    at("w2blk", [128, 1024], BF16, xst + 6144)

    for l in range(2):
        emit_ffn_layer_params(c, dram, l)
        for b in range(NB):
            emit_ffn_phaseA(c, dram, l, b)
            emit_ffn_phaseB(c, dram, l, b)

    ops.barrier("ffn_add")
    s_V = ops.sem("f_V")
    for b in range(NB):
        def v_add(v, b=b):
            last = None
            for dc in range(NCH):
                xs = x_sb[:, xcol(b, dc, 0):xcol(b, dc, 0) + T]
                hn = xn_sb[:, ncol(b, dc, 0):ncol(b, dc, 0) + T]
                last = v.tensor_tensor(out=xs, in0=xs, in1=hn, op=ALU.add)
            return last
        ops.sig("vector", s_V, v_add)
    ops.barrier("ffn_done")


def emit_ffn_layer_params(c, dram, l):
    ops, nc = c.ops, c.nc
    wqkv_sb, wo_sb, TA, tb = c.t["wqkv_sb"], c.t["wo_sb"], c.t["TA"], c.t["tbase"]
    lgb, b1p, b2p = c.t["lgb"], c.t["b1p"], c.t["b2p"]
    rowq, qkbc, cstg2 = c.t["rowq"], c.t["qkbc"], c.t["cstg2"]
    psD = c.t["psD"]
    identf, ones_rowf = c.t["identf"], c.t["ones_rowf"]
    s_V, s_A, s_T, s_G = (ops.sem("f_V"), ops.sem("f_A"), ops.sem("f_T"),
                          ops.sem("f_G"))
    s_pd = ops.sem(f"f_pd{l}")
    s_pg = ops.sem(f"f_pg{l}")
    K32 = CHK * 32
    ops.barrier(f"ffn_params{l}")

    s_dt = ops.sem(f"f_dt{l}")
    for i, name in enumerate(("t_cs", "t_sn", "t_scale")):
        src = dram[name].rearrange("(n p) c -> p n c", p=128)
        dst = tb[:].rearrange("p (k n c) -> p k n c", k=4, c=32)[:, i]
        ops.sig("sync", s_dt, lambda e, src=src, dst=dst: e.dma_start(out=dst,
                                                                     in_=src), n=16)
    ops.wait("vector", s_dt, s_dt.v)
    ops.sig("vector", s_V,
            lambda v: v.reciprocal(tb[:, 3 * K32:4 * K32], tb[:, 2 * K32:3 * K32]))
    ops.sig("sync", s_pd,
            lambda e: e.dma_start(out=rowq[0:1, 0:64],
                                  in_=dram["q_scale"][l][None, :]), n=16)
    ops.sig("sync", s_pd,
            lambda e: e.dma_start(out=rowq[0:1, 64:128],
                                  in_=dram["k_scale"][l][None, :]), n=16)
    ops.wait("tensor", s_pd, s_pd.v)
    ops.sig("tensor", s_T,
            lambda t: t.matmul(psD[:, 0:128], ones_rowf[:], rowq[0:1, 0:128],
                               start=True, stop=True))
    ops.wait("vector", s_T, s_T.v)
    ops.sig("vector", s_V, lambda v: v.tensor_copy(qkbc[:], psD[:, 0:128]))

    def v_tables(v):
        tmp = c.t["vtmp"]
        t3 = lambda a: a.rearrange("p (n c) -> p n c", c=32)
        qs3 = lambda sl: qkbc[:, sl, None].transpose([0, 2, 1]).broadcast_to(
            [128, CHK, 32]) if False else qkbc[:, None, sl].broadcast_to(
            [128, CHK, 32])
        v.drain()
        v.tensor_tensor(out=tmp[:, 0:K32], in0=tb[:, 0:K32],
                        in1=tb[:, 2 * K32:3 * K32], op=ALU.mult)
        v.tensor_tensor(out=tmp[:, K32:2 * K32], in0=tb[:, K32:2 * K32],
                        in1=tb[:, 2 * K32:3 * K32], op=ALU.mult)
        v.drain()
        TAv = TA[:].rearrange("p (k n c) -> p k n c", k=8, c=32)
        v.scalar_tensor_tensor(out=TAv[:, 0], in0=t3(tmp[:, 0:K32]), scalar=8.0,
                               in1=qs3(slice(0, 32)), op0=ALU.mult, op1=ALU.mult)
        v.scalar_tensor_tensor(out=TAv[:, 1], in0=t3(tmp[:, 0:K32]), scalar=8.0,
                               in1=qs3(slice(32, 64)), op0=ALU.mult, op1=ALU.mult)
        v.scalar_tensor_tensor(out=TAv[:, 2], in0=t3(tmp[:, K32:2 * K32]),
                               scalar=-8.0, in1=qs3(slice(32, 64)),
                               op0=ALU.mult, op1=ALU.mult)
        v.scalar_tensor_tensor(out=TAv[:, 3], in0=t3(tmp[:, K32:2 * K32]),
                               scalar=8.0, in1=qs3(slice(0, 32)),
                               op0=ALU.mult, op1=ALU.mult)
        v.drain()
        v.tensor_tensor(out=tmp[:, 0:K32], in0=tb[:, 0:K32],
                        in1=tb[:, 3 * K32:4 * K32], op=ALU.mult)
        v.tensor_tensor(out=tmp[:, K32:2 * K32], in0=tb[:, K32:2 * K32],
                        in1=tb[:, 3 * K32:4 * K32], op=ALU.mult)
        v.drain()
        v.scalar_tensor_tensor(out=TAv[:, 4], in0=t3(tmp[:, 0:K32]), scalar=1.0,
                               in1=qs3(slice(64, 96)), op0=ALU.mult, op1=ALU.mult)
        v.scalar_tensor_tensor(out=TAv[:, 5], in0=t3(tmp[:, 0:K32]), scalar=1.0,
                               in1=qs3(slice(96, 128)), op0=ALU.mult, op1=ALU.mult)
        v.scalar_tensor_tensor(out=TAv[:, 6], in0=t3(tmp[:, K32:2 * K32]),
                               scalar=-1.0, in1=qs3(slice(96, 128)),
                               op0=ALU.mult, op1=ALU.mult)
        return v.scalar_tensor_tensor(out=TAv[:, 7], in0=t3(tmp[:, K32:2 * K32]),
                                      scalar=1.0, in1=qs3(slice(64, 96)),
                                      op0=ALU.mult, op1=ALU.mult)
    vtb = ops.sig("vector", s_V, v_tables)
    ops.wait("gpsimd", s_V, vtb)

    def g_w(g):
        src = dram["wqkv"][l].rearrange("(n p) c -> p n c", p=128)
        g.dma_start(out=wqkv_sb[:].rearrange("p (n c) -> p n c", c=1536),
                    in_=src).then_inc(s_pg.h, 16)
        src2 = dram["wo"][l].rearrange("(n p) c -> p n c", p=128)
        g.dma_start(out=wo_sb[:].rearrange("p (n c) -> p n c", c=1024),
                    in_=src2).then_inc(s_pg.h, 16)
    s_pg.v += 32
    ops.on("gpsimd", g_w)
    for name, dst in (("ln_g", lgb[:, 0:NCH]), ("ln_b", lgb[:, NCH:2 * NCH]),
                      ("b2", b2p[:, 0:NCH])):
        ops.wait("sync", s_T, s_T.v)
        ops.sig("sync", s_pd,
                lambda e, name=name: e.dma_start(
                    out=cstg2[0:8, 0:128],
                    in_=dram[name][l].rearrange("(a b) -> a b", b=128)), n=16)
        ops.wait("tensor", s_pd, s_pd.v)
        ops.wait("tensor", s_V, s_V.v)
        ops.sig("tensor", s_T,
                lambda t: t.transpose(psD[:, 0:8], cstg2[0:8, 0:128],
                                      identf[0:8, 0:8]))
        ops.wait("vector", s_T, s_T.v)
        ops.sig("vector", s_V, lambda v, dst=dst: v.tensor_copy(dst, psD[:, 0:8]))
    s_gz = ops.sem(f"f_gz{l}")
    ops.wait("gpsimd", s_V, s_V.v)
    ops.sig("gpsimd", s_gz, lambda g: g.memset(cstg2[:], 0.0))
    ops.wait("sync", s_gz, s_gz.v)
    for part, coff in ((0, 0), (1, 24)):
        ops.wait("sync", s_T, s_T.v)
        ops.sig("sync", s_pd,
                lambda e, part=part: e.dma_start(
                    out=cstg2[0:21, 0:128],
                    in_=dram["b1"][l, part * 2730:part * 2730 + 2688]
                    .rearrange("(a b) -> a b", b=128)), n=16)
        ops.sig("sync", s_pd,
                lambda e, part=part: e.dma_start(
                    out=cstg2[21:22, 0:42],
                    in_=dram["b1"][l, part * 2730 + 2688:part * 2730 + 2730][None, :]),
                n=16)
        ops.wait("tensor", s_pd, s_pd.v)
        ops.wait("tensor", s_V, s_V.v)
        ops.sig("tensor", s_T,
                lambda t: t.transpose(psD[:, 0:22], cstg2[0:22, 0:128],
                                      identf[0:22, 0:22]))
        ops.wait("vector", s_T, s_T.v)
        vv = ops.sig("vector", s_V,
                     lambda v, coff=coff: v.tensor_copy(b1p[:, coff:coff + 22],
                                                        psD[:, 0:22]))
        ops.wait("sync", s_V, vv)
    ops.wait("tensor", s_pg, s_pg.v)
    ops.barrier(f"ffn_params_done{l}")


def emit_ffn_phaseA(c, dram, l, b):
    ops, nc = c.ops, c.nc
    xn_sb = c.t["xn_sb"]
    wqkv_sb, wo_sb, TA = c.t["wqkv_sb"], c.t["wo_sb"], c.t["TA"]
    krT, vring, qT = c.t["krT"], c.t["vring"], c.t["qT"]
    sqscr, attnb, attnT = c.t["sqscr"], c.t["attnb"], c.t["attnT"]
    qnb, ropet = c.t["qnb"], c.t["ropet"]
    obuf, lcbuf, fstat = c.t["obuf"], c.t["lcbuf"], c.t["fstat"]
    m_til, m_tiu, ident = c.t["m_til"], c.t["m_tiu"], c.t["ident"]
    psA, psB, psC, psD = c.t["psA"], c.t["psB"], c.t["psC"], c.t["psD"]
    psF, psG, psH, psI = c.t["psF"], c.t["psG"], c.t["psH"], c.t["psI"]
    s_V, s_A, s_T, s_G = (ops.sem("f_V"), ops.sem("f_A"), ops.sem("f_T"),
                          ops.sem("f_G"))
    val = {}
    TAv = TA[:].rearrange("p (k n c) -> p k n c", k=8, c=32)

    def hsl(dc, t0, n):
        o = ncol(b, dc, t0) - (1 if dc >= 4 else 0)
        return xn_sb[:, o:o + n]

    for i in range(CHK):
        ops.wait("tensor", s_V, val.get(("qkn", i - 1), 0))
        ops.wait("tensor", s_A, max(val.get(("vev", i - 1), 0),
                                    val.get(("oev", i - 1, 3), 0),
                                    val.get(("sev", i - 1, 3), 0)))

        def mm_qkv(t, i=i):
            last = None
            for part, ps in ((0, psA), (1, psB), (2, psC)):
                for dc in range(NCH):
                    last = t.matmul(
                        ps[:, 0:512], hsl(dc, i * 128, 128),
                        wqkv_sb[:, dc * 1536 + part * 512:dc * 1536 + (part + 1) * 512],
                        start=(dc == 0), stop=(dc == NCH - 1))
            return last
        val[("qkv", i)] = ops.sig("tensor", s_T, mm_qkv)
        if i % 4 == 0 and i > 0:
            ops.wait("vector", s_T, val[("qkv", i)])

            def v_lc(v, i=i):
                last = None
                for dc in range(NCH):
                    col = ncol(b, dc, i * 128 - 1)
                    last = v.tensor_tensor(out=xn_sb[:, col:col + 1],
                                           in0=xn_sb[:, col:col + 1],
                                           in1=lcbuf[:, dc:dc + 1], op=ALU.add)
                return last
            ops.sig("vector", s_V, v_lc)
        ops.wait("scalar", s_T, val[("qkv", i)])

        def a_sq(a, i=i):
            a.activation(out=sqscr[:, 0:512], in_=psA[:, 0:512], func=AF.Square)
            a.activation(out=sqscr[:, 512:1024], in_=psB[:, 0:512], func=AF.Square)
            return a.activation(out=vring[:, (i % 5) * 512:(i % 5) * 512 + 512],
                                in_=psC[:, 0:512], func=AF.Copy)
        val[("vev", i)] = ops.sig("scalar", s_A, a_sq)
        ops.wait("vector", s_A, val[("vev", i)])

        def v_l2(v):
            v.tensor_reduce(out=fstat[:, 0:8, None],
                            in_=sqscr[:, 0:512].rearrange("p (n c) -> p n c", c=64),
                            axis=mybir.AxisListType.X, op=ALU.add)
            return v.tensor_reduce(
                out=fstat[:, 8:16, None],
                in_=sqscr[:, 512:1024].rearrange("p (n c) -> p n c", c=64),
                axis=mybir.AxisListType.X, op=ALU.add)
        vl2 = ops.sig("vector", s_V, v_l2)
        ops.wait("scalar", s_V, vl2)
        va_n = ops.sig("scalar", s_A,
                       lambda a: a.sqrt(fstat[:, 0:16], fstat[:, 0:16]))
        ops.wait("vector", s_A, va_n)

        def v_rstd(v):
            v.tensor_scalar(out=fstat[:, 0:16], in0=fstat[:, 0:16],
                            scalar1=1e-12, scalar2=None, op0=ALU.max)
            v.drain()
            v.reciprocal(fstat[:, 0:16], fstat[:, 0:16])
            v.drain()
            v.tensor_tensor(
                out=qnb[:, 0:512].rearrange("p (n c) -> p n c", c=64),
                in0=psA[:, 0:512].rearrange("p (n c) -> p n c", c=64),
                in1=fstat[:, 0:8, None].broadcast_to([128, 8, 64]), op=ALU.mult)
            return v.tensor_tensor(
                out=qnb[:, 512:1024].rearrange("p (n c) -> p n c", c=64),
                in0=psB[:, 0:512].rearrange("p (n c) -> p n c", c=64),
                in1=fstat[:, 8:16, None].broadcast_to([128, 8, 64]), op=ALU.mult)
        val[("qkn", i)] = ops.sig("vector", s_V, v_rstd)
        def v_rope(v, i=i):
            qn3 = qnb[:, 0:512].rearrange("p (n c) -> p n c", c=64)
            kn3 = qnb[:, 512:1024].rearrange("p (n c) -> p n c", c=64)
            t13 = ropet[:, 0:512].rearrange("p (n c) -> p n c", c=64)
            t23 = ropet[:, 512:1024].rearrange("p (n c) -> p n c", c=64)
            last = None
            for (src3, tA, tB, dst) in ((qn3, 0, 2, sqscr[:, 0:512]),
                                        (kn3, 4, 6, sqscr[:, 512:1024])):
                def ts(k):
                    return TAv[:, k, i, None, :].broadcast_to([128, 8, 32])
                v.drain()
                v.tensor_tensor(out=t13[:, :, 0:32], in0=src3[:, :, 0:32],
                                in1=ts(tA), op=ALU.mult)
                v.tensor_tensor(out=t13[:, :, 32:64], in0=src3[:, :, 32:64],
                                in1=ts(tA + 1), op=ALU.mult)
                v.tensor_tensor(out=t23[:, :, 0:32], in0=src3[:, :, 32:64],
                                in1=ts(tB), op=ALU.mult)
                v.tensor_tensor(out=t23[:, :, 32:64], in0=src3[:, :, 0:32],
                                in1=ts(tB + 1), op=ALU.mult)
                v.drain()
                last = v.tensor_tensor(out=dst, in0=ropet[:, 0:512],
                                       in1=ropet[:, 512:1024], op=ALU.add)
            return last
        val[("rope", i)] = ops.sig("vector", s_V, v_rope)
        ops.wait("tensor", s_V, val[("rope", i)])
        ops.wait("tensor", s_A, val.get(("atev", i - 1, 3), 0))

        def mm_tr(t):
            last = None
            for j in range(8):
                last = t.transpose(psG[:, j * 128:j * 128 + 128],
                                   sqscr[:, j * 128:j * 128 + 128], ident[:])
            return last
        vtr = ops.sig("tensor", s_T, mm_tr)
        ops.wait("scalar", s_T, vtr)

        def a_qkT(a, i=i):
            a.activation(out=qT[:], in_=psG[:, 0:512], func=AF.Copy)
            return a.activation(out=krT[:, (i % 5) * 512:(i % 5) * 512 + 512],
                                in_=psG[:, 512:1024], func=AF.Copy)
        val[("qkT", i)] = ops.sig("scalar", s_A, a_qkT)

        nk = min(i, 4) + 1
        j0 = i - (nk - 1)
        for p in range(4):
            prev_sev = (val.get(("sev", i, p - 1), 0) if p >= 1
                        else val.get(("sev", i - 1, 3), 0))
            ops.wait("tensor", s_A, max(val[("qkT", i)], prev_sev))
            if i % 4 == 0 and i > 0:
                ops.wait("tensor", s_V, val.get(("woev", i // 4 - 1), 0))

            def mm_sc2(t, i=i, p=p, nk=nk, j0=j0):
                last = None
                for par in range(2):
                    main = psD if par == 0 else psH
                    diag = psA if par == 0 else psB
                    pslo = par * 64
                    for jj in range(nk):
                        sl = ((j0 + jj) % 5) * 512 + p * 128
                        out = (main[:, jj * 128:jj * 128 + 128] if jj < nk - 1
                               else diag[:, 0:128])
                        last = t.matmul(
                            out, qT[pslo:pslo + 64, p * 128:p * 128 + 128],
                            krT[pslo:pslo + 64, sl:sl + 128],
                            start=True, stop=True)
                return last
            val[("sc", i, p)] = ops.sig("tensor", s_T, mm_sc2)
            ops.wait("vector", s_T, val[("sc", i, p)])

            def v_mx2(v, nk=nk):
                last = None
                for par in range(2):
                    main = psD if par == 0 else psH
                    diag = psA if par == 0 else psB
                    if nk == 1:
                        last = v.tensor_reduce(
                            out=fstat[:, 16 + par:17 + par], in_=diag[:, 0:128],
                            axis=mybir.AxisListType.X, op=ALU.max, negate=True)
                        continue
                    v.tensor_reduce(out=fstat[:, 16 + par:17 + par],
                                    in_=main[:, 0:(nk - 1) * 128],
                                    axis=mybir.AxisListType.X,
                                    op=ALU.max, negate=True)
                    v.tensor_reduce(out=fstat[:, 20 + par:21 + par],
                                    in_=diag[:, 0:128], axis=mybir.AxisListType.X,
                                    op=ALU.max, negate=True)
                if nk > 1:
                    v.drain()
                    for par in range(2):
                        last = v.tensor_tensor(
                            out=fstat[:, 16 + par:17 + par],
                            in0=fstat[:, 16 + par:17 + par],
                            in1=fstat[:, 20 + par:21 + par], op=ALU.min)
                return last
            vmx = ops.sig("vector", s_V, v_mx2)
            ops.wait("scalar", s_V, vmx)

            def a_vex2(a, nk=nk):
                last = None
                for par in range(2):
                    main = psD if par == 0 else psH
                    diag = psA if par == 0 else psB
                    ab = attnb[:, par * 640:par * 640 + 640]
                    if nk > 1:
                        a.activation(out=ab[:, 0:(nk - 1) * 128],
                                     in_=main[:, 0:(nk - 1) * 128], func=AF.Exp,
                                     bias=fstat[:, 16 + par:17 + par])
                    last = a.activation(out=ab[:, (nk - 1) * 128:nk * 128],
                                        in_=diag[:, 0:128], func=AF.Exp,
                                        bias=fstat[:, 16 + par:17 + par])
                return last
            vex = ops.sig("scalar", s_A, a_vex2)
            val[("sev", i, p)] = vex
            ops.wait("vector", s_A, vex)

            def v_sm2(v, nk=nk):
                v.drain()
                for par in range(2):
                    ab = attnb[:, par * 640:par * 640 + 640]
                    v.tensor_tensor(out=ab[:, (nk - 1) * 128:nk * 128],
                                    in0=ab[:, (nk - 1) * 128:nk * 128],
                                    in1=m_til[:], op=ALU.mult)
                    if nk == 5:
                        v.tensor_tensor(out=ab[:, 0:128], in0=ab[:, 0:128],
                                        in1=m_tiu[:], op=ALU.mult)
                v.drain()
                for par in range(2):
                    ab = attnb[:, par * 640:par * 640 + 640]
                    v.tensor_reduce(out=fstat[:, 18 + par:19 + par],
                                    in_=ab[:, 0:nk * 128],
                                    axis=mybir.AxisListType.X, op=ALU.add)
                v.drain()
                v.reciprocal(fstat[:, 18:20], fstat[:, 18:20])
                v.drain()
                last = None
                for par in range(2):
                    ab = attnb[:, par * 640:par * 640 + 640]
                    last = v.tensor_scalar(out=ab[:, 0:nk * 128],
                                           in0=ab[:, 0:nk * 128],
                                           scalar1=fstat[:, 18 + par:19 + par],
                                           scalar2=None, op0=ALU.mult)
                return last
            vsm = ops.sig("vector", s_V, v_sm2)
            ops.wait("tensor", s_V, vsm)
            prev_at = (val.get(("atev", i, p - 1), 0) if p >= 1
                       else val.get(("atev", i - 1, 3), 0))
            if p == 0:
                prev_at = max(prev_at, val[("qkT", i)])
            ops.wait("tensor", s_A, prev_at)

            def mm_at2(t, nk=nk):
                last = None
                for par in range(2):
                    pst = psG if par == 0 else psI
                    ab = attnb[:, par * 640:par * 640 + 640]
                    for jj in range(nk):
                        last = t.transpose(pst[:, jj * 128:jj * 128 + 128],
                                           ab[:, jj * 128:jj * 128 + 128],
                                           ident[:])
                return last
            vat = ops.sig("tensor", s_T, mm_at2)
            ops.wait("scalar", s_T, vat)

            def a_atev2(a, nk=nk):
                last = None
                for par in range(2):
                    pst = psG if par == 0 else psI
                    at_ = attnT[:, par * 640:par * 640 + 640]
                    last = a.activation(out=at_[:, 0:nk * 128],
                                        in_=pst[:, 0:nk * 128], func=AF.Copy)
                return last
            val[("atev", i, p)] = ops.sig("scalar", s_A, a_atev2)
            prev_oev = (val.get(("oev", i, p - 1), 0) if p >= 1
                        else val.get(("oev", i - 1, 3), 0))
            ops.wait("tensor", s_A, max(val[("atev", i, p)], prev_oev))

            def mm_av2(t, p=p, nk=nk, j0=j0):
                last = None
                for par in range(2):
                    h = 2 * p + par
                    at_ = attnT[:, par * 640:par * 640 + 640]
                    avb = psF if par == 0 else psC
                    for jj in range(nk):
                        sl = ((j0 + jj) % 5) * 512 + h * 64
                        if par == 0:
                            last = t.matmul(avb[0:64, 256:384],
                                            vring[:, sl:sl + 64],
                                            at_[:, jj * 128:jj * 128 + 128],
                                            start=(jj == 0), stop=(jj == nk - 1))
                        else:
                            last = t.matmul(avb[64:128, 384:512],
                                            vring[:, sl:sl + 64],
                                            at_[:, jj * 128:jj * 128 + 128],
                                            start=(jj == 0), stop=(jj == nk - 1),
                                            tile_position=(0, 64))
                return last
            vav = ops.sig("tensor", s_T, mm_av2)
            ops.wait("scalar", s_T, vav)
            if i % 4 == 0 and i > 0:
                ops.wait("scalar", s_T, val.get(("woMM", i // 4 - 1), 0))

            def a_oev2(a, i=i, p=p):
                last = None
                for par in range(2):
                    pslo = par * 64
                    osl = obuf[pslo:pslo + 64,
                               p * 512 + (i % 4) * 128:
                               p * 512 + (i % 4) * 128 + 128]
                    src = (psF[0:64, 256:384] if par == 0
                           else psC[64:128, 384:512])
                    last = a.activation(out=osl, in_=src, func=AF.Copy)
                return last
            val[("oev", i, p)] = ops.sig("scalar", s_A, a_oev2)
        if i % 4 == 3:
            blk = i // 4
            ops.wait("tensor", s_A, val[("oev", i, 3)])
            lastwo = 0
            for dc in range(NCH):
                bank = psD if dc % 2 == 0 else psH

                def mm_wo(t, dc=dc, bank=bank):
                    last = None
                    for pc in range(4):
                        last = t.matmul(
                            bank[:, 0:512],
                            wo_sb[:, pc * 1024 + dc * 128:pc * 1024 + dc * 128 + 128],
                            obuf[:, pc * 512:pc * 512 + 512],
                            start=(pc == 0), stop=(pc == 3))
                    return last
                if dc >= 2:
                    ops.wait("tensor", s_V, lastwo)
                vwo = ops.sig("tensor", s_T, mm_wo)
                val[("woMM", blk)] = vwo
                ops.wait("vector", s_T, vwo)

                def v_woev(v, dc=dc, bank=bank, blk=blk):
                    col = ncol(b, dc, blk * 512)
                    v.scalar_tensor_tensor(out=xn_sb[:, col:col + 511],
                                           in0=bank[:, 0:511], scalar=1.0,
                                           in1=xn_sb[:, col:col + 511],
                                           op0=ALU.mult, op1=ALU.add)
                    return v.tensor_copy(lcbuf[:, dc:dc + 1], bank[:, 511:512])
                lastwo = ops.sig("vector", s_V, v_woev)
            val[("woev", blk)] = lastwo
    ops.wait("vector", s_V, s_V.v)

    def v_lcf(v):
        last = None
        for dc in range(NCH):
            col = ncol(b, dc, T - 1)
            last = v.tensor_tensor(out=xn_sb[:, col:col + 1],
                                   in0=xn_sb[:, col:col + 1],
                                   in1=lcbuf[:, dc:dc + 1], op=ALU.add)
        return last
    ops.sig("vector", s_V, v_lcf)
    ops.barrier(f"ffnA_{l}_{b}")


def emit_ffn_phaseB(c, dram, l, b):
    ops, nc = c.ops, c.nc
    xn_sb = c.t["xn_sb"]
    hid, lnout, w1blk, w2blk, gelt = (c.t["hid"], c.t["lnout"], c.t["w1blk"],
                                      c.t["w2blk"], c.t["gelt"])
    lgb, b1p, b2p, lcbuf = c.t["lgb"], c.t["b1p"], c.t["b2p"], c.t["lcbuf"]
    ones_col, ones_row = c.t["ones_col"], c.t["ones_row"]
    stats = c.t["ln_stats"]
    psA, psB, psC, psD = c.t["psA"], c.t["psB"], c.t["psC"], c.t["psD"]
    psF, psH = c.t["psF"], c.t["psH"]
    s_V, s_A, s_T, s_G = (ops.sem("f_V"), ops.sem("f_A"), ops.sem("f_T"),
                          ops.sem("f_G"))
    s_w = [ops.sem("f_w0"), ops.sem("f_w1"), ops.sem("f_w2")]
    s_u = [ops.sem("f_u0"), ops.sem("f_u1")]
    val = {}
    w1g = {}
    w2slot = {0: 0, 1: 0}

    def hsl(dc, t0, n):
        o = ncol(b, dc, t0) - (1 if dc >= 4 else 0)
        return xn_sb[:, o:o + n]

    for tau in range(T // 512):
        t0 = tau * 512
        ops.wait("scalar", s_T, val.get(("w2last", tau - 1), 0))
        ops.wait("scalar", s_V, s_V.v)

        def a_sq(a, t0=t0):
            last = None
            for dc in range(NCH):
                last = a.activation(out=lnout[:, dc * 512:dc * 512 + 512],
                                    in_=hsl(dc, t0, 512), func=AF.Square)
            return last
        vsq = ops.sig("scalar", s_A, a_sq)
        ops.wait("tensor", s_A, vsq)
        ops.wait("tensor", s_V, s_V.v)

        def mm_s(t, t0=t0):
            last = None
            for dc in range(NCH):
                t.matmul(psA[0:1, 0:512], ones_col[:], hsl(dc, t0, 512),
                         start=(dc == 0), stop=(dc == NCH - 1))
                last = t.matmul(psB[0:1, 0:512], ones_col[:],
                                lnout[:, dc * 512:dc * 512 + 512],
                                start=(dc == 0), stop=(dc == NCH - 1))
            return last
        vms = ops.sig("tensor", s_T, mm_s)
        if tau >= 1:
            ops.wait("vector", s_T, vms)

            def v_lc(v, tau=tau):
                last = None
                for dc in range(NCH):
                    col = ncol(b, dc, tau * 512 - 1)
                    last = v.tensor_tensor(out=xn_sb[:, col:col + 1],
                                           in0=xn_sb[:, col:col + 1],
                                           in1=lcbuf[:, dc:dc + 1], op=ALU.add)
                return last
            ops.sig("vector", s_V, v_lc)
        ops.wait("vector", s_T, vms)

        def v_st(v):
            v.drain()
            v.tensor_scalar(out=stats[0:1, 0:512], in0=psA[0:1, 0:512],
                            scalar1=-1.0 / D, scalar2=None, op0=ALU.mult)
            v.drain()
            v.tensor_tensor(out=stats[0:1, 512:1024], in0=stats[0:1, 0:512],
                            in1=stats[0:1, 0:512], op=ALU.mult)
            v.drain()
            v.scalar_tensor_tensor(out=stats[0:1, 512:1024], in0=psB[0:1, 0:512],
                                   scalar=1.0 / D, in1=stats[0:1, 512:1024],
                                   op0=ALU.mult, op1=ALU.subtract)
            v.drain()
            v.tensor_scalar(out=stats[0:1, 512:1024], in0=stats[0:1, 512:1024],
                            scalar1=float(LN_EPS), scalar2=None, op0=ALU.add)
            v.drain()
            with nc.allow_low_precision(reason="bf16 LN stats"):
                return v.reciprocal(stats[0:1, 512:1024], stats[0:1, 512:1024])
        vst = ops.sig("vector", s_V, v_st)
        ops.wait("scalar", s_V, vst)
        def a_bsqrt(a):
            with nc.allow_low_precision(reason="bf16 LN stats"):
                return a.sqrt(stats[0:1, 512:1024], stats[0:1, 512:1024])
        vsr = ops.sig("scalar", s_A, a_bsqrt)
        ops.wait("tensor", s_A, vsr)

        def mm_bc(t):
            t.matmul(psC[:, 0:512], ones_row[:], stats[0:1, 0:512],
                     start=True, stop=True)
            return t.matmul(psF[:, 0:512], ones_row[:], stats[0:1, 512:1024],
                            start=True, stop=True)
        vbc = ops.sig("tensor", s_T, mm_bc)
        ops.wait("vector", s_T, vbc)

        def v_ln(v, t0=t0):
            last = None
            for dc in range(NCH):
                sl = lnout[:, dc * 512:dc * 512 + 512]
                v.drain()
                v.tensor_tensor(out=sl, in0=hsl(dc, t0, 512), in1=psC[:, 0:512],
                                op=ALU.add)
                v.drain()
                v.tensor_tensor(out=sl, in0=sl, in1=psF[:, 0:512], op=ALU.mult)
                v.drain()
                last = v.tensor_scalar(out=sl, in0=sl, scalar1=lgb[:, dc:dc + 1],
                                       scalar2=lgb[:, NCH + dc:NCH + dc + 1],
                                       op0=ALU.mult, op1=ALU.add)
            return last
        val[("ln", tau)] = ops.sig("vector", s_V, v_ln)

        PASSES = ((0, 8), (8, 15), (15, 22))
        for half, (hlo, hhi) in enumerate(PASSES):
            npc = hhi - hlo
            for k in range(npc):
                hc = hlo + k
                u = 2 * hc
                asz = min(128, 2730 - hc * 128)
                for part in range(2):
                    slot = (u + part) % 3
                    sw = s_w[slot]
                    gu = tau * 44 + u + part
                    ops.wait("gpsimd", s_T, w1g.get(gu - 3, 0))
                    ops.sig("gpsimd", sw,
                            lambda g, l=l, u2=u + part, slot=slot: g.dma_start(
                                out=w1blk[:, slot * 1024:(slot + 1) * 1024],
                                in_=dram["s_w1"][l, u2]), n=16)
                    ops.wait("tensor", sw, sw.v)
                    ops.wait("tensor", s_V, val[("ln", tau)])
                    bank = psA if part == 0 else psB
                    if u + part >= 2:
                        ops.wait("tensor", s_V, val.get(("gg", tau, hc - 1), 0))
                        ops.wait("tensor", s_A, val.get(("gl", tau, hc - 1), 0))

                    def mm_w1(t, slot=slot, bank=bank):
                        last = None
                        d3 = w1blk[:, slot * 1024:(slot + 1) * 1024].rearrange(
                            "p (n c) -> p n c", c=128)
                        for dc in range(NCH):
                            last = t.matmul(bank[:, 0:512], d3[:, dc],
                                            lnout[:, dc * 512:dc * 512 + 512],
                                            start=(dc == 0), stop=(dc == NCH - 1))
                        return last
                    val[("w1mm", tau, u + part)] = ops.sig("tensor", s_T, mm_w1)
                    w1g[gu] = val[("w1mm", tau, u + part)]
                ops.wait("scalar", s_T, val[("w1mm", tau, u + 1)])
                val[("gl", tau, hc)] = ops.sig(
                    "scalar", s_A,
                    lambda a, hc=hc: a.activation(
                        out=gelt[:], in_=psB[:, 0:512], func=AF.Gelu,
                        bias=b1p[:, 24 + hc:25 + hc]))
                ops.wait("vector", s_A, val[("gl", tau, hc)])
                val[("gg", tau, hc)] = ops.sig(
                    "vector", s_V,
                    lambda v, k=k, hc=hc: v.scalar_tensor_tensor(
                        out=hid[:, k * 512:(k + 1) * 512], in0=psA[:, 0:512],
                        scalar=b1p[:, hc:hc + 1], in1=gelt[:],
                        op0=ALU.add, op1=ALU.mult))
            for dgrp in range(2):
                for k in range(npc):
                    hc = hlo + k
                    rsz = min(128, 2730 - hc * 128)
                    slot = k % 2
                    su = s_u[slot]
                    if w2slot[slot]:
                        ops.wait("gpsimd", s_T, w2slot[slot])
                    if rsz < 128:
                        ops.sig("gpsimd", s_G,
                                lambda g, slot=slot: g.memset(
                                    w2blk[:, slot * 512:slot * 512 + 512], 0.0))
                        ops.wait("gpsimd", s_G, s_G.v)

                    def g_w2(g, hc=hc, dgrp=dgrp, slot=slot, rsz=rsz, su=su):
                        src = dram["w2"][l, hc * 128:hc * 128 + rsz,
                                         dgrp * 512:(dgrp + 1) * 512]
                        dst = w2blk[0:rsz, slot * 512:slot * 512 + 512]
                        g.dma_start(out=dst, in_=src).then_inc(su.h, 16)
                    ops.on("gpsimd", g_w2)
                    su.v += 16
                    ops.wait("tensor", su, su.v)
                    ops.wait("tensor", s_V, val[("gg", tau, hc)])
                    if k == 0:
                        ops.wait("tensor", s_V,
                                 val.get(("w2ev", tau, half, dgrp - 1),
                                         val.get(("w2ev", tau, half - 1, 1), 0)))

                    def mm_w2(t, k=k, npc=npc, slot=slot):
                        last = None
                        d3 = w2blk[:, slot * 512:slot * 512 + 512].rearrange(
                            "p (n c) -> p n c", c=128)
                        banks = (psC, psD, psF, psH)
                        for j, bank in enumerate(banks):
                            last = t.matmul(bank[:, 0:512], d3[:, j],
                                            hid[:, k * 512:(k + 1) * 512],
                                            start=(k == 0), stop=(k == npc - 1))
                        return last
                    val[("w2mm", tau, half, dgrp, k)] = ops.sig("tensor", s_T, mm_w2)
                    w2slot[slot] = val[("w2mm", tau, half, dgrp, k)]
                val[("w2last", tau)] = val[("w2mm", tau, half, dgrp, npc - 1)]
                ops.wait("vector", s_T, val[("w2mm", tau, half, dgrp, npc - 1)])

                def v_w2ev(v, tau=tau, half=half, dgrp=dgrp):
                    banks = (psC, psD, psF, psH)
                    last = None
                    for j, bank in enumerate(banks):
                        dc = dgrp * 4 + j
                        col = ncol(b, dc, tau * 512)
                        if half == 0:
                            v.scalar_tensor_tensor(out=xn_sb[:, col:col + 511],
                                                   in0=bank[:, 0:511],
                                                   scalar=b2p[:, dc:dc + 1],
                                                   in1=xn_sb[:, col:col + 511],
                                                   op0=ALU.add, op1=ALU.add)
                            last = v.tensor_scalar(out=lcbuf[:, dc:dc + 1],
                                                   in0=bank[:, 511:512],
                                                   scalar1=b2p[:, dc:dc + 1],
                                                   scalar2=None, op0=ALU.add)
                        else:
                            v.scalar_tensor_tensor(out=xn_sb[:, col:col + 511],
                                                   in0=bank[:, 0:511], scalar=1.0,
                                                   in1=xn_sb[:, col:col + 511],
                                                   op0=ALU.mult, op1=ALU.add)
                            last = v.scalar_tensor_tensor(
                                out=lcbuf[:, dc:dc + 1], in0=bank[:, 511:512],
                                scalar=1.0, in1=lcbuf[:, dc:dc + 1],
                                op0=ALU.mult, op1=ALU.add)
                    return last
                val[("w2ev", tau, half, dgrp)] = ops.sig("vector", s_V, v_w2ev)
    ops.wait("vector", s_V, s_V.v)

    def v_lcf(v):
        last = None
        for dc in range(NCH):
            col = ncol(b, dc, T - 1)
            last = v.tensor_tensor(out=xn_sb[:, col:col + 1],
                                   in0=xn_sb[:, col:col + 1],
                                   in1=lcbuf[:, dc:dc + 1], op=ALU.add)
        return last
    ops.sig("vector", s_V, v_lcf)
    ops.barrier(f"ffnB_{l}_{b}")


def emit_l1(c, dram):
    nc, ops = c.nc, c.ops
    x_sb, xn_sb = c.t["x_sb"], c.t["xn_sb"]
    ident, identf = c.t["ident"], c.t["identf"]
    m_tiu, m_upp = c.t["m_tiu"], c.t["m_upp"]
    psA, psB, psC, psD = c.t["psA"], c.t["psB"], c.t["psC"], c.t["psD"]
    psF, psG, psH, psI = c.t["psF"], c.t["psG"], c.t["psH"], c.t["psI"]
    s_T, s_A, s_V, s_G = (ops.sem("l1_T"), ops.sem("l1_A"), ops.sem("l1_V"),
                          ops.sem("l1_G"))

    base = c.addr_of("WT0")
    end = c.addr_of("S0") + 2048
    cur = [base]

    def arena(name, shape, dtype):
        import concourse.mybir as mb
        per = 1
        for d in shape[1:]:
            per *= d
        per *= mb.dt.size(dtype)
        off = (cur[0] + 31) // 32 * 32
        assert off + per <= end, f"arena2 overflow {name}: {off+per-base} > {end-base}"
        cur[0] = off + per
        h = nc.alloc_sbuf_tensor_at(name, shape, dtype, offset=off)
        c.t[name] = h
        return h

    wq_g = arena("wq_g", [128, 8 * 512], BF16)
    wk_g = arena("wk_g", [128, 8 * 512], BF16)
    wv_g = arena("wv_g", [128, 8 * 512], BF16)
    wo_g = arena("wo_g", [128, 4 * 1024], BF16)
    w8T = arena("w8T", [128, 8 * 1024], F8)
    bias8 = arena("bias8", [128, 8], F32)
    stSK = arena("stSK", [64, 8 * 64], BF16)
    stCQ = arena("stCQ", [64, 8 * 64], BF16)
    stG = arena("stG", [64, 8 * 64], BF16)

    krb = c.addr_of("krT")
    lnx = c.addr_of("ln_xsq")

    def at(name, shape, dtype, addr):
        h = nc.alloc_sbuf_tensor_at(name, shape, dtype, offset=addr)
        c.t[name] = h
        return h
    qg = at("qg", [128, 512], BF16, krb + 0)
    kg = at("kg", [128, 512], BF16, krb + 1024)
    vg = at("vg", [128, 512], BF16, krb + 2048)
    qTs = at("qTs", [64, 1024], BF16, krb + 3072)
    kTs = at("kTs", [64, 1024], BF16, krb + 5120)
    oTs = at("oTs", [128, 512], BF16, krb + 7168)
    Asb = at("Asb", [128, 256], BF16, lnx + 0)
    Atsb = at("Atsb", [128, 256], BF16, lnx + 512)
    Btsb = at("Btsb", [128, 256], BF16, lnx + 1024)
    Awmsb = at("Awmsb", [128, 256], BF16, lnx + 1536)
    ZTsb = at("ZTsb", [128, 256], BF16, lnx + 2048)
    Wmsb = at("Wmsb", [128, 128], BF16, lnx + 2560)
    Hmsb = at("Hmsb", [64, 256], BF16, lnx + 2816)
    D0sb = at("D0sb", [64, 128], BF16, lnx + 3328)
    KKsb = at("KKsb", [64, 128], BF16, lnx + 3584)
    osb = at("osb", [128, 128], BF16, lnx + 3840)
    xn8 = at("xn8", [128, 1024], F8, lnx + 4096)
    Esb = at("Esb", [128, 256], BF16, lnx + 5120)
    wstg = c.t["xstg"]

    s_lw = [ops.sem("l1_lw0"), ops.sem("l1_lw1")]

    def queue_group_weights(grp):
        sw = s_lw[grp]

        def g_w(g, grp=grp, sw=sw):
            for wbuf, nm in ((wq_g, "Wq1"), (wk_g, "Wk1"), (wv_g, "Wv1")):
                src = dram[nm][:, grp * 512:(grp + 1) * 512].rearrange(
                    "(n p) j -> p n j", p=128)
                g.dma_start(out=wbuf[:].rearrange("p (n j) -> p n j", j=512),
                            in_=src).then_inc(sw.h, 16)
            src = dram["Wo1"][grp * 512:(grp + 1) * 512, :].rearrange(
                "(n p) d -> p n d", p=128)
            g.dma_start(out=wo_g[:].rearrange("p (n d) -> p n d", d=1024),
                        in_=src).then_inc(sw.h, 16)
        sw.v += 64
        ops.on("gpsimd", g_w)
    queue_group_weights(0)

    ops.barrier("l1_start")

    s_ws = [ops.sem("l1_ws0"), ops.sem("l1_ws1")]
    s_bd = ops.sem("l1_bd")
    ops.sig("sync", s_bd,
            lambda e: e.dma_start(out=wstg[0:8, 0:128],
                                  in_=dram["bf1"].rearrange("(a b) -> a b", b=128)),
            n=16)
    ops.wait("tensor", s_bd, s_bd.v)
    vb = ops.sig("tensor", s_T,
                 lambda t: t.transpose(psD[:, 0:8], wstg[0:8, 0:128],
                                       identf[0:8, 0:8]))
    ops.wait("vector", s_T, vb)
    vbe = ops.sig("vector", s_V, lambda v: v.tensor_copy(bias8[:], psD[:, 0:8]))
    ops.wait("sync", s_V, vbe)
    w8tr = {}
    w8cast = {}
    for jb in range(8):
        slot = jb % 2
        if jb >= 2:
            ops.wait("sync", s_T, w8tr[jb - 2])
        ops.sig("sync", s_ws[slot],
                lambda e, jb=jb, slot=slot: e.dma_start(
                    out=wstg[:, slot * 1024:(slot + 1) * 1024],
                    in_=dram["Wf1"][jb * 128:(jb + 1) * 128, :]), n=16)
        ops.wait("tensor", s_ws[slot], s_ws[slot].v)
        if jb >= 1:
            ops.wait("tensor", s_V, w8cast[jb - 1])

        def mm_w8(t, slot=slot):
            last = None
            for ci in range(8):
                bank = psA if ci < 4 else psB
                last = t.transpose(bank[:, (ci % 4) * 128:(ci % 4) * 128 + 128],
                                   wstg[:, slot * 1024 + ci * 128:
                                        slot * 1024 + ci * 128 + 128], identf[:])
            return last
        w8tr[jb] = ops.sig("tensor", s_T, mm_w8)
        ops.wait("vector", s_T, w8tr[jb])

        def v_w8(v, jb=jb):
            last = None
            for ci in range(8):
                bank = psA if ci < 4 else psB
                last = v.tensor_scalar(
                    out=w8T[:, ci * 1024 + jb * 128:ci * 1024 + jb * 128 + 128],
                    in0=bank[:, (ci % 4) * 128:(ci % 4) * 128 + 128],
                    scalar1=448.0, scalar2=-448.0, op0=ALU.min, op1=ALU.max)
            return last
        w8cast[jb] = ops.sig("vector", s_V, v_w8)
    ops.wait("tensor", s_V, w8cast[7])
    ops.barrier("l1_wb")

    for grp in range(min(2, L1_NGRP) if L1_LEVEL >= 1 else 0):
        if grp == 1:
            ops.barrier("l1_g1")
            queue_group_weights(1)
        ops.wait("tensor", s_lw[grp], s_lw[grp].v)
        ops.wait("scalar", s_lw[grp], s_lw[grp].v)
        for b in range(min(NB, L1_NB)):
            zv = s_G.v
            for st_ in (stSK, stCQ, stG):
                ops.sig("gpsimd", s_G,
                        lambda g, st_=st_: g.memset(st_[:], 0.0))
            ops.wait("tensor", s_G, s_G.v)
            ops.wait("vector", s_G, s_G.v)
            pv = {}
            for cc in range(min(CHK, L1_NCC)):
                emit_l1_chunk(c, grp, b, cc, pv)
            ops.barrier(f"l1_{grp}_{b}")

    f8p = {}
    for b in range(NB if L1_LEVEL >= 2 else 0):
        for cc in range(CHK):
            idx = b * CHK + cc
            ops.wait("vector", s_T, f8p.get(("mm", idx - 1), 0))

            def v_c8(v, b=b, cc=cc):
                xin3 = xn_sb[:].rearrange("p (n t) -> p n t", t=TP)[
                    :, b * NCH:(b + 1) * NCH,
                    PAD + cc * 128:PAD + cc * 128 + 128]
                return v.tensor_scalar(
                    out=xn8[:].rearrange("p (n t) -> p n t", t=128),
                    in0=xin3, scalar1=448.0, scalar2=-448.0,
                    op0=ALU.min, op1=ALU.max)
            f8p[("c8", idx)] = ops.sig("vector", s_V, v_c8)
            ops.wait("tensor", s_V, f8p[("c8", idx)])
            if idx >= 1:
                ops.wait("tensor", s_V, f8p[("ev", idx - 1)])

            def mm_f8(t):
                last = None
                for jb in range(8):
                    bank, col = (psD, jb * 128) if jb < 4 else (psH, (jb - 4) * 128)
                    for ci in range(8):
                        last = t.matmul(
                            bank[:, col:col + 128],
                            w8T[:, ci * 1024 + jb * 128:ci * 1024 + jb * 128 + 128],
                            xn8[:, ci * 128:(ci + 1) * 128],
                            start=(ci == 0), stop=(ci == 7))
                return last
            f8p[("mm", idx)] = ops.sig("tensor", s_T, mm_f8)
            ops.wait("vector", s_T, f8p[("mm", idx)])

            def v_f8ev(v, b=b, cc=cc):
                last = None
                for jb in range(8):
                    bank, col = (psD, jb * 128) if jb < 4 else (psH, (jb - 4) * 128)
                    xs = x_sb[:, xcol(b, jb, cc * 128):xcol(b, jb, cc * 128) + 128]
                    last = v.scalar_tensor_tensor(
                        out=xs, in0=bank[:, col:col + 128],
                        scalar=bias8[:, jb:jb + 1], in1=xs,
                        op0=ALU.add, op1=ALU.add)
                return last
            f8p[("ev", idx)] = ops.sig("vector", s_V, v_f8ev)
    ops.barrier("l1_done")


def emit_l1_chunk(c, grp, b, cc, pv):
    nc, ops = c.nc, c.ops
    xn_sb, x_sb = c.t["xn_sb"], c.t["x_sb"]
    ident = c.t["ident"]
    m_tiu, m_upp = c.t["m_tiu"], c.t["m_upp"]
    psA, psB, psC, psD = c.t["psA"], c.t["psB"], c.t["psC"], c.t["psD"]
    psF, psG, psH, psI = c.t["psF"], c.t["psG"], c.t["psH"], c.t["psI"]
    wq_g, wk_g, wv_g, wo_g = c.t["wq_g"], c.t["wk_g"], c.t["wv_g"], c.t["wo_g"]
    qg, kg, vg = c.t["qg"], c.t["kg"], c.t["vg"]
    qTs, kTs, oTs = c.t["qTs"], c.t["kTs"], c.t["oTs"]
    Asb, Atsb, Btsb = c.t["Asb"], c.t["Atsb"], c.t["Btsb"]
    Awmsb, ZTsb, Wmsb = c.t["Awmsb"], c.t["ZTsb"], c.t["Wmsb"]
    Hmsb, D0sb, KKsb, osb = c.t["Hmsb"], c.t["D0sb"], c.t["KKsb"], c.t["osb"]
    Esb = c.t["Esb"]
    stSK, stCQ, stG = c.t["stSK"], c.t["stCQ"], c.t["stG"]
    s_T, s_A, s_V, s_G = (ops.sem("l1_T"), ops.sem("l1_A"), ops.sem("l1_V"),
                          ops.sem("l1_G"))

    ops.wait("tensor", s_V, max(pv.get(("ZTadd", 6), 0), pv.get(("Oadd", 6), 0),
                                pv.get(("ZTadd", 7), 0), pv.get(("Oadd", 7), 0),
                                pv.get(("SKup", 7), 0), pv.get(("CQup", 7), 0),
                                pv.get(("Gup", 7), 0)))
    ops.wait("tensor", s_A, max(pv.get(("Wmev", 6), 0), pv.get(("Wmev", 7), 0),
                                pv.get(("KKev", 6), 0), pv.get(("KKev", 7), 0)))
    ops.wait("tensor", s_V, pv.get(("Woev", 0), 0))

    def mm_proj(t):
        last = None
        for wbuf, bank in ((wq_g, psA), (wk_g, psB), (wv_g, psC)):
            for fc in range(NCH):
                last = t.matmul(
                    bank[:, 0:512],
                    xn_sb[:, ncol(b, fc, cc * 128):ncol(b, fc, cc * 128) + 128],
                    wbuf[:, fc * 512:(fc + 1) * 512],
                    start=(fc == 0), stop=(fc == NCH - 1))
        return last
    vproj = ops.sig("tensor", s_T, mm_proj)
    ops.wait("scalar", s_T, vproj)
    ops.wait("scalar", s_T, max(pv.get(("QtV", 7), 0), pv.get(("KtWm", 7), 0),
                                pv.get(("ZV", 7), 0), pv.get(("Wm", 7), 0)))

    def a_pev(a):
        a.activation(out=qg[:], in_=psA[:, 0:512], func=AF.Copy)
        a.activation(out=kg[:], in_=psB[:, 0:512], func=AF.Copy)
        return a.activation(out=vg[:], in_=psC[:, 0:512], func=AF.Copy)
    pev = ops.sig("scalar", s_A, a_pev)

    ops.wait("tensor", s_A, pev)
    ops.wait("tensor", s_A, max(pv.get(("Atev", 6), 0), pv.get(("Atev", 7), 0),
                                pv.get(("oTev", 2), 0), pv.get(("oTev", 3), 0)))

    def mm_qkT(t):
        last = None
        for hl in range(8):
            t.transpose(psG[0:64, hl * 128:hl * 128 + 128],
                        qg[:, hl * 64:hl * 64 + 64], ident[:])
            last = t.transpose(psI[0:64, hl * 128:hl * 128 + 128],
                               kg[:, hl * 64:hl * 64 + 64], ident[:])
        return last
    vqkT = ops.sig("tensor", s_T, mm_qkT)
    ops.wait("scalar", s_T, vqkT)
    ops.wait("scalar", s_T, max(pv.get(("QD0", 7), 0), pv.get(("E", 7), 0),
                                pv.get(("A", 7), 0), pv.get(("Hm", 7), 0)))

    def a_qkTev(a):
        last = None
        for hl in range(8):
            a.activation(out=qTs[0:64, hl * 128:hl * 128 + 128],
                         in_=psG[0:64, hl * 128:hl * 128 + 128], func=AF.Copy)
            last = a.activation(out=kTs[0:64, hl * 128:hl * 128 + 128],
                                in_=psI[0:64, hl * 128:hl * 128 + 128],
                                func=AF.Copy)
        return last
    qkTev = ops.sig("scalar", s_A, a_qkTev)

    nv = {}
    for hl in range(8):
        m = hl % 2
        cp = hl // 2
        D_ = psD if m == 0 else psH
        P_ = psA if m == 0 else psB
        pT = psG if m == 0 else psI
        qT_ = qTs[0:64, hl * 128:hl * 128 + 128]
        kT_ = kTs[0:64, hl * 128:hl * 128 + 128]
        qg_ = qg[:, hl * 64:hl * 64 + 64]
        kg_ = kg[:, hl * 64:hl * 64 + 64]
        vg_ = vg[:, hl * 64:hl * 64 + 64]
        sk_ = stSK[0:64, hl * 64:hl * 64 + 64]
        cq_ = stCQ[0:64, hl * 64:hl * 64 + 64]
        g_ = stG[0:64, hl * 64:hl * 64 + 64]
        mA = slice(m * 128, m * 128 + 128)
        m64 = slice(m * 64, m * 64 + 64)
        pC0 = m * 192

        def g2(key, d=2):
            return nv.get((key, hl - d), pv.get((key, 8 + hl - d), 0))

        ops.wait("tensor", s_A, max(qkTev, g2("Aev")))
        nv[("A", hl)] = ops.sig("tensor", s_T, lambda t, qT_=qT_, kT_=kT_, D_=D_:
                                t.matmul(D_[:, 0:128], qT_, kT_,
                                         start=True, stop=True))
        ops.wait("tensor", s_V, pv.get(("Gup", hl), 0))
        ops.wait("tensor", s_A, g2("Hmev"))
        nv[("Hm", hl)] = ops.sig(
            "tensor", s_T,
            lambda t, sk_=sk_, qT_=qT_, m=m: t.matmul(
                psF[0:64, m * 256:m * 256 + 128], sk_, qT_, start=True, stop=True))
        ops.wait("scalar", s_T, nv[("Hm", hl)])
        ops.wait("scalar", s_T, max(g2("At"), 0))
        ops.wait("scalar", s_V, g2("Awm"))

        def a_ahm(a, D_=D_, m=m, mA=mA):
            a.activation(out=Asb[:, mA], in_=D_[:, 0:128], func=AF.Copy)
            return a.activation(out=Hmsb[0:64, mA], in_=psF[0:64, m * 256:m * 256 + 128],
                                func=AF.Copy)
        nv[("Aev", hl)] = nv[("Hmev", hl)] = ops.sig("scalar", s_A, a_ahm)
        ops.wait("tensor", s_A, nv[("Hmev", hl)])
        ops.wait("tensor", s_A, g2("Eev"))
        nv[("E", hl)] = ops.sig(
            "tensor", s_T,
            lambda t, qT_=qT_, m=m, mA=mA, D_=D_: t.matmul(
                D_[:, 128:256], qT_, Hmsb[0:64, mA], start=True, stop=True))
        ops.wait("scalar", s_T, nv[("E", hl)])
        ops.wait("scalar", s_V, g2("ZTadd"))
        nv[("Eev", hl)] = ops.sig(
            "scalar", s_A,
            lambda a, mA=mA, D_=D_: a.activation(out=Esb[:, mA],
                                                 in_=D_[:, 128:256], func=AF.Copy))
        ops.wait("tensor", s_A, max(nv[("Aev", hl)], g2("Atev")))
        nv[("At", hl)] = ops.sig(
            "tensor", s_T,
            lambda t, mA=mA, pT=pT: t.transpose(pT[:, 0:128], Asb[:, mA], ident[:]))
        ops.wait("scalar", s_T, nv[("At", hl)])
        ops.wait("scalar", s_V, g2("Btm"))
        ops.wait("scalar", s_T, g2("FT"))
        nv[("Atev", hl)] = ops.sig(
            "scalar", s_A,
            lambda a, mA=mA, pT=pT: a.activation(out=Atsb[:, mA], in_=pT[:, 0:128],
                                                 func=AF.Copy))
        ops.wait("vector", s_A, nv[("Atev", hl)])
        ops.wait("vector", s_T, max(g2("FT"), g2("Wm")))

        def v_bt(v, mA=mA):
            v.tensor_tensor(out=Btsb[:, mA], in0=Atsb[:, mA], in1=m_tiu[:],
                            op=ALU.mult)
            return v.tensor_tensor(out=Awmsb[:, mA], in0=Asb[:, mA], in1=m_upp[:],
                                   op=ALU.mult)
        nv[("Btm", hl)] = nv[("Awm", hl)] = ops.sig("vector", s_V, v_bt)
        ops.wait("tensor", s_V, max(nv[("Btm", hl)], g2("ZTadd")))
        nv[("FT", hl)] = ops.sig(
            "tensor", s_T,
            lambda t, mA=mA, P_=P_: t.matmul(P_[:, 0:128], Btsb[:, mA],
                                             Atsb[:, mA], start=True, stop=True))
        ops.wait("tensor", s_A, g2("Wmev"))
        nv[("Wm", hl)] = ops.sig(
            "tensor", s_T,
            lambda t, mA=mA, P_=P_, vg_=vg_: t.matmul(
                P_[:, 192:256], Awmsb[:, mA], vg_, start=True, stop=True))
        ops.wait("scalar", s_T, nv[("Wm", hl)])
        ops.wait("scalar", s_T, g2("KtWm"))
        nv[("Wmev", hl)] = ops.sig(
            "scalar", s_A,
            lambda a, m64=m64, P_=P_: a.activation(out=Wmsb[:, m64],
                                                   in_=P_[:, 192:256], func=AF.Copy))
        ops.wait("tensor", s_A, max(g2("KKev"), 0))
        ops.wait("tensor", s_V, max(g2("SKup"), g2("CQup")))
        nv[("KK", hl)] = ops.sig(
            "tensor", s_T,
            lambda t, kg_=kg_, pC0=pC0: t.matmul(psC[0:64, pC0:pC0 + 64], kg_, kg_,
                                                 start=True, stop=True))
        nv[("QtV", hl)] = ops.sig(
            "tensor", s_T,
            lambda t, qg_=qg_, vg_=vg_, pC0=pC0: t.matmul(
                psC[0:64, pC0 + 64:pC0 + 128], qg_, vg_, start=True, stop=True))
        ops.wait("scalar", s_T, nv[("KK", hl)])
        ops.wait("scalar", s_T, g2("KKC"))
        nv[("KKev", hl)] = ops.sig(
            "scalar", s_A,
            lambda a, m64=m64, pC0=pC0: a.activation(out=KKsb[0:64, m64],
                                                     in_=psC[0:64, pC0:pC0 + 64],
                                                     func=AF.Copy))
        ops.wait("tensor", s_V, max(g2("D0ev"), pv.get(("Gup", hl), 0)))
        nv[("SC", hl)] = ops.sig(
            "tensor", s_T,
            lambda t, sk_=sk_, cq_=cq_, m=m: t.matmul(
                psF[0:64, m * 256 + 128:m * 256 + 192], sk_, cq_,
                start=True, stop=True))
        ops.wait("vector", s_T, nv[("SC", hl)])
        ops.wait("vector", s_T, g2("QD0"))
        nv[("D0ev", hl)] = ops.sig(
            "vector", s_V,
            lambda v, m=m, m64=m64, g_=g_: v.tensor_tensor(
                out=D0sb[0:64, m64], in0=psF[0:64, m * 256 + 128:m * 256 + 192],
                in1=g_, op=ALU.subtract))
        ops.wait("tensor", s_V, nv[("D0ev", hl)])
        ops.wait("tensor", s_V, g2("Oadd"))
        nv[("QD0", hl)] = ops.sig(
            "tensor", s_T,
            lambda t, qT_=qT_, m64=m64, D_=D_: t.matmul(
                D_[:, 256:320], qT_, D0sb[0:64, m64], start=True, stop=True))
        ops.wait("vector", s_A, nv[("Eev", hl)])
        ops.wait("vector", s_T, max(nv[("FT", hl)], g2("ZV")))
        def v_zt(v, mA=mA, P_=P_):
            v.tensor_tensor(out=ZTsb[:, mA], in0=P_[:, 0:128], in1=Esb[:, mA],
                            op=ALU.add)
            v.drain()
            return v.tensor_tensor(out=ZTsb[:, mA], in0=ZTsb[:, mA],
                                   in1=m_tiu[:], op=ALU.mult)
        nv[("ZTadd", hl)] = nv[("ZTm", hl)] = ops.sig("vector", s_V, v_zt)
        ops.wait("tensor", s_V, nv[("ZTm", hl)])
        ops.wait("tensor", s_A, g2("ZVev"))
        nv[("ZV", hl)] = ops.sig(
            "tensor", s_T,
            lambda t, mA=mA, P_=P_, vg_=vg_: t.matmul(
                P_[:, 128:192], ZTsb[:, mA], vg_, start=True, stop=True))
        ops.wait("scalar", s_T, nv[("ZV", hl)])
        ops.wait("scalar", s_T, nv.get(("oT", cp - 1), pv.get(("oT", 3), 0)))
        nv[("ZVev", hl)] = ops.sig(
            "scalar", s_A,
            lambda a, m64=m64, P_=P_: a.activation(out=osb[:, m64],
                                                   in_=P_[:, 128:192], func=AF.Copy))
        ops.wait("vector", s_A, nv[("ZVev", hl)])
        ops.wait("vector", s_T, nv[("QD0", hl)])
        nv[("Oadd", hl)] = ops.sig(
            "vector", s_V,
            lambda v, m64=m64, D_=D_: v.tensor_tensor(
                out=osb[:, m64], in0=D_[:, 256:320], in1=osb[:, m64],
                op=ALU.add))
        ops.wait("tensor", s_A, nv[("Wmev", hl)])
        nv[("KtWm", hl)] = ops.sig(
            "tensor", s_T,
            lambda t, kg_=kg_, m64=m64, pC0=pC0: t.matmul(
                psC[0:64, pC0 + 128:pC0 + 192], kg_, Wmsb[:, m64],
                start=True, stop=True))
        ops.wait("tensor", s_A, nv[("KKev", hl)])
        nv[("KKC", hl)] = ops.sig(
            "tensor", s_T,
            lambda t, m64=m64, cq_=cq_, m=m: t.matmul(
                psF[0:64, m * 256 + 192:m * 256 + 256], KKsb[0:64, m64], cq_,
                start=True, stop=True))
        ops.wait("vector", s_T, max(nv[("KK", hl)], nv[("QtV", hl)],
                                    nv[("KtWm", hl)], nv[("KKC", hl)]))

        def v_upd(v, sk_=sk_, cq_=cq_, g_=g_, m=m, pC0=pC0):
            v.tensor_tensor(out=sk_, in0=sk_, in1=psC[0:64, pC0:pC0 + 64],
                            op=ALU.add)
            v.tensor_tensor(out=g_, in0=g_,
                            in1=psF[0:64, m * 256 + 192:m * 256 + 256], op=ALU.add)
            v.drain()
            v.tensor_tensor(out=g_, in0=g_, in1=psC[0:64, pC0 + 128:pC0 + 192],
                            op=ALU.add)
            return v.tensor_tensor(out=cq_, in0=cq_,
                                   in1=psC[0:64, pC0 + 64:pC0 + 128], op=ALU.add)
        last_upd = ops.sig("vector", s_V, v_upd)
        nv[("SKup", hl)] = nv[("CQup", hl)] = nv[("Gup", hl)] = last_upd

        if m == 1:
            ops.wait("tensor", s_V, max(nv[("Oadd", hl - 1)], nv[("Oadd", hl)]))
            ops.wait("tensor", s_A,
                     nv.get(("oTev", cp - 2), pv.get(("oTev", cp + 2), 0)))
            pTo = psG if cp % 2 == 0 else psI
            nv[("oT", cp)] = ops.sig(
                "tensor", s_T,
                lambda t, pTo=pTo: t.transpose(pTo[:, 128:256], osb[:], ident[:]))
            ops.wait("scalar", s_T, nv[("oT", cp)])
            ops.wait("scalar", s_T, pv.get(("Wo", 0), 0) if cp == 0 else 0)
            nv[("oTev", cp)] = ops.sig(
                "scalar", s_A,
                lambda a, cp=cp, pTo=pTo: a.activation(
                    out=oTs[:, cp * 128:cp * 128 + 128], in_=pTo[:, 128:256],
                    func=AF.Copy))

    ops.wait("tensor", s_A, nv[("oTev", 3)])
    ops.wait("tensor", s_V, max(nv[("Oadd", 6)], nv[("Oadd", 7)],
                                nv[("ZTadd", 6)], nv[("ZTadd", 7)]))
    ops.wait("tensor", s_V, pv.get(("Woev", 0), 0))

    def mm_wo(t):
        last = None
        for dch in range(8):
            bank, col = (psD, dch * 128) if dch < 4 else (psH, (dch - 4) * 128)
            for cp in range(4):
                last = t.matmul(
                    bank[:, col:col + 128],
                    wo_g[:, cp * 1024 + dch * 128:cp * 1024 + dch * 128 + 128],
                    oTs[:, cp * 128:cp * 128 + 128],
                    start=(cp == 0), stop=(cp == 3))
        return last
    nv[("Wo", 0)] = ops.sig("tensor", s_T, mm_wo)
    ops.wait("vector", s_T, nv[("Wo", 0)])

    def v_woev(v):
        last = None
        for dch in range(8):
            bank, col = (psD, dch * 128) if dch < 4 else (psH, (dch - 4) * 128)
            xs = x_sb[:, xcol(b, dch, cc * 128):xcol(b, dch, cc * 128) + 128]
            last = v.scalar_tensor_tensor(out=xs, in0=bank[:, col:col + 128],
                                          scalar=1.0, in1=xs,
                                          op0=ALU.mult, op1=ALU.add)
        return last
    nv[("Woev", 0)] = ops.sig("vector", s_V, v_woev)

    pv.clear()
    pv.update(nv)


def make_in_maps(inputs):
    t_cs, t_sn, t_scale = make_tables()
    in_maps = []
    for core in range(8):
        m = {
            "x_in": np.ascontiguousarray(inputs["L0_attn_x"][2 * core:2 * core + 2]),
            "W0": inputs["L0_attn_W"],
            "k_norm_w": inputs["L0_attn_k_norm_w"],
            "o_norm_w": inputs["L0_attn_o_norm_w"],
            "o_norm_b": inputs["L0_attn_o_norm_b"],
            "eta_raw": inputs["L0_attn_eta_raw"],
            "lam_logit": inputs["L0_attn_lam_logit"],
            "wqkv": inputs["L0_ffn_wqkv"],
            "q_scale": inputs["L0_ffn_q_scale"],
            "k_scale": inputs["L0_ffn_k_scale"],
            "wo": inputs["L0_ffn_wo"],
            "ln_g": inputs["L0_ffn_ln_g"],
            "ln_b": inputs["L0_ffn_ln_b"],
            "w1": inputs["L0_ffn_w1"],
            "b1": inputs["L0_ffn_b1"],
            "w2": inputs["L0_ffn_w2"],
            "b2": inputs["L0_ffn_b2"],
            "Wq1": inputs["L1_attn_W_q"],
            "Wk1": inputs["L1_attn_W_k"],
            "Wv1": inputs["L1_attn_W_v"],
            "Wo1": inputs["L1_attn_W_o"],
            "Wf1": inputs["L1_ffn_weight"],
            "bf1": inputs["L1_ffn_bias"],
            "t_cs": t_cs, "t_sn": t_sn, "t_scale": t_scale,
        }
        in_maps.append({k: np.ascontiguousarray(v) for k, v in m.items()})
    return in_maps


def kernel(**inputs):
    nc = build_program()
    in_maps = make_in_maps(inputs)
    res = run_bass_kernel_spmd(nc, in_maps, core_ids=list(range(8)))
    return np.concatenate([r["y"] for r in res.results], axis=0)


# version 07
# speedup vs baseline: 1.2060x; 1.2060x over previous
import math
import numpy as np
import concourse.bass as bass
import concourse.mybir as mybir
from concourse.bass_utils import run_bass_kernel_spmd

F32 = mybir.dt.float32
BF16 = mybir.dt.bfloat16
F8 = mybir.dt.float8e4
AF = mybir.ActivationFunctionType
ALU = mybir.AluOpType

NB = 2
T = 2048
D = 1024
NCH = 8
PAD = 8
TP = T + PAD
CHK = T // 128
H0 = 16
DH0 = 64
WIN = 512
HF = 8
DHF = 64
HID = 2730
HIDP = 2816
NHC = HIDP // 128
LN_EPS = 1e-5
RMS_EPS = 1.1920929e-07
ENGS = ("tensor", "vector", "scalar", "gpsimd", "sync")
L0A_LEVEL = 99
L0A_NSTEP = 99
L1_LEVEL = 99
L1_NGRP = 2
L1_NB = 2
L1_NCC = 99


def xcol(b, ch, t0=0):
    return (b * NCH + ch) * T + t0


def ncol(b, ch, t0=0):
    return (b * NCH + ch) * TP + PAD + t0


class Sem:
    def __init__(self, nc, name):
        self.h = nc.alloc_semaphore(name)
        self.v = 0
        self.name = name


class Ops:

    def __init__(self, nc):
        self.nc = nc
        self.q = {e: [] for e in ENGS}
        self.sems = {}

    def sem(self, name):
        if name not in self.sems:
            self.sems[name] = Sem(self.nc, name)
        return self.sems[name]

    def on(self, eng, fn):
        self.q[eng].append(fn)

    def wait(self, eng, sem, val):
        self.q[eng].append(lambda e: e.wait_ge(sem.h, val))

    def sig(self, eng, sem, fn, n=1):
        sem.v += n
        val = sem.v

        def run(e):
            ins = fn(e)
            assert ins is not None, "sig() closure must return the instruction"
            ins.then_inc(sem.h, n)
        self.q[eng].append(run)
        return val

    def flush(self, block):
        for eng in ENGS:
            if not self.q[eng]:
                continue
            ops = self.q[eng]
            self.q[eng] = []

            def section(e, ops=ops):
                for fn in ops:
                    fn(e)
            getattr(block, eng)(section)

    def barrier(self, tag):
        s = self.sem("bar_" + tag)
        engs = ("tensor", "vector", "scalar", "gpsimd")
        base = s.v
        for eng in engs:
            self.sig(eng, s, lambda e: e.drain())
        for eng in engs + ("sync",):
            self.wait(eng, s, base + len(engs))


class Ctx:
    def __init__(self, nc):
        self.nc = nc
        self.ops = Ops(nc)
        self.t = {}
        for nm in ("psA", "psB", "psC", "psD", "psF", "psH"):
            self.ps(nm, [128, 512], F32)
        self.ps("psG", [128, 1024], BF16)
        self.ps("psI", [128, 1024], BF16)

    def sb(self, name, shape, dtype):
        h = self.nc.alloc_sbuf_tensor(name, shape, dtype)
        self.t[name] = h
        return h

    def ps(self, name, shape, dtype=F32):
        h = self.nc.alloc_psum_tensor(name, shape, dtype)
        self.t[name] = h
        return h

    def addr_of(self, name):
        import concourse.mybir as mb
        for a in self.nc.m.functions[0].allocations:
            if isinstance(a, mb.MemoryLocationSet):
                for ml in a.memorylocations:
                    if ml.name == name or ml.name.startswith(name + "_"):
                        return ml.addr
        raise KeyError(name)

    def alias(self, name, shape, dtype, host, off_bytes=0):
        base = self.addr_of(self.t[host].name if host in self.t else host)
        h = self.nc.alloc_sbuf_tensor_at(name, shape, dtype,
                                         offset=base + off_bytes)
        self.t[name] = h
        return h


def emit_load(c, dram):
    nc, ops = c.nc, c.ops
    x_sb = c.sb("x_sb", [128, NB * NCH * T], BF16)
    xn_sb = c.sb("xn_sb", [128, NB * NCH * TP], BF16)
    s_in = ops.sem("s_in")

    ident = c.sb("ident", [128, 128], BF16)
    identf = c.sb("identf", [128, 128], F32)
    iota_pr = c.sb("iota_pr", [128, 128], F32)
    iota_p1 = c.sb("iota_p1", [128, 1], F32)
    iota_rev = c.sb("iota_rev", [128, 1], F32)
    m_low = c.sb("m_low", [128, 128], BF16)
    m_upp = c.sb("m_upp", [128, 128], BF16)
    m_til = c.sb("m_til", [128, 128], BF16)
    m_tiu = c.sb("m_tiu", [128, 128], BF16)
    ones_col = c.sb("ones_col", [128, 1], BF16)
    ones_row = c.sb("ones_row", [1, 128], BF16)
    ones_rowf = c.sb("ones_rowf", [1, 128], F32)

    def g_init(g):
        n = 0
        for tile, const in ((ones_col, 1.0), (ones_row, 1.0), (ones_rowf, 1.0)):
            g.memset(tile[:], const).then_inc(s_pre.h, 1)
            n += 1
        g.iota(iota_pr[:], pattern=[[-1, 128]], base=0, channel_multiplier=1,
               allow_small_or_imprecise_dtypes=True).then_inc(s_pre.h, 1)
        g.iota(iota_p1[:], pattern=[[0, 1]], base=1, channel_multiplier=1,
               allow_small_or_imprecise_dtypes=True).then_inc(s_pre.h, 1)
        g.iota(iota_rev[:], pattern=[[0, 1]], base=127, channel_multiplier=-1,
               allow_small_or_imprecise_dtypes=True).then_inc(s_pre.h, 1)
        g.memset(xn_sb[:], 0.0).then_inc(s_pre.h, 1)

    def v_masks(v):
        v.tensor_scalar(out=ident[:], in0=iota_pr[:], scalar1=0.0, scalar2=None,
                        op0=ALU.is_equal)
        v.tensor_scalar(out=identf[:], in0=iota_pr[:], scalar1=0.0, scalar2=None,
                        op0=ALU.is_equal)
        v.tensor_scalar(out=m_low[:], in0=iota_pr[:], scalar1=0.0, scalar2=None,
                        op0=ALU.is_gt)
        v.tensor_scalar(out=m_upp[:], in0=iota_pr[:], scalar1=0.0, scalar2=None,
                        op0=ALU.is_lt)
        v.tensor_scalar(out=m_til[:], in0=iota_pr[:], scalar1=0.0, scalar2=None,
                        op0=ALU.is_ge)
        return v.tensor_scalar(out=m_tiu[:], in0=iota_pr[:], scalar1=0.0,
                               scalar2=None, op0=ALU.is_le)

    s_pre = ops.sem("s_pre")
    s_pre.v += 7
    ops.on("gpsimd", g_init)
    ops.wait("vector", s_pre, s_pre.v)
    ops.sig("vector", s_pre, v_masks)
    for eng in ("tensor", "scalar", "sync", "gpsimd", "vector"):
        ops.wait(eng, s_pre, s_pre.v)

    xstg = c.sb("xstg", [128, 2 * 1024], F32)
    ps_t0, ps_t1 = c.t["psA"], c.t["psB"]
    s_tr = ops.sem("s_tr")
    s_tev = ops.sem("s_tev")
    s_ld = [ops.sem("s_ld0"), ops.sem("s_ld1")]
    x_in = dram["x_in"]
    items = [(b, tc) for b in range(NB) for tc in range(CHK)]
    for idx, (b, tc) in enumerate(items):
        buf = idx % 2
        if idx >= 2:
            ops.wait("sync", s_tr, idx - 1)
        ops.sig("sync", s_ld[buf],
                lambda e, b=b, tc=tc, buf=buf: e.dma_start(
                    out=xstg[:, buf * 1024:(buf + 1) * 1024],
                    in_=x_in[b, tc * 128:(tc + 1) * 128, :]), n=16)
        ops.wait("tensor", s_ld[buf], s_ld[buf].v)
        if idx >= 1:
            ops.wait("tensor", s_tev, idx - 1 + 1)

        def mm_tr(t, buf=buf):
            last = None
            for ch in range(NCH):
                ps = (ps_t0 if ch % 2 == 0 else ps_t1)
                last = t.transpose(ps[:, (ch // 2) * 128:(ch // 2) * 128 + 128],
                                   xstg[:, buf * 1024 + ch * 128:buf * 1024 + ch * 128 + 128],
                                   identf[:])
            return last
        ops.sig("tensor", s_tr, mm_tr)
        ops.wait("vector", s_tr, idx + 1)

        def v_ev(v, b=b, tc=tc):
            last = None
            for ch in range(NCH):
                ps = (ps_t0 if ch % 2 == 0 else ps_t1)
                last = v.tensor_copy(
                    x_sb[:, xcol(b, ch, tc * 128):xcol(b, ch, tc * 128) + 128],
                    ps[:, (ch // 2) * 128:(ch // 2) * 128 + 128])
            return last
        ops.sig("vector", s_tev, v_ev)
    for eng in ("tensor", "vector", "scalar", "gpsimd"):
        ops.wait(eng, s_tev, s_tev.v)


def emit_top_ln(c, tag):
    nc, ops = c.nc, c.ops
    x_sb, xn_sb = c.t["x_sb"], c.t["xn_sb"]
    ones_col, ones_rowf = c.t["ones_col"], c.t["ones_rowf"]
    ones_row = c.t["ones_row"]

    if "ln_xsq" not in c.t:
        c.sb("ln_xsq", [128, NCH * 512], BF16)
    xsq = c.t["ln_xsq"]
    if "ln_stats" not in c.t:
        c.sb("ln_stats", [1, 1024], BF16)
    stats = c.t["ln_stats"]
    ps_s, ps_q = c.t["psA"], c.t["psB"]
    ps_mu, ps_rs = c.t["psD"], c.t["psF"]

    s_sq = ops.sem(tag + "sq")
    s_mm = ops.sem(tag + "mm")
    s_st = ops.sem(tag + "st")
    s_rs = ops.sem(tag + "rs")
    s_ev = ops.sem(tag + "ev")

    TBL = min(512, T)
    items = [(b, tb) for b in range(NB) for tb in range(T // TBL)]
    for idx, (b, tb) in enumerate(items):
        if idx >= 1:
            ops.wait("scalar", s_mm, 2 * (idx - 1) + 1)
        xin3 = x_sb[:].rearrange("p (n t) -> p n t", t=T)[
            :, b * NCH:(b + 1) * NCH, tb * TBL:(tb + 1) * TBL]
        ops.sig("scalar", s_sq,
                lambda s, xin3=xin3: s.activation(
                    out=xsq[:].rearrange("p (n t) -> p n t", t=512)[:, :, 0:TBL],
                    in_=xin3, func=AF.Square))
        ops.wait("tensor", s_sq, idx + 1)
        if idx >= 1:
            ops.wait("tensor", s_st, idx)

        def mm_sums(t, b=b, tb=tb):
            for chx in range(NCH):
                xin = x_sb[:, xcol(b, chx, tb * TBL):xcol(b, chx, tb * TBL) + TBL]
                t.matmul(ps_s[0:1, 0:TBL], ones_col[:], xin,
                         start=(chx == 0), stop=(chx == NCH - 1))
                last = t.matmul(ps_q[0:1, 0:TBL], ones_col[:],
                                xsq[:, chx * 512:chx * 512 + TBL],
                                start=(chx == 0), stop=(chx == NCH - 1))
            return last
        ops.sig("tensor", s_mm, mm_sums)
        ops.wait("vector", s_mm, 2 * idx + 1)

        def v_stats(v):
            v.tensor_scalar(out=stats[0:1, 0:TBL], in0=ps_s[0:1, 0:TBL],
                            scalar1=-1.0 / D, scalar2=None, op0=ALU.mult)
            v.drain()
            v.tensor_tensor(out=stats[0:1, 512:512 + TBL], in0=stats[0:1, 0:TBL],
                            in1=stats[0:1, 0:TBL], op=ALU.mult)
            v.drain()
            v.scalar_tensor_tensor(out=stats[0:1, 512:512 + TBL], in0=ps_q[0:1, 0:TBL],
                                   scalar=1.0 / D, in1=stats[0:1, 512:512 + TBL],
                                   op0=ALU.mult, op1=ALU.subtract)
            v.drain()
            v.tensor_scalar(out=stats[0:1, 512:512 + TBL], in0=stats[0:1, 512:512 + TBL],
                            scalar1=float(LN_EPS), scalar2=None, op0=ALU.add)
            v.drain()
            with nc.allow_low_precision(reason="bf16 LN stats"):
                return v.reciprocal(stats[0:1, 512:512 + TBL],
                                    stats[0:1, 512:512 + TBL])
        ops.sig("vector", s_st, v_stats)
        ops.wait("scalar", s_st, idx + 1)
        def a_sqrt(s):
            with nc.allow_low_precision(reason="bf16 LN stats"):
                return s.sqrt(stats[0:1, 512:512 + TBL], stats[0:1, 512:512 + TBL])
        ops.sig("scalar", s_rs, a_sqrt)
        ops.wait("tensor", s_rs, idx + 1)
        if idx >= 1:
            ops.wait("tensor", s_ev, idx)
        def mm_bcast(t):
            t.matmul(ps_mu[:, 0:TBL], ones_row[:], stats[0:1, 0:TBL],
                     start=True, stop=True)
            return t.matmul(ps_rs[:, 0:TBL], ones_row[:],
                            stats[0:1, 512:512 + TBL], start=True, stop=True)
        ops.sig("tensor", s_mm, mm_bcast)
        ops.wait("vector", s_mm, 2 * idx + 2)

        def v_norm(v, b=b, tb=tb):
            xin3 = x_sb[:].rearrange("p (n t) -> p n t", t=T)[
                :, b * NCH:(b + 1) * NCH, tb * TBL:(tb + 1) * TBL]
            xout3 = xn_sb[:].rearrange("p (n t) -> p n t", t=TP)[
                :, b * NCH:(b + 1) * NCH, PAD + tb * TBL:PAD + (tb + 1) * TBL]
            mub = ps_mu[:, None, 0:TBL].broadcast_to([128, NCH, TBL])
            rsb = ps_rs[:, None, 0:TBL].broadcast_to([128, NCH, TBL])
            v.tensor_tensor(out=xout3, in0=xin3, in1=mub, op=ALU.add)
            v.drain()
            return v.tensor_tensor(out=xout3, in0=xout3, in1=rsb, op=ALU.mult)
        ops.sig("vector", s_ev, v_norm)


def make_tables():
    half = np.arange(0, DHF, 2, dtype=np.float64)
    inv = 1.0 / 10000.0 ** (half / DHF)
    t = np.arange(T, dtype=np.float64)
    fr = t[:, None] * inv[None, :]
    base = (half + 0.4 * DHF) / (1.4 * DHF)
    pw = (t - T // 2) / float(WIN // 2)
    sc = base[None, :] ** pw[:, None]
    return (np.cos(fr).astype(np.float32), np.sin(fr).astype(np.float32),
            sc.astype(np.float32))


def declare_inputs(nc):
    dram = {}

    def din(name, shape, dtype=F32):
        dram[name] = nc.dram_tensor(name, list(shape), dtype,
                                    kind="ExternalInput").ap()
    din("x_in", (NB, T, D))
    din("W0", (D, D))
    din("k_norm_w", (DH0,))
    din("o_norm_w", (D,))
    din("o_norm_b", (D,))
    din("eta_raw", (H0,))
    din("lam_logit", (H0,))
    din("wqkv", (2, D, 1536))
    din("q_scale", (2, DHF))
    din("k_scale", (2, DHF))
    din("wo", (2, 512, D))
    din("ln_g", (2, D))
    din("ln_b", (2, D))
    din("w1", (2, D, 5460))
    din("b1", (2, 5460))
    din("w2", (2, HID, D))
    din("b2", (2, D))
    din("Wq1", (D, D))
    din("Wk1", (D, D))
    din("Wv1", (D, D))
    din("Wo1", (D, D))
    din("Wf1", (D, D))
    din("bf1", (D,))
    din("t_cs", (T, 32))
    din("t_sn", (T, 32))
    din("t_scale", (T, 32))
    dram["s_w1"] = nc.dram_tensor("s_w1", [2, 44, 128, 1024], BF16,
                                  kind="Internal").ap()
    dram["s_w2"] = nc.dram_tensor("s_w2", [2, 22, 2, 128, 512], BF16,
                                  kind="Internal").ap()
    dram["y"] = nc.dram_tensor("y", [NB, T, D], F32, kind="ExternalOutput").ap()
    return dram


def build_program(stages=None):
    nc = bass.Bass("TRN2", target_bir_lowering=False)
    c = Ctx(nc)
    dram = declare_inputs(nc)
    mode = "full" if stages is None else stages[0]
    with nc.Block() as block:
        emit_load(c, dram)
        emit_top_ln(c, "ln0")
        if mode in ("l0a", "l0", "full"):
            emit_l0_attn(c, dram)
            if mode != "l0a":
                emit_w_scratch(c, dram)
        if mode in ("l0", "full"):
            emit_l0_ffn(c, dram)
        if mode == "full":
            emit_top_ln(c, "ln1")
            emit_l1(c, dram)
        emit_store(c, dram, debug=(mode == "ln"))
        c.ops.flush(block)
    return nc


def emit_store(c, dram, debug=False):
    ops = c.ops
    ident = c.t["ident"]
    y = dram["y"]
    ystg = c.t["xstg"]
    ps_g = c.t["psG"]
    ps_y0 = ps_g[:, 0:512]
    ps_y1 = ps_g[:, 512:1024]
    s_ytr = ops.sem("s_ytr")
    s_yev = ops.sem("s_yev")
    s_outs = [ops.sem("s_out0"), ops.sem("s_out1")]
    ops.barrier("store")
    src_t = c.t["xn_sb"] if debug else c.t["x_sb"]

    def scol(b, ch, t0):
        return ncol(b, ch, t0) if debug else xcol(b, ch, t0)

    items = [(b, tc) for b in range(NB) for tc in range(CHK)]
    for idx, (b, tc) in enumerate(items):
        buf = idx % 2
        if idx >= 1:
            ops.wait("tensor", s_yev, idx - 1 + 1)

        def mm_tr(t, b=b, tc=tc):
            last = None
            for ch in range(NCH):
                ps = ps_y0 if ch % 2 == 0 else ps_y1
                last = t.transpose(ps[:, (ch // 2) * 128:(ch // 2) * 128 + 128],
                                   src_t[:, scol(b, ch, tc * 128):scol(b, ch, tc * 128) + 128],
                                   ident[:])
            return last
        ops.sig("tensor", s_ytr, mm_tr)
        if idx >= 2:
            ops.wait("vector", s_outs[buf], s_outs[buf].v)
        ops.wait("vector", s_ytr, idx + 1)

        def v_ev(v, buf=buf):
            last = None
            for ch in range(NCH):
                ps = ps_y0 if ch % 2 == 0 else ps_y1
                last = v.tensor_copy(
                    ystg[:, buf * 1024 + ch * 128:buf * 1024 + (ch + 1) * 128],
                    ps[:, (ch // 2) * 128:(ch // 2) * 128 + 128])
            return last
        ops.sig("vector", s_yev, v_ev)
        ops.wait("sync", s_yev, idx + 1)
        ops.sig("sync", s_outs[buf],
                lambda e, b=b, tc=tc, buf=buf: e.dma_start(
                    out=y[b, tc * 128:(tc + 1) * 128, :],
                    in_=ystg[:, buf * 1024:(buf + 1) * 1024]), n=16)
    ops.wait("sync", s_outs[0], s_outs[0].v)
    ops.wait("sync", s_outs[1], s_outs[1].v)


def emit_l0_attn(c, dram):
    nc, ops = c.nc, c.ops
    x_sb, xn_sb = c.t["x_sb"], c.t["xn_sb"]
    ident, identf = c.t["ident"], c.t["identf"]
    iota_pr, iota_p1, iota_rev = c.t["iota_pr"], c.t["iota_p1"], c.t["iota_rev"]
    m_low = c.t["m_low"]
    ones_rowf = c.t["ones_rowf"]
    psA, psB, psC, psD = c.t["psA"], c.t["psB"], c.t["psC"], c.t["psD"]
    psF, psG, psH, psI = c.t["psF"], c.t["psG"], c.t["psH"], c.t["psI"]
    xsq = c.t["ln_xsq"]

    WT0 = c.sb("WT0", [128, NCH * 1024], BF16)
    knw2 = c.sb("knw2", [128, 1], F32)
    onw = c.sb("onw", [128, NCH], F32)
    onb = c.sb("onb", [128, NCH], F32)
    ones2 = c.sb("ones2", [128, 2], BF16)
    ident2 = c.sb("ident2", [2, 2], BF16)
    sel2 = c.sb("sel2", [2, 128], F32)
    iotaLM = c.sb("iotaLM", [128, 128], F32)
    iotaUM = c.sb("iotaUM", [128, 128], F32)
    prow = c.sb("prow", [1, 64], F32)
    pbc = c.sb("pbc", [128, 48], F32)
    negb_bc = c.sb("negb_bc", [128, 16], F32)
    lamC_bc = c.sb("lamC_bc", [128, 16], F32)
    gneg = c.sb("gneg", [128, 16], F32)
    sdec = c.sb("sdec", [128, 16], F32)
    dpair = c.sb("dpair", [128, 32 * 128], BF16)
    vneg = c.sb("vneg", [128, 1024], BF16)
    kt = c.sb("kt", [128, 1024], BF16)
    r2i = c.alias("r2i", [2, 1024], F32, "xstg", 0)
    rstd2 = c.alias("rstd2", [2, 1024], F32, "xstg", 4096)
    krow = c.sb("krow", [2, 1024], BF16)
    kbt = c.sb("kbt", [128, 16], F32)
    osum = c.sb("osum", [128, 1024], BF16)
    lnvec = c.sb("lnvec", [128, 4], F32)
    nh = c.sb("nh", [128, 2 * 256], BF16)
    nhT = c.sb("nhT", [128, 2 * 256], BF16)
    sqp = c.sb("sqp", [128, 2 * 5 * 512], BF16)
    rbuf = c.sb("rbuf", [128, 2 * 2 * 128], BF16)
    kdec = c.sb("kdec", [128, 2 * 128], BF16)
    S0 = c.sb("S0", [128, NB * 8 * 64], BF16)

    def sq_tile(px, j):
        o = (px * 9 + j) * 256
        return sqp[:, o:o + 256]

    s_pr = ops.sem("a_pr")
    s_T = ops.sem("a_T")
    s_V = ops.sem("a_V")
    s_A = ops.sem("a_A")
    s_ld = ops.sem("a_ld")
    s_gd = ops.sem("a_gd")

    s_pd = ops.sem("a_pd")

    def g_dma(g):
        g.dma_start(out=prow[0:1, 0:16], in_=dram["lam_logit"][None, :]).then_inc(s_pd.h, 16)
        g.dma_start(out=prow[0:1, 16:32], in_=dram["eta_raw"][None, :]).then_inc(s_pd.h, 16)
        g.dma_start(out=knw2[0:64, 0:1], in_=dram["k_norm_w"][:, None]).then_inc(s_pd.h, 16)
        g.dma_start(out=knw2[64:128, 0:1], in_=dram["k_norm_w"][:, None]).then_inc(s_pd.h, 16)
    s_pd.v += 64
    ops.on("gpsimd", g_dma)

    def g_tiles(g):
        g.iota(iotaLM[:], pattern=[[-1, 128]], base=0, channel_multiplier=1,
               allow_small_or_imprecise_dtypes=True).then_inc(s_pr.h, 1)
        g.iota(iotaUM[:], pattern=[[1, 128]], base=0, channel_multiplier=-1,
               allow_small_or_imprecise_dtypes=True).then_inc(s_pr.h, 1)
        g.memset(ones2[:], 0.0).then_inc(s_pr.h, 1)
        g.memset(sel2[:], 1.0).then_inc(s_pr.h, 1)
        g.memset(ident2[:], 1.0).then_inc(s_pr.h, 1)
    s_pr.v += 5
    ops.on("gpsimd", g_tiles)
    ops.wait("gpsimd", s_pr, s_pr.v)

    def g_tiles2(g):
        g.affine_select(out=iotaLM[:], in_=iotaLM[:], compare_op=ALU.is_gt,
                        fill=1e30, base=0, pattern=[[-1, 128]],
                        channel_multiplier=1).then_inc(s_pr.h, 1)
        g.affine_select(out=iotaUM[:], in_=iotaUM[:], compare_op=ALU.is_gt,
                        fill=1e30, base=0, pattern=[[1, 128]],
                        channel_multiplier=-1).then_inc(s_pr.h, 1)
        g.memset(ones2[0:64, 0:1], 1.0).then_inc(s_pr.h, 1)
        g.memset(ones2[64:128, 1:2], 1.0).then_inc(s_pr.h, 1)
        g.affine_select(out=sel2[:], in_=sel2[:], compare_op=ALU.is_ge,
                        fill=0.0, base=0, pattern=[[1, 128]],
                        channel_multiplier=-64).then_inc(s_pr.h, 1)
        g.affine_select(out=ident2[:], in_=ident2[:], compare_op=ALU.not_equal,
                        fill=0.0, base=0, pattern=[[-1, 2]],
                        channel_multiplier=1).then_inc(s_pr.h, 1)
        g.memset(S0[:], 0.0).then_inc(s_pr.h, 1)
    s_pr.v += 7
    ops.on("gpsimd", g_tiles2)
    ops.wait("gpsimd", s_pr, s_pr.v)

    def g_tiles3(g):
        g.affine_select(out=sel2[:], in_=sel2[:], compare_op=ALU.is_ge,
                        fill=0.0, base=63, pattern=[[-1, 128]],
                        channel_multiplier=64).then_inc(s_pr.h, 1)
    s_pr.v += 1
    ops.on("gpsimd", g_tiles3)

    ops.wait("scalar", s_pd, 64)

    def a_prep(a):
        a.activation(out=prow[0:1, 32:48], in_=prow[0:1, 0:16], func=AF.Sigmoid)
        a.drain()
        a.activation(out=prow[0:1, 48:64], in_=prow[0:1, 16:32], func=AF.Tanh)
        a.drain()
        return a.activation(out=prow[0:1, 0:16], in_=prow[0:1, 32:48], func=AF.Ln)
    ops.sig("scalar", s_A, a_prep)
    ops.wait("vector", s_A, s_A.v)

    def v_beta(v):
        v.tensor_scalar(out=prow[0:1, 32:48], in0=prow[0:1, 32:48], scalar1=-1.0,
                        scalar2=None, op0=ALU.add)
        v.drain()
        v.tensor_tensor(out=prow[0:1, 16:32], in0=prow[0:1, 48:64],
                        in1=prow[0:1, 32:48], op=ALU.mult)
        v.drain()
        return v.tensor_scalar(out=prow[0:1, 16:32], in0=prow[0:1, 16:32],
                               scalar1=1.0 / 64.0, scalar2=None, op0=ALU.mult)
    ops.sig("vector", s_V, v_beta)
    ops.wait("tensor", s_V, s_V.v)
    ops.sig("tensor", s_T,
            lambda t: t.matmul(psD[:, 0:32], ones_rowf[:], prow[0:1, 0:32],
                               start=True, stop=True))
    ops.wait("vector", s_T, s_T.v)

    def v_pbc(v):
        v.tensor_copy(pbc[:, 0:32], psD[:, 0:32])
        v.drain()
        return v.tensor_scalar(out=negb_bc[:], in0=pbc[:, 16:32], scalar1=-1.0,
                               scalar2=None, op0=ALU.mult)
    ops.sig("vector", s_V, v_pbc)
    ops.wait("scalar", s_V, s_V.v)

    def a_exp(a):
        a.activation(out=lamC_bc[:], in_=pbc[:, 0:16], func=AF.Exp, scale=128.0)
        for h in range(16):
            a.activation(out=gneg[:, h:h + 1], in_=iota_p1[:], func=AF.Exp,
                         scale=pbc[:, h:h + 1])
        last = None
        for h in range(16):
            last = a.activation(out=sdec[:, h:h + 1], in_=iota_rev[:], func=AF.Exp,
                                scale=pbc[:, h:h + 1])
        return last
    ops.sig("scalar", s_A, a_exp)
    ops.wait("vector", s_A, s_A.v)

    def v_gs(v):
        v.tensor_scalar(out=gneg[:], in0=gneg[:], scalar1=-1.0, scalar2=None,
                        op0=ALU.mult)
        return v.tensor_tensor(out=sdec[:], in0=sdec[:], in1=pbc[:, 16:32],
                               op=ALU.mult)
    ops.sig("vector", s_V, v_gs)
    ops.wait("scalar", s_pr, s_pr.v)

    dtmp = c.t["ln_xsq"]
    for h in range(16):
        for which, src in ((0, iotaLM), (1, iotaUM)):
            slot = (2 * h + which) % 4
            dsl = dtmp[:, 2048 + slot * 128:2048 + (slot + 1) * 128]
            if 2 * h + which >= 4:
                ops.wait("scalar", s_V, s_V.v - 3)
            ops.sig("scalar", s_A,
                    lambda a, src=src, h=h, dsl=dsl: a.activation(
                        out=dsl, in_=src[:], func=AF.Exp, scale=pbc[:, h:h + 1]))
            ops.wait("vector", s_A, s_A.v)
            idx = 4 * (h // 2) + (h % 2) + 2 * which
            ops.sig("vector", s_V,
                    lambda v, h=h, idx=idx, dsl=dsl: v.tensor_scalar(
                        out=dpair[:, idx * 128:(idx + 1) * 128],
                        in0=dsl, scalar1=negb_bc[:, h:h + 1], scalar2=None,
                        op0=ALU.mult))

    cstg = c.sb("cstg", [128, 128], F32)
    for name, dst in (("o_norm_w", onw), ("o_norm_b", onb)):
        ops.wait("sync", s_T, s_T.v)
        ops.sig("sync", s_ld,
                lambda e, name=name: e.dma_start(
                    out=cstg[0:8, 0:128],
                    in_=dram[name].rearrange("(a b) -> a b", b=128)), n=16)
        ops.wait("tensor", s_ld, s_ld.v)
        ops.wait("tensor", s_V, s_V.v)
        ops.sig("tensor", s_T,
                lambda t: t.transpose(psD[:, 0:8], cstg[0:8, 0:128], identf[0:8, 0:8]))
        ops.wait("vector", s_T, s_T.v)
        ops.sig("vector", s_V,
                lambda v, dst=dst: v.tensor_copy(dst[:], psD[:, 0:8]))

    ops.barrier("l0attn_wprep")
    wld = c.t["ln_xsq"]
    wtb_vals = []
    s_gw = [ops.sem("a_gw0"), ops.sem("a_gw1")]
    for oc in range(NCH):
        buf = oc % 2
        if oc >= 2:
            ops.wait("gpsimd", s_T, wtb_vals[oc - 2])
        ops.sig("gpsimd", s_gw[buf],
                lambda g, oc=oc, buf=buf: g.dma_start(
                    out=wld[:, buf * 1024:(buf + 1) * 1024],
                    in_=dram["W0"][oc * 128:(oc + 1) * 128, :]), n=16)
        ops.wait("tensor", s_gw[buf], s_gw[buf].v)

        def mm_wt_a(t, oc=oc, buf=buf):
            last = None
            for dc in range(4):
                last = t.transpose(psG[:, dc * 128:dc * 128 + 128],
                                   wld[:, buf * 1024 + dc * 128:buf * 1024 + dc * 128 + 128],
                                   ident[:])
            return last

        def mm_wt_b(t, oc=oc, buf=buf):
            last = None
            for dc in range(4, 8):
                last = t.transpose(psG[:, dc * 128:dc * 128 + 128],
                                   wld[:, buf * 1024 + dc * 128:buf * 1024 + dc * 128 + 128],
                                   ident[:])
            return last
        if oc >= 1:
            ops.wait("tensor", s_V, s_V.v)
        vta = ops.sig("tensor", s_T, mm_wt_a)
        ops.wait("vector", s_T, vta)

        def ev_a(v, oc=oc):
            last = None
            for dc in range(4):
                last = v.tensor_scalar(
                    out=WT0[:, dc * 1024 + oc * 128:dc * 1024 + oc * 128 + 128],
                    in0=psG[:, dc * 128:dc * 128 + 128], scalar1=-1.0, scalar2=None,
                    op0=ALU.mult)
            return last
        ops.sig("vector", s_V, ev_a)
        ops.wait("tensor", s_V, s_V.v)
        vtb = ops.sig("tensor", s_T, mm_wt_b)
        wtb_vals.append(vtb)
        ops.wait("vector", s_T, vtb)

        def ev_b(v, oc=oc):
            last = None
            for dc in range(4, 8):
                last = v.tensor_scalar(
                    out=WT0[:, dc * 1024 + oc * 128:dc * 1024 + oc * 128 + 128],
                    in0=psG[:, dc * 128:dc * 128 + 128], scalar1=-1.0, scalar2=None,
                    op0=ALU.mult)
            return last
        ops.sig("vector", s_V, ev_b)

    ops.barrier("l0attn_prep")

    if L0A_LEVEL >= 1:
        state_ev = {}
        for b in range(NB):
            for cc in range(CHK):
                emit_l0_chunk(c, b, cc, state_ev)

    ops.barrier("l0attn_done")


def emit_l0_chunk(c, b, cc, state_ev):
    nc, ops = c.nc, c.ops
    x_sb, xn_sb = c.t["x_sb"], c.t["xn_sb"]
    psA, psB, psC, psD = c.t["psA"], c.t["psB"], c.t["psC"], c.t["psD"]
    psF, psG, psH, psI = c.t["psF"], c.t["psG"], c.t["psH"], c.t["psI"]
    WT0, knw2, onw, onb = c.t["WT0"], c.t["knw2"], c.t["onw"], c.t["onb"]
    ones2, ident2, sel2 = c.t["ones2"], c.t["ident2"], c.t["sel2"]
    pbc, negb_bc, lamC_bc = c.t["pbc"], c.t["negb_bc"], c.t["lamC_bc"]
    gneg, sdec, dpair = c.t["gneg"], c.t["sdec"], c.t["dpair"]
    vneg, kt, r2i, rstd2 = c.t["vneg"], c.t["kt"], c.t["r2i"], c.t["rstd2"]
    krow, kbt, osum, lnvec = c.t["krow"], c.t["kbt"], c.t["osum"], c.t["lnvec"]
    nh, nhT, sqp, rbuf, kdec = c.t["nh"], c.t["nhT"], c.t["sqp"], c.t["rbuf"], c.t["kdec"]
    S0 = c.t["S0"]
    ident = c.t["ident"]
    xsq = c.t["ln_xsq"]
    s_T, s_V, s_A = ops.sem("a_T"), ops.sem("a_V"), ops.sem("a_A")

    def sq_tile(px, j):
        o = (px * 9 + j) * 256
        return c.t["sqp"][:, o:o + 256]

    xn3 = xn_sb[:].rearrange("p (n t) -> p n t", t=TP)[
        :, b * NCH:(b + 1) * NCH, PAD + cc * 128:PAD + cc * 128 + 128]

    ops.wait("tensor", s_V, s_V.v)
    ops.wait("tensor", s_A, s_A.v)

    def mm_static(t):
        last = None
        for half, ps in ((0, psA), (1, psB)):
            for dc in range(NCH):
                last = t.matmul(
                    ps[:, 0:512],
                    xn_sb[:, ncol(b, dc, cc * 128):ncol(b, dc, cc * 128) + 128],
                    WT0[:, dc * 1024 + half * 512:dc * 1024 + (half + 1) * 512],
                    start=(dc == 0), stop=(dc == NCH - 1))
        return last
    vs = ops.sig("tensor", s_T, mm_static)
    ops.wait("scalar", s_T, vs)

    def a_vneg(a):
        a.activation(out=vneg[:, 0:512], in_=psA[:, 0:512], func=AF.Copy)
        return a.activation(out=vneg[:, 512:1024], in_=psB[:, 0:512], func=AF.Copy)
    va_v = ops.sig("scalar", s_A, a_vneg)

    def a_sq(a):
        a.activation(out=xsq[:, 0:1024].rearrange("p (n t) -> p n t", t=128),
                     in_=xn3, func=AF.Square)
        return a.activation(out=xsq[:, 1024:2048].rearrange("p (n t) -> p n t", t=128),
                            in_=xn3, func=AF.Square, scale=knw2[:, 0:1])
    va_sq = ops.sig("scalar", s_A, a_sq)
    ops.wait("tensor", s_A, va_sq)
    ops.wait("tensor", s_V, s_V.v)

    def mm_hs(t):
        last = None
        for dc in range(NCH):
            ps = psC if dc < 4 else psD
            col = (dc % 4) * 128
            last = t.matmul(ps[0:2, col:col + 128], ones2[:],
                            xsq[:, dc * 128:dc * 128 + 128], start=True, stop=True)
        for dc in range(NCH):
            ps = psF if dc < 4 else psH
            col = (dc % 4) * 128
            last = t.matmul(ps[0:2, col:col + 128], ones2[:],
                            xsq[:, 1024 + dc * 128:1024 + dc * 128 + 128],
                            start=True, stop=True)
        return last
    vhs = ops.sig("tensor", s_T, mm_hs)
    ops.wait("vector", s_T, vhs)

    def v_rstd(v):
        v.tensor_scalar(out=r2i[0:2, 0:512], in0=psC[0:2, 0:512],
                        scalar1=1.0 / 64.0, scalar2=float(RMS_EPS),
                        op0=ALU.mult, op1=ALU.add)
        v.tensor_scalar(out=r2i[0:2, 512:1024], in0=psD[0:2, 0:512],
                        scalar1=1.0 / 64.0, scalar2=float(RMS_EPS),
                        op0=ALU.mult, op1=ALU.add)
        v.drain()
        v.reciprocal(r2i[0:2, 0:512], r2i[0:2, 0:512])
        v.reciprocal(r2i[0:2, 512:1024], r2i[0:2, 512:1024])
        v.drain()
        v.tensor_tensor(out=krow[0:2, 0:512], in0=psF[0:2, 0:512],
                        in1=r2i[0:2, 0:512], op=ALU.mult)
        return v.tensor_tensor(out=krow[0:2, 512:1024], in0=psH[0:2, 0:512],
                               in1=r2i[0:2, 512:1024], op=ALU.mult)
    vr = ops.sig("vector", s_V, v_rstd)
    ops.wait("scalar", s_V, vr)
    ops.sig("scalar", s_A,
            lambda a: a.activation(out=rstd2[0:2, 0:1024], in_=r2i[0:2, 0:1024],
                                   func=AF.Sqrt))
    ops.wait("tensor", s_A, s_A.v)

    def mm_rb(t):
        last = None
        for dc in range(NCH):
            ps = psA if dc < 4 else psB
            col = (dc % 4) * 128
            last = t.matmul(ps[:, col:col + 128], sel2[:],
                            rstd2[0:2, dc * 128:dc * 128 + 128],
                            start=True, stop=True)
        for hp in range(8):
            last = t.matmul(psC[:, 2 * hp:2 * hp + 2],
                            krow[0:2, hp * 128:hp * 128 + 128], ident2[:],
                            start=True, stop=True)
        return last
    vrb = ops.sig("tensor", s_T, mm_rb)
    ops.wait("vector", s_T, vrb)

    def v_kfin(v):
        rb3a = psA[:, None, 0:512].rearrange("p a (n t) -> p (a n) t", t=128)
        rb3b = psB[:, None, 0:512].rearrange("p a (n t) -> p (a n) t", t=128)
        xn3a = xn_sb[:].rearrange("p (n t) -> p n t", t=TP)[
            :, b * NCH:b * NCH + 4, PAD + cc * 128:PAD + cc * 128 + 128]
        xn3b = xn_sb[:].rearrange("p (n t) -> p n t", t=TP)[
            :, b * NCH + 4:b * NCH + 8, PAD + cc * 128:PAD + cc * 128 + 128]
        kt3a = kt[:, 0:512].rearrange("p (n t) -> p n t", t=128)
        kt3b = kt[:, 512:1024].rearrange("p (n t) -> p n t", t=128)
        v.scalar_tensor_tensor(out=kt3a, in0=xn3a, scalar=knw2[:, 0:1], in1=rb3a,
                               op0=ALU.mult, op1=ALU.mult)
        v.scalar_tensor_tensor(out=kt3b, in0=xn3b, scalar=knw2[:, 0:1], in1=rb3b,
                               op0=ALU.mult, op1=ALU.mult)
        v.drain()
        v.tensor_tensor(out=kbt[:], in0=psC[:, 0:16], in1=pbc[:, 16:32], op=ALU.mult)
        v.drain()
        return v.tensor_scalar(out=kbt[:], in0=kbt[:], scalar1=-1.0, scalar2=None,
                               op0=ALU.add)
    vkf = ops.sig("vector", s_V, v_kfin)

    if L0A_LEVEL >= 2:
        emit_scan_pairs(c, b, cc, state_ev, vkf, va_v)

    if L0A_LEVEL >= 3:
        emit_l0_outln(c, b, cc)





def dl_pair(c, pr):
    return c.t["dpair"][:, (4 * pr) * 128:(4 * pr + 2) * 128]


def du_pair(c, pr):
    return c.t["dpair"][:, (4 * pr + 2) * 128:(4 * pr + 4) * 128]


def emit_scan_pairs(c, b, cc, state_ev, vkf, va_v):
    ops = c.ops
    psA, psB, psC, psD = c.t["psA"], c.t["psB"], c.t["psC"], c.t["psD"]
    psF, psG, psH, psI = c.t["psF"], c.t["psG"], c.t["psH"], c.t["psI"]
    kt, kdec, rbuf, nh, nhT = c.t["kt"], c.t["kdec"], c.t["rbuf"], c.t["nh"], c.t["nhT"]
    sqp, S0, osum = c.t["sqp"], c.t["S0"], c.t["osum"]
    vneg, gneg, sdec, lamC_bc, kbt = (c.t["vneg"], c.t["gneg"], c.t["sdec"],
                                      c.t["lamC_bc"], c.t["kbt"])
    ident = c.t["ident"]
    s_T, s_V, s_A = ops.sem("a_T"), ops.sem("a_V"), ops.sem("a_A")
    BANKS = {0: (psA, psB, psC), 1: (psD, psF, psH)}

    def sql(px, lev):
        o = px * 2560 + (lev - 1) * 512
        return sqp[:, o:o + 512]

    val = {}

    def emit_step(pr, step):
        px = pr % 2
        bankE, bankO, alt = BANKS[px]
        he, ho = 2 * pr, 2 * pr + 1
        kte = kt[0:64, pr * 128:(pr + 1) * 128]
        kto = kt[64:128, pr * 128:(pr + 1) * 128]
        scol = (b * 8 + pr) * 64
        Spair = S0[:, scol:scol + 64]
        slot0 = rbuf[:, px * 256:px * 256 + 128]
        slot1 = rbuf[:, px * 256 + 128:px * 256 + 256]
        kd = kdec[:, px * 128:(px + 1) * 128]

        if step == "AP":
            ops.wait("tensor", s_V, max(vkf, state_ev.get((b, pr), 0),
                                        val.get((pr - 2, "SE"), 0)))
            ops.wait("tensor", s_A, max(va_v, val.get((pr - 2, "OS"), 0)))

            def mm(t):
                t.matmul(bankE[:, 0:128], kte, kte, start=True, stop=True)
                t.matmul(bankE[:, 256:320], kte, Spair[0:64, :], start=True, stop=True)
                t.matmul(bankO[:, 0:128], kto, kto, start=True, stop=True)
                return t.matmul(bankO[:, 256:320], kto, Spair[64:128, :],
                                start=True, stop=True)
            val[(pr, "AP")] = ops.sig("tensor", s_T, mm)
        elif step == "EV1":
            ops.wait("vector", s_T, val[(pr, "AP")])
            dp = c.t["dpair"]

            def ev(v):
                o = 4 * pr * 128
                v.tensor_tensor(out=nh[:, px * 256:px * 256 + 128],
                                in0=bankE[:, 0:128], in1=dp[:, o:o + 128],
                                op=ALU.mult)
                v.tensor_tensor(out=nh[:, px * 256 + 128:px * 256 + 256],
                                in0=bankO[:, 0:128], in1=dp[:, o + 128:o + 256],
                                op=ALU.mult)
                v.tensor_tensor(out=nhT[:, px * 256:px * 256 + 128],
                                in0=bankE[:, 0:128], in1=dp[:, o + 256:o + 384],
                                op=ALU.mult)
                v.tensor_tensor(out=nhT[:, px * 256 + 128:px * 256 + 256],
                                in0=bankO[:, 0:128], in1=dp[:, o + 384:o + 512],
                                op=ALU.mult)
                v.scalar_tensor_tensor(out=slot0[:, 0:64], in0=bankE[:, 256:320],
                                       scalar=gneg[:, he:he + 1],
                                       in1=vneg[:, he * 64:he * 64 + 64],
                                       op0=ALU.mult, op1=ALU.add)
                return v.scalar_tensor_tensor(out=slot0[:, 64:128],
                                              in0=bankO[:, 256:320],
                                              scalar=gneg[:, ho:ho + 1],
                                              in1=vneg[:, ho * 64:ho * 64 + 64],
                                              op0=ALU.mult, op1=ALU.add)
            val[(pr, "EV1")] = ops.sig("vector", s_V, ev)
        elif step.startswith("L"):
            lev = int(step[1:])
            psX = alt if lev % 2 == 1 else bankE
            if lev == 1:
                ops.wait("tensor", s_V, val[(pr, "EV1")])
                pe = nh[:, px * 256:px * 256 + 128]
                po = nh[:, px * 256 + 128:px * 256 + 256]
                be = nhT[:, px * 256:px * 256 + 128]
                bo = nhT[:, px * 256 + 128:px * 256 + 256]
            else:
                ops.wait("tensor", s_A, val[(pr, f"E{lev-1}")])
                prev = sql(px, lev - 1)
                pe, po = prev[:, 0:128], prev[:, 128:256]
                be, bo = prev[:, 256:384], prev[:, 384:512]

            def mm(t, pe=pe, po=po, be=be, bo=bo, lev=lev, psX=psX):
                if lev < 5:
                    t.matmul(psX[:, 0:128], be, pe, start=True, stop=True)
                    t.matmul(psX[:, 128:256], bo, po, start=True, stop=True)
                t.matmul(psX[:, 256:384], pe, be, start=True, stop=True)
                return t.matmul(psX[:, 384:512], po, bo, start=True, stop=True)
            val[(pr, step)] = ops.sig("tensor", s_T, mm)
        elif step.startswith("E"):
            lev = int(step[1:])
            psX = alt if lev % 2 == 1 else bankE
            ops.wait("scalar", s_T, val[(pr, f"L{lev}")])
            dst = sql(px, lev)
            if lev < 5:
                ops.sig("scalar", s_A,
                        lambda a, dst=dst, psX=psX: a.activation(
                            out=dst[:, 0:512], in_=psX[:, 0:512], func=AF.Copy))
            else:
                ops.sig("scalar", s_A,
                        lambda a, dst=dst, psX=psX: a.activation(
                            out=dst[:, 256:512], in_=psX[:, 256:512], func=AF.Copy))
            val[(pr, step)] = ops.sem("a_A").v
        elif step.startswith("AE"):
            ai = int(step[2:])
            psX = alt if ai % 2 == 0 else bankE
            ops.wait("scalar" if ai == 0 else "vector", s_T, val[(pr, f"A{ai}")])
            if ai == 0:
                ops.sig("scalar", s_A,
                        lambda a, psX=psX: a.activation(out=slot1[:, 0:128],
                                                        in_=psX[:, 0:128],
                                                        func=AF.Copy))
                val[(pr, "AE0")] = ops.sem("a_A").v
            else:
                def ev(v, psX=psX):
                    return v.scalar_tensor_tensor(out=slot0[:, 0:128],
                                                  in0=psX[:, 0:128], scalar=1.0,
                                                  in1=slot0[:, 0:128],
                                                  op0=ALU.mult, op1=ALU.add)
                val[(pr, f"AE{ai}")] = ops.sig("vector", s_V, ev)
        elif step.startswith("A") and step != "AP":
            ai = int(step[1:])
            psX = alt if ai % 2 == 0 else bankE
            if ai == 0 or ai == 1:
                bt = sql(px, 5)[:, 256:512]
            elif ai < 8:
                lev = 7 - ai
                lev = 7 - ai
                if lev == 0:
                    bt = nhT[:, px * 256:px * 256 + 256]
                else:
                    bt = sql(px, lev)[:, 256:512]
            rhs = slot0 if ai != 1 else slot1
            if ai == 0:
                ops.wait("tensor", s_A, val[(pr, "E5")])
            else:
                ops.wait("tensor", s_V if ai != 1 else s_A, val[(pr, f"AE{ai-1}")])

            def mm(t, bt=bt, rhs=rhs, psX=psX):
                t.matmul(psX[:, 0:64], bt[:, 0:128], rhs[:, 0:64],
                         start=True, stop=True)
                return t.matmul(psX[:, 64:128], bt[:, 128:256], rhs[:, 64:128],
                                start=True, stop=True)
            val[(pr, step)] = ops.sig("tensor", s_T, mm)
        elif step == "TR":
            ops.wait("tensor", s_V, max(vkf, val.get((pr - 2, "KD"), 0)))

            def mm(t):
                t.transpose(psG[:, px * 128:px * 128 + 64], kte, ident[0:64, 0:64])
                return t.transpose(psI[:, px * 128:px * 128 + 64], kto,
                                   ident[64:128, 64:128])
            val[(pr, "TR")] = ops.sig("tensor", s_T, mm)
        elif step == "KD":
            ops.wait("vector", s_T, val[(pr, "TR")])

            def ev(v):
                v.tensor_scalar(out=kd[:, 0:64], in0=psG[:, px * 128:px * 128 + 64],
                                scalar1=sdec[:, he:he + 1], scalar2=None, op0=ALU.mult)
                return v.tensor_scalar(out=kd[:, 64:128],
                                       in0=psI[:, px * 128:px * 128 + 64],
                                       scalar1=sdec[:, ho:ho + 1], scalar2=None,
                                       op0=ALU.mult)
            val[(pr, "KD")] = ops.sig("vector", s_V, ev)
        elif step == "ST":
            ops.wait("tensor", s_V, val[(pr, "KD")])

            def mm(t):
                t.matmul(bankE[0:64, 384:448], kd[:, 0:64], slot0[:, 0:64],
                         start=True, stop=True)
                return t.matmul(bankO[64:128, 384:448], kd[:, 64:128],
                                slot0[:, 64:128], start=True, stop=True,
                                tile_position=(0, 64))
            val[(pr, "ST")] = ops.sig("tensor", s_T, mm)
        elif step == "SE":
            ops.wait("vector", s_T, val[(pr, "ST")])

            def ev(v):
                v.scalar_tensor_tensor(out=Spair[0:64, :], in0=Spair[0:64, :],
                                       scalar=lamC_bc[0:64, he:he + 1],
                                       in1=bankE[0:64, 384:448],
                                       op0=ALU.mult, op1=ALU.add)
                return v.scalar_tensor_tensor(out=Spair[64:128, :],
                                              in0=Spair[64:128, :],
                                              scalar=lamC_bc[64:128, ho:ho + 1],
                                              in1=bankO[64:128, 384:448],
                                              op0=ALU.mult, op1=ALU.add)
            val[(pr, "SE")] = ops.sig("vector", s_V, ev)
            state_ev[(b, pr)] = val[(pr, "SE")]
        elif step == "OS":
            ops.wait("scalar", s_V, val[(pr, "AE7")])

            def ev(a):
                a.activation(out=osum[:, he * 64:he * 64 + 64], in_=slot0[:, 0:64],
                             func=AF.Copy, scale=kbt[:, he:he + 1])
                return a.activation(out=osum[:, ho * 64:ho * 64 + 64],
                                    in_=slot0[:, 64:128], func=AF.Copy,
                                    scale=kbt[:, ho:ho + 1])
            val[(pr, "OS")] = ops.sig("scalar", s_A, ev)

    STEPS = (["AP", "EV1", "L1", "E1", "L2", "E2", "L3", "E3", "L4", "E4",
              "L5", "E5", "A0", "AE0", "A1", "AE1", "A2", "AE2", "A3", "AE3",
              "A4", "AE4", "A5", "AE5", "A6", "AE6", "A7", "AE7",
              "TR", "KD", "ST", "SE", "OS"])[:L0A_NSTEP]
    for g in range(4):
        for step in STEPS:
            for pr in (2 * g, 2 * g + 1):
                emit_step(pr, step)


def emit_l0_outln(c, b, cc):
    ops = c.ops
    osum, lnvec = c.t["osum"], c.t["lnvec"]
    onw, onb = c.t["onw"], c.t["onb"]
    x_sb, ident, psG = c.t["x_sb"], c.t["ident"], c.t["psG"]
    xsq = c.t["ln_xsq"]
    afftmp = xsq[:, 2048:3072]
    s_T, s_V, s_A = ops.sem("a_T"), ops.sem("a_V"), ops.sem("a_A")
    vOS = ops.sem("a_A").v
    vSE = ops.sem("a_V").v

    ops.wait("vector", s_A, vOS)
    ops.sig("vector", ops.sem("a_V"),
            lambda v: v.reduce_sum(lnvec[:, 2:3], osum[:], axis=mybir.AxisListType.X))
    ops.wait("scalar", s_A, vOS)
    ops.sig("scalar", ops.sem("a_A"),
            lambda a: a.activation(out=xsq[:, 3072:4096], in_=osum[:],
                                   func=AF.Square, accum_out=lnvec[:, 3:4]))
    ops.wait("vector", ops.sem("a_A"), ops.sem("a_A").v)

    def v_stats(v):
        v.drain()
        v.tensor_scalar(out=lnvec[:, 0:1], in0=lnvec[:, 2:3], scalar1=-1.0 / D,
                        scalar2=None, op0=ALU.mult)
        v.drain()
        v.tensor_tensor(out=lnvec[:, 2:3], in0=lnvec[:, 0:1], in1=lnvec[:, 0:1],
                        op=ALU.mult)
        v.drain()
        v.scalar_tensor_tensor(out=lnvec[:, 1:2], in0=lnvec[:, 3:4],
                               scalar=1.0 / D, in1=lnvec[:, 2:3],
                               op0=ALU.mult, op1=ALU.subtract)
        v.drain()
        v.tensor_scalar(out=lnvec[:, 1:2], in0=lnvec[:, 1:2],
                        scalar1=float(LN_EPS), scalar2=None, op0=ALU.add)
        v.drain()
        return v.reciprocal(lnvec[:, 1:2], lnvec[:, 1:2])
    vst = ops.sig("vector", ops.sem("a_V"), v_stats)
    ops.wait("scalar", ops.sem("a_V"), vst)
    vsq = ops.sig("scalar", ops.sem("a_A"),
                  lambda a: a.sqrt(lnvec[:, 1:2], lnvec[:, 1:2]))
    ops.wait("vector", ops.sem("a_A"), vsq)
    def v_norm(v):
        v.drain()
        return v.tensor_scalar(out=osum[:], in0=osum[:], scalar1=lnvec[:, 0:1],
                               scalar2=lnvec[:, 1:2], op0=ALU.add, op1=ALU.mult)
    vn = ops.sig("vector", ops.sem("a_V"), v_norm)
    ops.wait("tensor", ops.sem("a_V"), vn)

    def mm_tr(t):
        last = None
        for dc in range(NCH):
            last = t.transpose(psG[:, dc * 128:dc * 128 + 128],
                               osum[:, dc * 128:dc * 128 + 128], ident[:])
        return last
    vtr = ops.sig("tensor", ops.sem("a_T"), mm_tr)
    ops.wait("scalar", ops.sem("a_T"), vtr)

    def a_aff(a):
        last = None
        for dc in range(NCH):
            last = a.activation(out=afftmp[:, dc * 128:dc * 128 + 128],
                                in_=psG[:, dc * 128:dc * 128 + 128],
                                func=AF.Identity,
                                scale=onw[:, dc:dc + 1], bias=onb[:, dc:dc + 1])
        return last
    vaf = ops.sig("scalar", ops.sem("a_A"), a_aff)
    ops.wait("vector", ops.sem("a_A"), vaf)

    def v_add(v):
        last = None
        for dc in range(NCH):
            xs = x_sb[:, xcol(b, dc, cc * 128):xcol(b, dc, cc * 128) + 128]
            last = v.tensor_tensor(out=xs, in0=xs,
                                   in1=afftmp[:, dc * 128:dc * 128 + 128],
                                   op=ALU.add)
        return last
    ops.sig("vector", ops.sem("a_V"), v_add)


def emit_w_scratch(c, dram):
    nc, ops = c.nc, c.ops
    cvt = c.sb("cvt", [128, 4 * 1024], BF16)
    s_ci = [ops.sem(f"cv_i{k}") for k in range(4)]
    s_co = [ops.sem(f"cv_o{k}") for k in range(4)]
    s_cm = ops.sem("cv_m")
    sw1, sw2 = dram["s_w1"], dram["s_w2"]
    items = []
    for l in range(2):
        for hc in range(22):
            for part in range(2):
                items.append(("w1", l, hc, part))
    for idx, (kind, l, hc, x) in enumerate(items):
        slot = idx % 4
        ops.wait("gpsimd", s_co[slot], s_co[slot].v)
        if kind == "w1":
            asz = min(128, HID - hc * 128)
            u = 2 * hc + x
            if asz < 128:
                ops.sig("gpsimd", s_cm, lambda g, slot=slot: g.memset(
                    cvt[:, slot * 1024:(slot + 1) * 1024], 0.0))
                ops.wait("gpsimd", s_cm, s_cm.v)

            def g_in(g, l=l, hc=hc, part=x, asz=asz, slot=slot):
                cb = part * HID + hc * 128
                src = dram["w1"][l, :, cb:cb + asz].rearrange(
                    "(n p) c -> p n c", p=128)
                dst3 = cvt[:, slot * 1024:(slot + 1) * 1024].rearrange(
                    "p (n c) -> p n c", c=128)
                return g.dma_start(out=dst3[:, :, 0:asz] if asz < 128 else dst3,
                                   in_=src)
            ops.sig("gpsimd", s_ci[slot], g_in, n=16)
            ops.wait("gpsimd", s_ci[slot], s_ci[slot].v)
            ops.sig("gpsimd", s_co[slot],
                    lambda g, l=l, u=u, slot=slot: g.dma_start(
                        out=sw1[l, u],
                        in_=cvt[:, slot * 1024:(slot + 1) * 1024]), n=16)
        else:
            dgrp = x
            rsz = min(128, HID - hc * 128)
            tile = cvt[:, slot * 1024:slot * 1024 + 512]
            if rsz < 128:
                ops.sig("gpsimd", s_cm,
                        lambda g, tile=tile: g.memset(tile, 0.0))
                ops.wait("gpsimd", s_cm, s_cm.v)

            def g_in(g, l=l, hc=hc, dgrp=dgrp, rsz=rsz, tile=tile):
                src = dram["w2"][l, hc * 128:hc * 128 + rsz,
                                 dgrp * 512:(dgrp + 1) * 512]
                return g.dma_start(out=tile[0:rsz, :], in_=src)
            ops.sig("gpsimd", s_ci[slot], g_in, n=16)
            ops.wait("gpsimd", s_ci[slot], s_ci[slot].v)
            ops.sig("gpsimd", s_co[slot],
                    lambda g, l=l, hc=hc, dgrp=dgrp, tile=tile: g.dma_start(
                        out=sw2[l, hc, dgrp], in_=tile), n=16)
    c.cv_done = [(sem, sem.v) for sem in s_co]


def emit_l0_ffn(c, dram):
    nc, ops = c.nc, c.ops
    ops.barrier("ffn_start")
    x_sb, xn_sb = c.t["x_sb"], c.t["xn_sb"]

    base = c.addr_of("WT0")
    end = c.addr_of("S0") + 2048
    cur = [base]

    def arena(name, shape, dtype):
        import concourse.mybir as mb
        per = 1
        for d in shape[1:]:
            per *= d
        per *= mb.dt.size(dtype)
        off = (cur[0] + 31) // 32 * 32
        assert off + per <= end, f"arena overflow {name}: {off+per-base} > {end-base}"
        cur[0] = off + per
        h = nc.alloc_sbuf_tensor_at(name, shape, dtype, offset=off)
        c.t[name] = h
        return h

    arena("wqkv_sb", [128, NCH * 1536], BF16)
    arena("TA", [128, 4 * CHK * 64], BF16)
    arena("wo_sb", [128, 4 * 1024], BF16)
    arena("qnb", [128, 1024], BF16)
    arena("ropet", [128, 1024], BF16)
    arena("attnT", [128, 1280], BF16)
    arena("rowq", [1, 128], F32)
    arena("qkbc", [128, 128], F32)
    arena("fstat", [128, 24], F32)
    arena("lgb", [128, 2 * NCH], F32)
    arena("b1p", [128, 48], F32)
    arena("b2p", [128, NCH], F32)
    arena("lcbuf", [128, NCH], F32)
    c.t["tbase"] = nc.alloc_sbuf_tensor_at(
        "tbase", [128, 4 * CHK * 32], F32, offset=c.addr_of("wo_sb"))

    for sem, vv in c.cv_done:
        for eng in ENGS:
            ops.wait(eng, sem, vv)
    cvb = c.addr_of("cvt")
    c.t["krT"] = nc.alloc_sbuf_tensor_at("krT", [128, 5 * 512], BF16,
                                         offset=cvb)
    c.t["ffnpad"] = nc.alloc_sbuf_tensor_at("ffnpad", [128, 1536], BF16,
                                            offset=cvb + 5120)
    c.t["cstg2"] = c.t["cstg"]

    lnx = c.addr_of("ln_xsq")
    xst = c.addr_of("xstg")

    def at(name, shape, dtype, addr):
        h = nc.alloc_sbuf_tensor_at(name, shape, dtype, offset=addr)
        c.t[name] = h
        return h
    at("qT", [128, 512], BF16, lnx + 0)
    at("sqscr", [128, 1024], BF16, lnx + 1024)
    at("vring", [128, 5 * 512], BF16, lnx + 3072)
    at("obuf", [128, 2048], BF16, xst + 0)
    at("attnb", [128, 1280], BF16, xst + 4096)

    krb = c.addr_of("krT")
    at("hid", [128, 8 * 512], BF16, krb + 0)
    at("vtmp", [128, 1024], F32, krb + 0)
    at("gelt", [128, 1024], BF16, c.addr_of("qnb"))
    at("lnout", [128, 8 * 512], BF16, lnx + 0)
    at("w1blk", [128, 3 * 1024], BF16, xst + 0)
    at("w2blk", [128, 1024], BF16, xst + 6144)

    for l in range(2):
        emit_ffn_layer_params(c, dram, l)
        for b in range(NB):
            emit_ffn_phaseA(c, dram, l, b)
            emit_ffn_phaseB(c, dram, l, b)

    ops.barrier("ffn_add")
    s_V = ops.sem("f_V")
    for b in range(NB):
        def v_add(v, b=b):
            last = None
            for dc in range(NCH):
                xs = x_sb[:, xcol(b, dc, 0):xcol(b, dc, 0) + T]
                hn = xn_sb[:, ncol(b, dc, 0):ncol(b, dc, 0) + T]
                last = v.tensor_tensor(out=xs, in0=xs, in1=hn, op=ALU.add)
            return last
        ops.sig("vector", s_V, v_add)
    ops.barrier("ffn_done")


def emit_ffn_layer_params(c, dram, l):
    ops, nc = c.ops, c.nc
    wqkv_sb, wo_sb, TA, tb = c.t["wqkv_sb"], c.t["wo_sb"], c.t["TA"], c.t["tbase"]
    lgb, b1p, b2p = c.t["lgb"], c.t["b1p"], c.t["b2p"]
    rowq, qkbc, cstg2 = c.t["rowq"], c.t["qkbc"], c.t["cstg2"]
    psD = c.t["psD"]
    identf, ones_rowf = c.t["identf"], c.t["ones_rowf"]
    s_V, s_A, s_T, s_G = (ops.sem("f_V"), ops.sem("f_A"), ops.sem("f_T"),
                          ops.sem("f_G"))
    s_pd = ops.sem(f"f_pd{l}")
    s_pg = ops.sem(f"f_pg{l}")
    K32 = CHK * 32
    ops.barrier(f"ffn_params{l}")

    s_dt = ops.sem(f"f_dt{l}")
    for i, name in enumerate(("t_cs", "t_sn", "t_scale")):
        src = dram[name].rearrange("(n p) c -> p n c", p=128)
        dst = tb[:].rearrange("p (k n c) -> p k n c", k=4, c=32)[:, i]
        ops.sig("sync", s_dt, lambda e, src=src, dst=dst: e.dma_start(out=dst,
                                                                     in_=src), n=16)
    ops.wait("vector", s_dt, s_dt.v)
    ops.sig("vector", s_V,
            lambda v: v.reciprocal(tb[:, 3 * K32:4 * K32], tb[:, 2 * K32:3 * K32]))
    ops.sig("sync", s_pd,
            lambda e: e.dma_start(out=rowq[0:1, 0:64],
                                  in_=dram["q_scale"][l][None, :]), n=16)
    ops.sig("sync", s_pd,
            lambda e: e.dma_start(out=rowq[0:1, 64:128],
                                  in_=dram["k_scale"][l][None, :]), n=16)
    ops.wait("tensor", s_pd, s_pd.v)
    ops.sig("tensor", s_T,
            lambda t: t.matmul(psD[:, 0:128], ones_rowf[:], rowq[0:1, 0:128],
                               start=True, stop=True))
    ops.wait("vector", s_T, s_T.v)
    ops.sig("vector", s_V, lambda v: v.tensor_copy(qkbc[:], psD[:, 0:128]))

    def v_tables(v):
        tmp = c.t["vtmp"]
        t3 = lambda a: a.rearrange("p (n c) -> p n c", c=32)
        qs3 = lambda sl: qkbc[:, sl, None].transpose([0, 2, 1]).broadcast_to(
            [128, CHK, 32]) if False else qkbc[:, None, sl].broadcast_to(
            [128, CHK, 32])
        v.drain()
        v.tensor_tensor(out=tmp[:, 0:K32], in0=tb[:, 0:K32],
                        in1=tb[:, 2 * K32:3 * K32], op=ALU.mult)
        v.tensor_tensor(out=tmp[:, K32:2 * K32], in0=tb[:, K32:2 * K32],
                        in1=tb[:, 2 * K32:3 * K32], op=ALU.mult)
        v.drain()
        TAv = TA[:].rearrange("p (k n c) -> p k n c", k=8, c=32)
        v.scalar_tensor_tensor(out=TAv[:, 0], in0=t3(tmp[:, 0:K32]), scalar=8.0,
                               in1=qs3(slice(0, 32)), op0=ALU.mult, op1=ALU.mult)
        v.scalar_tensor_tensor(out=TAv[:, 1], in0=t3(tmp[:, 0:K32]), scalar=8.0,
                               in1=qs3(slice(32, 64)), op0=ALU.mult, op1=ALU.mult)
        v.scalar_tensor_tensor(out=TAv[:, 2], in0=t3(tmp[:, K32:2 * K32]),
                               scalar=-8.0, in1=qs3(slice(32, 64)),
                               op0=ALU.mult, op1=ALU.mult)
        v.scalar_tensor_tensor(out=TAv[:, 3], in0=t3(tmp[:, K32:2 * K32]),
                               scalar=8.0, in1=qs3(slice(0, 32)),
                               op0=ALU.mult, op1=ALU.mult)
        v.drain()
        v.tensor_tensor(out=tmp[:, 0:K32], in0=tb[:, 0:K32],
                        in1=tb[:, 3 * K32:4 * K32], op=ALU.mult)
        v.tensor_tensor(out=tmp[:, K32:2 * K32], in0=tb[:, K32:2 * K32],
                        in1=tb[:, 3 * K32:4 * K32], op=ALU.mult)
        v.drain()
        v.scalar_tensor_tensor(out=TAv[:, 4], in0=t3(tmp[:, 0:K32]), scalar=1.0,
                               in1=qs3(slice(64, 96)), op0=ALU.mult, op1=ALU.mult)
        v.scalar_tensor_tensor(out=TAv[:, 5], in0=t3(tmp[:, 0:K32]), scalar=1.0,
                               in1=qs3(slice(96, 128)), op0=ALU.mult, op1=ALU.mult)
        v.scalar_tensor_tensor(out=TAv[:, 6], in0=t3(tmp[:, K32:2 * K32]),
                               scalar=-1.0, in1=qs3(slice(96, 128)),
                               op0=ALU.mult, op1=ALU.mult)
        return v.scalar_tensor_tensor(out=TAv[:, 7], in0=t3(tmp[:, K32:2 * K32]),
                                      scalar=1.0, in1=qs3(slice(64, 96)),
                                      op0=ALU.mult, op1=ALU.mult)
    vtb = ops.sig("vector", s_V, v_tables)
    ops.wait("gpsimd", s_V, vtb)

    def g_w(g):
        src = dram["wqkv"][l].rearrange("(n p) c -> p n c", p=128)
        g.dma_start(out=wqkv_sb[:].rearrange("p (n c) -> p n c", c=1536),
                    in_=src).then_inc(s_pg.h, 16)
        src2 = dram["wo"][l].rearrange("(n p) c -> p n c", p=128)
        g.dma_start(out=wo_sb[:].rearrange("p (n c) -> p n c", c=1024),
                    in_=src2).then_inc(s_pg.h, 16)
    s_pg.v += 32
    ops.on("gpsimd", g_w)
    for name, dst in (("ln_g", lgb[:, 0:NCH]), ("ln_b", lgb[:, NCH:2 * NCH]),
                      ("b2", b2p[:, 0:NCH])):
        ops.wait("sync", s_T, s_T.v)
        ops.sig("sync", s_pd,
                lambda e, name=name: e.dma_start(
                    out=cstg2[0:8, 0:128],
                    in_=dram[name][l].rearrange("(a b) -> a b", b=128)), n=16)
        ops.wait("tensor", s_pd, s_pd.v)
        ops.wait("tensor", s_V, s_V.v)
        ops.sig("tensor", s_T,
                lambda t: t.transpose(psD[:, 0:8], cstg2[0:8, 0:128],
                                      identf[0:8, 0:8]))
        ops.wait("vector", s_T, s_T.v)
        ops.sig("vector", s_V, lambda v, dst=dst: v.tensor_copy(dst, psD[:, 0:8]))
    s_gz = ops.sem(f"f_gz{l}")
    ops.wait("gpsimd", s_V, s_V.v)
    ops.sig("gpsimd", s_gz, lambda g: g.memset(cstg2[:], 0.0))
    ops.wait("sync", s_gz, s_gz.v)
    for part, coff in ((0, 0), (1, 24)):
        ops.wait("sync", s_T, s_T.v)
        ops.sig("sync", s_pd,
                lambda e, part=part: e.dma_start(
                    out=cstg2[0:21, 0:128],
                    in_=dram["b1"][l, part * 2730:part * 2730 + 2688]
                    .rearrange("(a b) -> a b", b=128)), n=16)
        ops.sig("sync", s_pd,
                lambda e, part=part: e.dma_start(
                    out=cstg2[21:22, 0:42],
                    in_=dram["b1"][l, part * 2730 + 2688:part * 2730 + 2730][None, :]),
                n=16)
        ops.wait("tensor", s_pd, s_pd.v)
        ops.wait("tensor", s_V, s_V.v)
        ops.sig("tensor", s_T,
                lambda t: t.transpose(psD[:, 0:22], cstg2[0:22, 0:128],
                                      identf[0:22, 0:22]))
        ops.wait("vector", s_T, s_T.v)
        vv = ops.sig("vector", s_V,
                     lambda v, coff=coff: v.tensor_copy(b1p[:, coff:coff + 22],
                                                        psD[:, 0:22]))
        ops.wait("sync", s_V, vv)
    ops.wait("tensor", s_pg, s_pg.v)
    ops.barrier(f"ffn_params_done{l}")


def emit_ffn_phaseA(c, dram, l, b):
    ops, nc = c.ops, c.nc
    xn_sb = c.t["xn_sb"]
    wqkv_sb, wo_sb, TA = c.t["wqkv_sb"], c.t["wo_sb"], c.t["TA"]
    krT, vring, qT = c.t["krT"], c.t["vring"], c.t["qT"]
    sqscr, attnb, attnT = c.t["sqscr"], c.t["attnb"], c.t["attnT"]
    qnb, ropet = c.t["qnb"], c.t["ropet"]
    obuf, lcbuf, fstat = c.t["obuf"], c.t["lcbuf"], c.t["fstat"]
    m_til, m_tiu, ident = c.t["m_til"], c.t["m_tiu"], c.t["ident"]
    psA, psB, psC, psD = c.t["psA"], c.t["psB"], c.t["psC"], c.t["psD"]
    psF, psG, psH, psI = c.t["psF"], c.t["psG"], c.t["psH"], c.t["psI"]
    s_V, s_A, s_T, s_G = (ops.sem("f_V"), ops.sem("f_A"), ops.sem("f_T"),
                          ops.sem("f_G"))
    val = {}
    TAv = TA[:].rearrange("p (k n c) -> p k n c", k=8, c=32)

    def hsl(dc, t0, n):
        o = ncol(b, dc, t0) - (1 if dc >= 4 else 0)
        return xn_sb[:, o:o + n]

    for i in range(CHK):
        ops.wait("tensor", s_V, val.get(("qkn", i - 1), 0))
        ops.wait("tensor", s_A, max(val.get(("vev", i - 1), 0),
                                    val.get(("oev", i - 1, 3), 0),
                                    val.get(("sev", i - 1, 3), 0)))

        def mm_qkv(t, i=i):
            last = None
            for part, ps in ((0, psA), (1, psB), (2, psC)):
                for dc in range(NCH):
                    last = t.matmul(
                        ps[:, 0:512], hsl(dc, i * 128, 128),
                        wqkv_sb[:, dc * 1536 + part * 512:dc * 1536 + (part + 1) * 512],
                        start=(dc == 0), stop=(dc == NCH - 1))
            return last
        val[("qkv", i)] = ops.sig("tensor", s_T, mm_qkv)
        if i % 4 == 0 and i > 0:
            ops.wait("vector", s_T, val[("qkv", i)])

            def v_lc(v, i=i):
                last = None
                for dc in range(NCH):
                    col = ncol(b, dc, i * 128 - 1)
                    last = v.tensor_tensor(out=xn_sb[:, col:col + 1],
                                           in0=xn_sb[:, col:col + 1],
                                           in1=lcbuf[:, dc:dc + 1], op=ALU.add)
                return last
            ops.sig("vector", s_V, v_lc)
        ops.wait("scalar", s_T, val[("qkv", i)])

        def a_sq(a, i=i):
            a.activation(out=sqscr[:, 0:512], in_=psA[:, 0:512], func=AF.Square)
            a.activation(out=sqscr[:, 512:1024], in_=psB[:, 0:512], func=AF.Square)
            return a.activation(out=vring[:, (i % 5) * 512:(i % 5) * 512 + 512],
                                in_=psC[:, 0:512], func=AF.Copy)
        val[("vev", i)] = ops.sig("scalar", s_A, a_sq)
        ops.wait("vector", s_A, val[("vev", i)])

        def v_l2(v):
            v.tensor_reduce(out=fstat[:, 0:8, None],
                            in_=sqscr[:, 0:512].rearrange("p (n c) -> p n c", c=64),
                            axis=mybir.AxisListType.X, op=ALU.add)
            return v.tensor_reduce(
                out=fstat[:, 8:16, None],
                in_=sqscr[:, 512:1024].rearrange("p (n c) -> p n c", c=64),
                axis=mybir.AxisListType.X, op=ALU.add)
        vl2 = ops.sig("vector", s_V, v_l2)
        ops.wait("scalar", s_V, vl2)
        va_n = ops.sig("scalar", s_A,
                       lambda a: a.sqrt(fstat[:, 0:16], fstat[:, 0:16]))
        ops.wait("vector", s_A, va_n)

        def v_rstd(v):
            v.tensor_scalar(out=fstat[:, 0:16], in0=fstat[:, 0:16],
                            scalar1=1e-12, scalar2=None, op0=ALU.max)
            v.drain()
            v.reciprocal(fstat[:, 0:16], fstat[:, 0:16])
            v.drain()
            v.tensor_tensor(
                out=qnb[:, 0:512].rearrange("p (n c) -> p n c", c=64),
                in0=psA[:, 0:512].rearrange("p (n c) -> p n c", c=64),
                in1=fstat[:, 0:8, None].broadcast_to([128, 8, 64]), op=ALU.mult)
            return v.tensor_tensor(
                out=qnb[:, 512:1024].rearrange("p (n c) -> p n c", c=64),
                in0=psB[:, 0:512].rearrange("p (n c) -> p n c", c=64),
                in1=fstat[:, 8:16, None].broadcast_to([128, 8, 64]), op=ALU.mult)
        val[("qkn", i)] = ops.sig("vector", s_V, v_rstd)
        def v_rope(v, i=i):
            qn3 = qnb[:, 0:512].rearrange("p (n c) -> p n c", c=64)
            kn3 = qnb[:, 512:1024].rearrange("p (n c) -> p n c", c=64)
            t13 = ropet[:, 0:512].rearrange("p (n c) -> p n c", c=64)
            t23 = ropet[:, 512:1024].rearrange("p (n c) -> p n c", c=64)
            last = None
            for (src3, tA, tB, dst) in ((qn3, 0, 2, sqscr[:, 0:512]),
                                        (kn3, 4, 6, sqscr[:, 512:1024])):
                def ts(k):
                    return TAv[:, k, i, None, :].broadcast_to([128, 8, 32])
                v.drain()
                v.tensor_tensor(out=t13[:, :, 0:32], in0=src3[:, :, 0:32],
                                in1=ts(tA), op=ALU.mult)
                v.tensor_tensor(out=t13[:, :, 32:64], in0=src3[:, :, 32:64],
                                in1=ts(tA + 1), op=ALU.mult)
                v.tensor_tensor(out=t23[:, :, 0:32], in0=src3[:, :, 32:64],
                                in1=ts(tB), op=ALU.mult)
                v.tensor_tensor(out=t23[:, :, 32:64], in0=src3[:, :, 0:32],
                                in1=ts(tB + 1), op=ALU.mult)
                v.drain()
                last = v.tensor_tensor(out=dst, in0=ropet[:, 0:512],
                                       in1=ropet[:, 512:1024], op=ALU.add)
            return last
        val[("rope", i)] = ops.sig("vector", s_V, v_rope)
        ops.wait("tensor", s_V, val[("rope", i)])
        ops.wait("tensor", s_A, val.get(("atev", i - 1, 3), 0))

        def mm_tr(t):
            last = None
            for j in range(8):
                last = t.transpose(psG[:, j * 128:j * 128 + 128],
                                   sqscr[:, j * 128:j * 128 + 128], ident[:])
            return last
        vtr = ops.sig("tensor", s_T, mm_tr)
        ops.wait("scalar", s_T, vtr)

        def a_qkT(a, i=i):
            a.activation(out=qT[:], in_=psG[:, 0:512], func=AF.Copy)
            return a.activation(out=krT[:, (i % 5) * 512:(i % 5) * 512 + 512],
                                in_=psG[:, 512:1024], func=AF.Copy)
        val[("qkT", i)] = ops.sig("scalar", s_A, a_qkT)

        nk = min(i, 4) + 1
        j0 = i - (nk - 1)
        for p in range(4):
            prev_sev = (val.get(("sev", i, p - 1), 0) if p >= 1
                        else val.get(("sev", i - 1, 3), 0))
            ops.wait("tensor", s_A, max(val[("qkT", i)], prev_sev))
            if i % 4 == 0 and i > 0:
                ops.wait("tensor", s_V, val.get(("woev", i // 4 - 1), 0))

            def mm_sc2(t, i=i, p=p, nk=nk, j0=j0):
                last = None
                for par in range(2):
                    main = psD if par == 0 else psH
                    diag = psA if par == 0 else psB
                    pslo = par * 64
                    for jj in range(nk):
                        sl = ((j0 + jj) % 5) * 512 + p * 128
                        out = (main[:, jj * 128:jj * 128 + 128] if jj < nk - 1
                               else diag[:, 0:128])
                        last = t.matmul(
                            out, qT[pslo:pslo + 64, p * 128:p * 128 + 128],
                            krT[pslo:pslo + 64, sl:sl + 128],
                            start=True, stop=True)
                return last
            val[("sc", i, p)] = ops.sig("tensor", s_T, mm_sc2)
            ops.wait("vector", s_T, val[("sc", i, p)])

            def v_mx2(v, nk=nk):
                last = None
                for par in range(2):
                    main = psD if par == 0 else psH
                    diag = psA if par == 0 else psB
                    if nk == 1:
                        last = v.tensor_reduce(
                            out=fstat[:, 16 + par:17 + par], in_=diag[:, 0:128],
                            axis=mybir.AxisListType.X, op=ALU.max, negate=True)
                        continue
                    v.tensor_reduce(out=fstat[:, 16 + par:17 + par],
                                    in_=main[:, 0:(nk - 1) * 128],
                                    axis=mybir.AxisListType.X,
                                    op=ALU.max, negate=True)
                    v.tensor_reduce(out=fstat[:, 20 + par:21 + par],
                                    in_=diag[:, 0:128], axis=mybir.AxisListType.X,
                                    op=ALU.max, negate=True)
                if nk > 1:
                    v.drain()
                    for par in range(2):
                        last = v.tensor_tensor(
                            out=fstat[:, 16 + par:17 + par],
                            in0=fstat[:, 16 + par:17 + par],
                            in1=fstat[:, 20 + par:21 + par], op=ALU.min)
                return last
            vmx = ops.sig("vector", s_V, v_mx2)
            ops.wait("scalar", s_V, vmx)

            def a_vex2(a, nk=nk):
                last = None
                for par in range(2):
                    main = psD if par == 0 else psH
                    diag = psA if par == 0 else psB
                    ab = attnb[:, par * 640:par * 640 + 640]
                    if nk > 1:
                        a.activation(out=ab[:, 0:(nk - 1) * 128],
                                     in_=main[:, 0:(nk - 1) * 128], func=AF.Exp,
                                     bias=fstat[:, 16 + par:17 + par])
                    last = a.activation(out=ab[:, (nk - 1) * 128:nk * 128],
                                        in_=diag[:, 0:128], func=AF.Exp,
                                        bias=fstat[:, 16 + par:17 + par])
                return last
            vex = ops.sig("scalar", s_A, a_vex2)
            val[("sev", i, p)] = vex
            ops.wait("vector", s_A, vex)

            def v_sm2(v, nk=nk):
                v.drain()
                for par in range(2):
                    ab = attnb[:, par * 640:par * 640 + 640]
                    v.tensor_tensor(out=ab[:, (nk - 1) * 128:nk * 128],
                                    in0=ab[:, (nk - 1) * 128:nk * 128],
                                    in1=m_til[:], op=ALU.mult)
                    if nk == 5:
                        v.tensor_tensor(out=ab[:, 0:128], in0=ab[:, 0:128],
                                        in1=m_tiu[:], op=ALU.mult)
                v.drain()
                for par in range(2):
                    ab = attnb[:, par * 640:par * 640 + 640]
                    v.tensor_reduce(out=fstat[:, 18 + par:19 + par],
                                    in_=ab[:, 0:nk * 128],
                                    axis=mybir.AxisListType.X, op=ALU.add)
                v.drain()
                v.reciprocal(fstat[:, 18:20], fstat[:, 18:20])
                v.drain()
                last = None
                for par in range(2):
                    ab = attnb[:, par * 640:par * 640 + 640]
                    last = v.tensor_scalar(out=ab[:, 0:nk * 128],
                                           in0=ab[:, 0:nk * 128],
                                           scalar1=fstat[:, 18 + par:19 + par],
                                           scalar2=None, op0=ALU.mult)
                return last
            vsm = ops.sig("vector", s_V, v_sm2)
            ops.wait("tensor", s_V, vsm)
            prev_at = (val.get(("atev", i, p - 1), 0) if p >= 1
                       else val.get(("atev", i - 1, 3), 0))
            if p == 0:
                prev_at = max(prev_at, val[("qkT", i)])
            ops.wait("tensor", s_A, prev_at)

            def mm_at2(t, nk=nk):
                last = None
                for par in range(2):
                    pst = psG if par == 0 else psI
                    ab = attnb[:, par * 640:par * 640 + 640]
                    for jj in range(nk):
                        last = t.transpose(pst[:, jj * 128:jj * 128 + 128],
                                           ab[:, jj * 128:jj * 128 + 128],
                                           ident[:])
                return last
            vat = ops.sig("tensor", s_T, mm_at2)
            ops.wait("scalar", s_T, vat)

            def a_atev2(a, nk=nk):
                last = None
                for par in range(2):
                    pst = psG if par == 0 else psI
                    at_ = attnT[:, par * 640:par * 640 + 640]
                    last = a.activation(out=at_[:, 0:nk * 128],
                                        in_=pst[:, 0:nk * 128], func=AF.Copy)
                return last
            val[("atev", i, p)] = ops.sig("scalar", s_A, a_atev2)
            prev_oev = (val.get(("oev", i, p - 1), 0) if p >= 1
                        else val.get(("oev", i - 1, 3), 0))
            ops.wait("tensor", s_A, max(val[("atev", i, p)], prev_oev))

            def mm_av2(t, p=p, nk=nk, j0=j0):
                last = None
                for par in range(2):
                    h = 2 * p + par
                    at_ = attnT[:, par * 640:par * 640 + 640]
                    avb = psF if par == 0 else psC
                    for jj in range(nk):
                        sl = ((j0 + jj) % 5) * 512 + h * 64
                        if par == 0:
                            last = t.matmul(avb[0:64, 256:384],
                                            vring[:, sl:sl + 64],
                                            at_[:, jj * 128:jj * 128 + 128],
                                            start=(jj == 0), stop=(jj == nk - 1))
                        else:
                            last = t.matmul(avb[64:128, 384:512],
                                            vring[:, sl:sl + 64],
                                            at_[:, jj * 128:jj * 128 + 128],
                                            start=(jj == 0), stop=(jj == nk - 1),
                                            tile_position=(0, 64))
                return last
            vav = ops.sig("tensor", s_T, mm_av2)
            ops.wait("scalar", s_T, vav)
            if i % 4 == 0 and i > 0:
                ops.wait("scalar", s_T, val.get(("woMM", i // 4 - 1), 0))

            def a_oev2(a, i=i, p=p):
                last = None
                for par in range(2):
                    pslo = par * 64
                    osl = obuf[pslo:pslo + 64,
                               p * 512 + (i % 4) * 128:
                               p * 512 + (i % 4) * 128 + 128]
                    src = (psF[0:64, 256:384] if par == 0
                           else psC[64:128, 384:512])
                    last = a.activation(out=osl, in_=src, func=AF.Copy)
                return last
            val[("oev", i, p)] = ops.sig("scalar", s_A, a_oev2)
        if i % 4 == 3:
            blk = i // 4
            ops.wait("tensor", s_A, val[("oev", i, 3)])
            lastwo = 0
            for dc in range(NCH):
                bank = psD if dc % 2 == 0 else psH

                def mm_wo(t, dc=dc, bank=bank):
                    last = None
                    for pc in range(4):
                        last = t.matmul(
                            bank[:, 0:512],
                            wo_sb[:, pc * 1024 + dc * 128:pc * 1024 + dc * 128 + 128],
                            obuf[:, pc * 512:pc * 512 + 512],
                            start=(pc == 0), stop=(pc == 3))
                    return last
                if dc >= 2:
                    ops.wait("tensor", s_V, lastwo)
                vwo = ops.sig("tensor", s_T, mm_wo)
                val[("woMM", blk)] = vwo
                ops.wait("vector", s_T, vwo)

                def v_woev(v, dc=dc, bank=bank, blk=blk):
                    col = ncol(b, dc, blk * 512)
                    v.scalar_tensor_tensor(out=xn_sb[:, col:col + 511],
                                           in0=bank[:, 0:511], scalar=1.0,
                                           in1=xn_sb[:, col:col + 511],
                                           op0=ALU.mult, op1=ALU.add)
                    return v.tensor_copy(lcbuf[:, dc:dc + 1], bank[:, 511:512])
                lastwo = ops.sig("vector", s_V, v_woev)
            val[("woev", blk)] = lastwo
    ops.wait("vector", s_V, s_V.v)

    def v_lcf(v):
        last = None
        for dc in range(NCH):
            col = ncol(b, dc, T - 1)
            last = v.tensor_tensor(out=xn_sb[:, col:col + 1],
                                   in0=xn_sb[:, col:col + 1],
                                   in1=lcbuf[:, dc:dc + 1], op=ALU.add)
        return last
    ops.sig("vector", s_V, v_lcf)
    ops.barrier(f"ffnA_{l}_{b}")


def emit_ffn_phaseB(c, dram, l, b):
    ops, nc = c.ops, c.nc
    xn_sb = c.t["xn_sb"]
    hid, lnout, w1blk, w2blk, gelt = (c.t["hid"], c.t["lnout"], c.t["w1blk"],
                                      c.t["w2blk"], c.t["gelt"])
    lgb, b1p, b2p, lcbuf = c.t["lgb"], c.t["b1p"], c.t["b2p"], c.t["lcbuf"]
    ones_col, ones_row = c.t["ones_col"], c.t["ones_row"]
    stats = c.t["ln_stats"]
    psA, psB, psC, psD = c.t["psA"], c.t["psB"], c.t["psC"], c.t["psD"]
    psF, psH = c.t["psF"], c.t["psH"]
    s_V, s_A, s_T, s_G = (ops.sem("f_V"), ops.sem("f_A"), ops.sem("f_T"),
                          ops.sem("f_G"))
    s_w = [ops.sem("f_w0"), ops.sem("f_w1"), ops.sem("f_w2")]
    s_u = [ops.sem("f_u0"), ops.sem("f_u1")]
    val = {}
    w1g = {}
    w2slot = {0: 0, 1: 0}

    def hsl(dc, t0, n):
        o = ncol(b, dc, t0) - (1 if dc >= 4 else 0)
        return xn_sb[:, o:o + n]

    for tau in range(T // 512):
        t0 = tau * 512
        ops.wait("scalar", s_T, val.get(("w2last", tau - 1), 0))
        ops.wait("scalar", s_V, s_V.v)

        def a_sq(a, t0=t0):
            last = None
            for dc in range(NCH):
                last = a.activation(out=lnout[:, dc * 512:dc * 512 + 512],
                                    in_=hsl(dc, t0, 512), func=AF.Square)
            return last
        vsq = ops.sig("scalar", s_A, a_sq)
        ops.wait("tensor", s_A, vsq)
        ops.wait("tensor", s_V, s_V.v)

        def mm_s(t, t0=t0):
            last = None
            for dc in range(NCH):
                t.matmul(psA[0:1, 0:512], ones_col[:], hsl(dc, t0, 512),
                         start=(dc == 0), stop=(dc == NCH - 1))
                last = t.matmul(psB[0:1, 0:512], ones_col[:],
                                lnout[:, dc * 512:dc * 512 + 512],
                                start=(dc == 0), stop=(dc == NCH - 1))
            return last
        vms = ops.sig("tensor", s_T, mm_s)
        if tau >= 1:
            ops.wait("vector", s_T, vms)

            def v_lc(v, tau=tau):
                last = None
                for dc in range(NCH):
                    col = ncol(b, dc, tau * 512 - 1)
                    last = v.tensor_tensor(out=xn_sb[:, col:col + 1],
                                           in0=xn_sb[:, col:col + 1],
                                           in1=lcbuf[:, dc:dc + 1], op=ALU.add)
                return last
            ops.sig("vector", s_V, v_lc)
        ops.wait("vector", s_T, vms)

        def v_st(v):
            v.drain()
            v.tensor_scalar(out=stats[0:1, 0:512], in0=psA[0:1, 0:512],
                            scalar1=-1.0 / D, scalar2=None, op0=ALU.mult)
            v.drain()
            v.tensor_tensor(out=stats[0:1, 512:1024], in0=stats[0:1, 0:512],
                            in1=stats[0:1, 0:512], op=ALU.mult)
            v.drain()
            v.scalar_tensor_tensor(out=stats[0:1, 512:1024], in0=psB[0:1, 0:512],
                                   scalar=1.0 / D, in1=stats[0:1, 512:1024],
                                   op0=ALU.mult, op1=ALU.subtract)
            v.drain()
            v.tensor_scalar(out=stats[0:1, 512:1024], in0=stats[0:1, 512:1024],
                            scalar1=float(LN_EPS), scalar2=None, op0=ALU.add)
            v.drain()
            with nc.allow_low_precision(reason="bf16 LN stats"):
                return v.reciprocal(stats[0:1, 512:1024], stats[0:1, 512:1024])
        vst = ops.sig("vector", s_V, v_st)
        ops.wait("scalar", s_V, vst)
        def a_bsqrt(a):
            with nc.allow_low_precision(reason="bf16 LN stats"):
                return a.sqrt(stats[0:1, 512:1024], stats[0:1, 512:1024])
        vsr = ops.sig("scalar", s_A, a_bsqrt)
        ops.wait("tensor", s_A, vsr)

        def mm_bc(t):
            t.matmul(psC[:, 0:512], ones_row[:], stats[0:1, 0:512],
                     start=True, stop=True)
            return t.matmul(psF[:, 0:512], ones_row[:], stats[0:1, 512:1024],
                            start=True, stop=True)
        vbc = ops.sig("tensor", s_T, mm_bc)
        ops.wait("vector", s_T, vbc)

        def v_ln(v, t0=t0):
            last = None
            for dc in range(NCH):
                sl = lnout[:, dc * 512:dc * 512 + 512]
                v.drain()
                v.tensor_tensor(out=sl, in0=hsl(dc, t0, 512), in1=psC[:, 0:512],
                                op=ALU.add)
                v.drain()
                v.tensor_tensor(out=sl, in0=sl, in1=psF[:, 0:512], op=ALU.mult)
                v.drain()
                last = v.tensor_scalar(out=sl, in0=sl, scalar1=lgb[:, dc:dc + 1],
                                       scalar2=lgb[:, NCH + dc:NCH + dc + 1],
                                       op0=ALU.mult, op1=ALU.add)
            return last
        val[("ln", tau)] = ops.sig("vector", s_V, v_ln)

        PASSES = ((0, 8), (8, 15), (15, 22))
        for half, (hlo, hhi) in enumerate(PASSES):
            npc = hhi - hlo
            for k in range(npc):
                hc = hlo + k
                u = 2 * hc
                par1 = hc % 2
                bank_a = psA if par1 == 0 else psC
                bank_g = psB if par1 == 0 else psD
                gsl = gelt[:, par1 * 512:par1 * 512 + 512]
                for part in range(2):
                    slot = (u + part) % 3
                    sw = s_w[slot]
                    gu = tau * 44 + u + part
                    ops.wait("gpsimd", s_T, w1g.get(gu - 3, 0))
                    ops.sig("gpsimd", sw,
                            lambda g, l=l, u2=u + part, slot=slot: g.dma_start(
                                out=w1blk[:, slot * 1024:(slot + 1) * 1024],
                                in_=dram["s_w1"][l, u2]), n=16)
                    ops.wait("tensor", sw, sw.v)
                    ops.wait("tensor", s_V, val[("ln", tau)])
                    bank = bank_a if part == 0 else bank_g
                    if hc >= 2:
                        ops.wait("tensor", s_V, val.get(("gg", tau, hc - 2), 0))
                        ops.wait("tensor", s_A, val.get(("gl", tau, hc - 2), 0))
                    if k <= 1 and par1 == 1:
                        ops.wait("tensor", s_V,
                                 val.get(("w2ev", tau, half - 1, 1),
                                         val.get(("w2ev", tau - 1, 2, 1), 0)))

                    def mm_w1(t, slot=slot, bank=bank):
                        last = None
                        d3 = w1blk[:, slot * 1024:(slot + 1) * 1024].rearrange(
                            "p (n c) -> p n c", c=128)
                        for dc in range(NCH):
                            last = t.matmul(bank[:, 0:512], d3[:, dc],
                                            lnout[:, dc * 512:dc * 512 + 512],
                                            start=(dc == 0), stop=(dc == NCH - 1))
                        return last
                    val[("w1mm", tau, u + part)] = ops.sig("tensor", s_T, mm_w1)
                    w1g[gu] = val[("w1mm", tau, u + part)]
                ops.wait("scalar", s_T, val[("w1mm", tau, u + 1)])
                val[("gl", tau, hc)] = ops.sig(
                    "scalar", s_A,
                    lambda a, hc=hc, bank_g=bank_g, gsl=gsl: a.activation(
                        out=gsl, in_=bank_g[:, 0:512], func=AF.Gelu,
                        bias=b1p[:, 24 + hc:25 + hc]))
                ops.wait("vector", s_A, val[("gl", tau, hc)])
                val[("gg", tau, hc)] = ops.sig(
                    "vector", s_V,
                    lambda v, k=k, hc=hc, bank_a=bank_a, gsl=gsl:
                    v.scalar_tensor_tensor(
                        out=hid[:, k * 512:(k + 1) * 512], in0=bank_a[:, 0:512],
                        scalar=b1p[:, hc:hc + 1], in1=gsl,
                        op0=ALU.add, op1=ALU.mult))
            lastodd = max(h2 for h2 in range(hlo, hhi) if h2 % 2 == 1)
            for dgrp in range(2):
                for k in range(npc):
                    hc = hlo + k
                    rsz = min(128, 2730 - hc * 128)
                    slot = k % 2
                    su = s_u[slot]
                    if w2slot[slot]:
                        ops.wait("gpsimd", s_T, w2slot[slot])
                    if rsz < 128:
                        ops.sig("gpsimd", s_G,
                                lambda g, slot=slot: g.memset(
                                    w2blk[:, slot * 512:slot * 512 + 512], 0.0))
                        ops.wait("gpsimd", s_G, s_G.v)

                    def g_w2(g, hc=hc, dgrp=dgrp, slot=slot, rsz=rsz, su=su):
                        src = dram["w2"][l, hc * 128:hc * 128 + rsz,
                                         dgrp * 512:(dgrp + 1) * 512]
                        dst = w2blk[0:rsz, slot * 512:slot * 512 + 512]
                        g.dma_start(out=dst, in_=src).then_inc(su.h, 16)
                    ops.on("gpsimd", g_w2)
                    su.v += 16
                    ops.wait("tensor", su, su.v)
                    ops.wait("tensor", s_V, val[("gg", tau, hc)])
                    if dgrp == 0 and k == 0:
                        ops.wait("tensor", s_V, val[("gg", tau, lastodd)])
                        ops.wait("tensor", s_A, val[("gl", tau, lastodd)])
                    if k == 0:
                        ops.wait("tensor", s_V,
                                 val.get(("w2ev", tau, half, dgrp - 1),
                                         val.get(("w2ev", tau, half - 1, 1), 0)))

                    def mm_w2(t, k=k, npc=npc, slot=slot):
                        last = None
                        d3 = w2blk[:, slot * 512:slot * 512 + 512].rearrange(
                            "p (n c) -> p n c", c=128)
                        banks = (psC, psD, psF, psH)
                        for j, bank in enumerate(banks):
                            last = t.matmul(bank[:, 0:512], d3[:, j],
                                            hid[:, k * 512:(k + 1) * 512],
                                            start=(k == 0), stop=(k == npc - 1))
                        return last
                    val[("w2mm", tau, half, dgrp, k)] = ops.sig("tensor", s_T, mm_w2)
                    w2slot[slot] = val[("w2mm", tau, half, dgrp, k)]
                val[("w2last", tau)] = val[("w2mm", tau, half, dgrp, npc - 1)]
                ops.wait("vector", s_T, val[("w2mm", tau, half, dgrp, npc - 1)])

                def v_w2ev(v, tau=tau, half=half, dgrp=dgrp):
                    banks = (psC, psD, psF, psH)
                    last = None
                    for j, bank in enumerate(banks):
                        dc = dgrp * 4 + j
                        col = ncol(b, dc, tau * 512)
                        if half == 0:
                            v.scalar_tensor_tensor(out=xn_sb[:, col:col + 511],
                                                   in0=bank[:, 0:511],
                                                   scalar=b2p[:, dc:dc + 1],
                                                   in1=xn_sb[:, col:col + 511],
                                                   op0=ALU.add, op1=ALU.add)
                            last = v.tensor_scalar(out=lcbuf[:, dc:dc + 1],
                                                   in0=bank[:, 511:512],
                                                   scalar1=b2p[:, dc:dc + 1],
                                                   scalar2=None, op0=ALU.add)
                        else:
                            v.scalar_tensor_tensor(out=xn_sb[:, col:col + 511],
                                                   in0=bank[:, 0:511], scalar=1.0,
                                                   in1=xn_sb[:, col:col + 511],
                                                   op0=ALU.mult, op1=ALU.add)
                            last = v.scalar_tensor_tensor(
                                out=lcbuf[:, dc:dc + 1], in0=bank[:, 511:512],
                                scalar=1.0, in1=lcbuf[:, dc:dc + 1],
                                op0=ALU.mult, op1=ALU.add)
                    return last
                val[("w2ev", tau, half, dgrp)] = ops.sig("vector", s_V, v_w2ev)
    ops.wait("vector", s_V, s_V.v)

    def v_lcf(v):
        last = None
        for dc in range(NCH):
            col = ncol(b, dc, T - 1)
            last = v.tensor_tensor(out=xn_sb[:, col:col + 1],
                                   in0=xn_sb[:, col:col + 1],
                                   in1=lcbuf[:, dc:dc + 1], op=ALU.add)
        return last
    ops.sig("vector", s_V, v_lcf)
    ops.barrier(f"ffnB_{l}_{b}")


def emit_l1(c, dram):
    nc, ops = c.nc, c.ops
    x_sb, xn_sb = c.t["x_sb"], c.t["xn_sb"]
    ident, identf = c.t["ident"], c.t["identf"]
    m_tiu, m_upp = c.t["m_tiu"], c.t["m_upp"]
    psA, psB, psC, psD = c.t["psA"], c.t["psB"], c.t["psC"], c.t["psD"]
    psF, psG, psH, psI = c.t["psF"], c.t["psG"], c.t["psH"], c.t["psI"]
    s_T, s_A, s_V, s_G = (ops.sem("l1_T"), ops.sem("l1_A"), ops.sem("l1_V"),
                          ops.sem("l1_G"))

    base = c.addr_of("WT0")
    end = c.addr_of("S0") + 2048
    cur = [base]

    def arena(name, shape, dtype):
        import concourse.mybir as mb
        per = 1
        for d in shape[1:]:
            per *= d
        per *= mb.dt.size(dtype)
        off = (cur[0] + 31) // 32 * 32
        assert off + per <= end, f"arena2 overflow {name}: {off+per-base} > {end-base}"
        cur[0] = off + per
        h = nc.alloc_sbuf_tensor_at(name, shape, dtype, offset=off)
        c.t[name] = h
        return h

    wq_g = arena("wq_g", [128, 8 * 512], BF16)
    wk_g = arena("wk_g", [128, 8 * 512], BF16)
    wv_g = arena("wv_g", [128, 8 * 512], BF16)
    wo_g = arena("wo_g", [128, 4 * 1024], BF16)
    w8T = arena("w8T", [128, 8 * 1024], F8)
    bias8 = arena("bias8", [128, 8], F32)
    stSK = arena("stSK", [64, 8 * 64], BF16)
    stCQ = arena("stCQ", [64, 8 * 64], BF16)
    stG = arena("stG", [64, 8 * 64], BF16)

    krb = c.addr_of("krT")
    lnx = c.addr_of("ln_xsq")

    def at(name, shape, dtype, addr):
        h = nc.alloc_sbuf_tensor_at(name, shape, dtype, offset=addr)
        c.t[name] = h
        return h
    qg = at("qg", [128, 512], BF16, krb + 0)
    kg = at("kg", [128, 512], BF16, krb + 1024)
    vg = at("vg", [128, 512], BF16, krb + 2048)
    qTs = at("qTs", [64, 1024], BF16, krb + 3072)
    kTs = at("kTs", [64, 1024], BF16, krb + 5120)
    oTs = at("oTs", [128, 512], BF16, krb + 7168)
    Asb = at("Asb", [128, 256], BF16, lnx + 0)
    Atsb = at("Atsb", [128, 256], BF16, lnx + 512)
    Btsb = at("Btsb", [128, 256], BF16, lnx + 1024)
    Awmsb = at("Awmsb", [128, 256], BF16, lnx + 1536)
    ZTsb = at("ZTsb", [128, 256], BF16, lnx + 2048)
    Wmsb = at("Wmsb", [128, 128], BF16, lnx + 2560)
    Hmsb = at("Hmsb", [64, 256], BF16, lnx + 2816)
    D0sb = at("D0sb", [64, 128], BF16, lnx + 3328)
    KKsb = at("KKsb", [64, 128], BF16, lnx + 3584)
    osb = at("osb", [128, 128], BF16, lnx + 3840)
    xn8 = at("xn8", [128, 1024], F8, lnx + 4096)
    Esb = at("Esb", [128, 256], BF16, lnx + 5120)
    wstg = c.t["xstg"]

    s_lw = [ops.sem("l1_lw0"), ops.sem("l1_lw1")]

    def queue_group_weights(grp):
        sw = s_lw[grp]

        def g_w(g, grp=grp, sw=sw):
            for wbuf, nm in ((wq_g, "Wq1"), (wk_g, "Wk1"), (wv_g, "Wv1")):
                src = dram[nm][:, grp * 512:(grp + 1) * 512].rearrange(
                    "(n p) j -> p n j", p=128)
                g.dma_start(out=wbuf[:].rearrange("p (n j) -> p n j", j=512),
                            in_=src).then_inc(sw.h, 16)
            src = dram["Wo1"][grp * 512:(grp + 1) * 512, :].rearrange(
                "(n p) d -> p n d", p=128)
            g.dma_start(out=wo_g[:].rearrange("p (n d) -> p n d", d=1024),
                        in_=src).then_inc(sw.h, 16)
        sw.v += 64
        ops.on("gpsimd", g_w)
    queue_group_weights(0)

    ops.barrier("l1_start")

    s_ws = [ops.sem("l1_ws0"), ops.sem("l1_ws1")]
    s_bd = ops.sem("l1_bd")
    ops.sig("sync", s_bd,
            lambda e: e.dma_start(out=wstg[0:8, 0:128],
                                  in_=dram["bf1"].rearrange("(a b) -> a b", b=128)),
            n=16)
    ops.wait("tensor", s_bd, s_bd.v)
    vb = ops.sig("tensor", s_T,
                 lambda t: t.transpose(psD[:, 0:8], wstg[0:8, 0:128],
                                       identf[0:8, 0:8]))
    ops.wait("vector", s_T, vb)
    vbe = ops.sig("vector", s_V, lambda v: v.tensor_copy(bias8[:], psD[:, 0:8]))
    ops.wait("sync", s_V, vbe)
    w8tr = {}
    w8cast = {}
    for jb in range(8):
        slot = jb % 2
        if jb >= 2:
            ops.wait("sync", s_T, w8tr[jb - 2])
        ops.sig("sync", s_ws[slot],
                lambda e, jb=jb, slot=slot: e.dma_start(
                    out=wstg[:, slot * 1024:(slot + 1) * 1024],
                    in_=dram["Wf1"][jb * 128:(jb + 1) * 128, :]), n=16)
        ops.wait("tensor", s_ws[slot], s_ws[slot].v)
        if jb >= 1:
            ops.wait("tensor", s_V, w8cast[jb - 1])

        def mm_w8(t, slot=slot):
            last = None
            for ci in range(8):
                bank = psA if ci < 4 else psB
                last = t.transpose(bank[:, (ci % 4) * 128:(ci % 4) * 128 + 128],
                                   wstg[:, slot * 1024 + ci * 128:
                                        slot * 1024 + ci * 128 + 128], identf[:])
            return last
        w8tr[jb] = ops.sig("tensor", s_T, mm_w8)
        ops.wait("vector", s_T, w8tr[jb])

        def v_w8(v, jb=jb):
            last = None
            for ci in range(8):
                bank = psA if ci < 4 else psB
                last = v.tensor_scalar(
                    out=w8T[:, ci * 1024 + jb * 128:ci * 1024 + jb * 128 + 128],
                    in0=bank[:, (ci % 4) * 128:(ci % 4) * 128 + 128],
                    scalar1=448.0, scalar2=-448.0, op0=ALU.min, op1=ALU.max)
            return last
        w8cast[jb] = ops.sig("vector", s_V, v_w8)
    ops.wait("tensor", s_V, w8cast[7])
    ops.barrier("l1_wb")

    for grp in range(min(2, L1_NGRP) if L1_LEVEL >= 1 else 0):
        if grp == 1:
            ops.barrier("l1_g1")
            queue_group_weights(1)
        ops.wait("tensor", s_lw[grp], s_lw[grp].v)
        ops.wait("scalar", s_lw[grp], s_lw[grp].v)
        for b in range(min(NB, L1_NB)):
            zv = s_G.v
            for st_ in (stSK, stCQ, stG):
                ops.sig("gpsimd", s_G,
                        lambda g, st_=st_: g.memset(st_[:], 0.0))
            ops.wait("tensor", s_G, s_G.v)
            ops.wait("vector", s_G, s_G.v)
            pv = {}
            for cc in range(min(CHK, L1_NCC)):
                emit_l1_chunk(c, grp, b, cc, pv)
            ops.barrier(f"l1_{grp}_{b}")

    f8p = {}
    for b in range(NB if L1_LEVEL >= 2 else 0):
        for cc in range(CHK):
            idx = b * CHK + cc
            ops.wait("vector", s_T, f8p.get(("mm", idx - 1), 0))

            def v_c8(v, b=b, cc=cc):
                xin3 = xn_sb[:].rearrange("p (n t) -> p n t", t=TP)[
                    :, b * NCH:(b + 1) * NCH,
                    PAD + cc * 128:PAD + cc * 128 + 128]
                return v.tensor_scalar(
                    out=xn8[:].rearrange("p (n t) -> p n t", t=128),
                    in0=xin3, scalar1=448.0, scalar2=-448.0,
                    op0=ALU.min, op1=ALU.max)
            f8p[("c8", idx)] = ops.sig("vector", s_V, v_c8)
            ops.wait("tensor", s_V, f8p[("c8", idx)])
            if idx >= 1:
                ops.wait("tensor", s_V, f8p[("ev", idx - 1)])

            def mm_f8(t):
                last = None
                for jb in range(8):
                    bank, col = (psD, jb * 128) if jb < 4 else (psH, (jb - 4) * 128)
                    for ci in range(8):
                        last = t.matmul(
                            bank[:, col:col + 128],
                            w8T[:, ci * 1024 + jb * 128:ci * 1024 + jb * 128 + 128],
                            xn8[:, ci * 128:(ci + 1) * 128],
                            start=(ci == 0), stop=(ci == 7))
                return last
            f8p[("mm", idx)] = ops.sig("tensor", s_T, mm_f8)
            ops.wait("vector", s_T, f8p[("mm", idx)])

            def v_f8ev(v, b=b, cc=cc):
                last = None
                for jb in range(8):
                    bank, col = (psD, jb * 128) if jb < 4 else (psH, (jb - 4) * 128)
                    xs = x_sb[:, xcol(b, jb, cc * 128):xcol(b, jb, cc * 128) + 128]
                    last = v.scalar_tensor_tensor(
                        out=xs, in0=bank[:, col:col + 128],
                        scalar=bias8[:, jb:jb + 1], in1=xs,
                        op0=ALU.add, op1=ALU.add)
                return last
            f8p[("ev", idx)] = ops.sig("vector", s_V, v_f8ev)
    ops.barrier("l1_done")


def emit_l1_chunk(c, grp, b, cc, pv):
    nc, ops = c.nc, c.ops
    xn_sb, x_sb = c.t["xn_sb"], c.t["x_sb"]
    ident = c.t["ident"]
    m_tiu, m_upp = c.t["m_tiu"], c.t["m_upp"]
    psA, psB, psC, psD = c.t["psA"], c.t["psB"], c.t["psC"], c.t["psD"]
    psF, psG, psH, psI = c.t["psF"], c.t["psG"], c.t["psH"], c.t["psI"]
    wq_g, wk_g, wv_g, wo_g = c.t["wq_g"], c.t["wk_g"], c.t["wv_g"], c.t["wo_g"]
    qg, kg, vg = c.t["qg"], c.t["kg"], c.t["vg"]
    qTs, kTs, oTs = c.t["qTs"], c.t["kTs"], c.t["oTs"]
    Asb, Atsb, Btsb = c.t["Asb"], c.t["Atsb"], c.t["Btsb"]
    Awmsb, ZTsb, Wmsb = c.t["Awmsb"], c.t["ZTsb"], c.t["Wmsb"]
    Hmsb, D0sb, KKsb, osb = c.t["Hmsb"], c.t["D0sb"], c.t["KKsb"], c.t["osb"]
    Esb = c.t["Esb"]
    stSK, stCQ, stG = c.t["stSK"], c.t["stCQ"], c.t["stG"]
    s_T, s_A, s_V, s_G = (ops.sem("l1_T"), ops.sem("l1_A"), ops.sem("l1_V"),
                          ops.sem("l1_G"))

    ops.wait("tensor", s_V, max(pv.get(("ZTadd", 6), 0), pv.get(("Oadd", 6), 0),
                                pv.get(("ZTadd", 7), 0), pv.get(("Oadd", 7), 0),
                                pv.get(("SKup", 7), 0), pv.get(("CQup", 7), 0),
                                pv.get(("Gup", 7), 0)))
    ops.wait("tensor", s_A, max(pv.get(("Wmev", 6), 0), pv.get(("Wmev", 7), 0),
                                pv.get(("KKev", 6), 0), pv.get(("KKev", 7), 0)))
    ops.wait("tensor", s_V, pv.get(("Woev", 0), 0))

    def mm_proj(t):
        last = None
        for wbuf, bank in ((wq_g, psA), (wk_g, psB), (wv_g, psC)):
            for fc in range(NCH):
                last = t.matmul(
                    bank[:, 0:512],
                    xn_sb[:, ncol(b, fc, cc * 128):ncol(b, fc, cc * 128) + 128],
                    wbuf[:, fc * 512:(fc + 1) * 512],
                    start=(fc == 0), stop=(fc == NCH - 1))
        return last
    vproj = ops.sig("tensor", s_T, mm_proj)
    ops.wait("scalar", s_T, vproj)
    ops.wait("scalar", s_T, max(pv.get(("QtV", 7), 0), pv.get(("KtWm", 7), 0),
                                pv.get(("ZV", 7), 0), pv.get(("Wm", 7), 0)))

    def a_pev(a):
        a.activation(out=qg[:], in_=psA[:, 0:512], func=AF.Copy)
        a.activation(out=kg[:], in_=psB[:, 0:512], func=AF.Copy)
        return a.activation(out=vg[:], in_=psC[:, 0:512], func=AF.Copy)
    pev = ops.sig("scalar", s_A, a_pev)

    ops.wait("tensor", s_A, pev)
    ops.wait("tensor", s_A, max(pv.get(("Atev", 6), 0), pv.get(("Atev", 7), 0),
                                pv.get(("oTev", 2), 0), pv.get(("oTev", 3), 0)))

    def mm_qkT(t):
        last = None
        for hl in range(8):
            t.transpose(psG[0:64, hl * 128:hl * 128 + 128],
                        qg[:, hl * 64:hl * 64 + 64], ident[:])
            last = t.transpose(psI[0:64, hl * 128:hl * 128 + 128],
                               kg[:, hl * 64:hl * 64 + 64], ident[:])
        return last
    vqkT = ops.sig("tensor", s_T, mm_qkT)
    ops.wait("scalar", s_T, vqkT)
    ops.wait("scalar", s_T, max(pv.get(("QD0", 7), 0), pv.get(("E", 7), 0),
                                pv.get(("A", 7), 0), pv.get(("Hm", 7), 0)))

    def a_qkTev(a):
        last = None
        for hl in range(8):
            a.activation(out=qTs[0:64, hl * 128:hl * 128 + 128],
                         in_=psG[0:64, hl * 128:hl * 128 + 128], func=AF.Copy)
            last = a.activation(out=kTs[0:64, hl * 128:hl * 128 + 128],
                                in_=psI[0:64, hl * 128:hl * 128 + 128],
                                func=AF.Copy)
        return last
    qkTev = ops.sig("scalar", s_A, a_qkTev)

    nv = {}
    for hl in range(8):
        m = hl % 2
        cp = hl // 2
        D_ = psD if m == 0 else psH
        P_ = psA if m == 0 else psB
        pT = psG if m == 0 else psI
        qT_ = qTs[0:64, hl * 128:hl * 128 + 128]
        kT_ = kTs[0:64, hl * 128:hl * 128 + 128]
        qg_ = qg[:, hl * 64:hl * 64 + 64]
        kg_ = kg[:, hl * 64:hl * 64 + 64]
        vg_ = vg[:, hl * 64:hl * 64 + 64]
        sk_ = stSK[0:64, hl * 64:hl * 64 + 64]
        cq_ = stCQ[0:64, hl * 64:hl * 64 + 64]
        g_ = stG[0:64, hl * 64:hl * 64 + 64]
        mA = slice(m * 128, m * 128 + 128)
        m64 = slice(m * 64, m * 64 + 64)
        pC0 = m * 192

        def g2(key, d=2):
            return nv.get((key, hl - d), pv.get((key, 8 + hl - d), 0))

        ops.wait("tensor", s_A, max(qkTev, g2("Aev")))
        nv[("A", hl)] = ops.sig("tensor", s_T, lambda t, qT_=qT_, kT_=kT_, D_=D_:
                                t.matmul(D_[:, 0:128], qT_, kT_,
                                         start=True, stop=True))
        ops.wait("tensor", s_V, pv.get(("Gup", hl), 0))
        ops.wait("tensor", s_A, g2("Hmev"))
        nv[("Hm", hl)] = ops.sig(
            "tensor", s_T,
            lambda t, sk_=sk_, qT_=qT_, m=m: t.matmul(
                psF[0:64, m * 256:m * 256 + 128], sk_, qT_, start=True, stop=True))
        ops.wait("scalar", s_T, nv[("Hm", hl)])
        ops.wait("scalar", s_T, max(g2("At"), 0))
        ops.wait("scalar", s_V, g2("Awm"))

        def a_ahm(a, D_=D_, m=m, mA=mA):
            a.activation(out=Asb[:, mA], in_=D_[:, 0:128], func=AF.Copy)
            return a.activation(out=Hmsb[0:64, mA], in_=psF[0:64, m * 256:m * 256 + 128],
                                func=AF.Copy)
        nv[("Aev", hl)] = nv[("Hmev", hl)] = ops.sig("scalar", s_A, a_ahm)
        ops.wait("tensor", s_A, nv[("Hmev", hl)])
        ops.wait("tensor", s_A, g2("Eev"))
        nv[("E", hl)] = ops.sig(
            "tensor", s_T,
            lambda t, qT_=qT_, m=m, mA=mA, D_=D_: t.matmul(
                D_[:, 128:256], qT_, Hmsb[0:64, mA], start=True, stop=True))
        ops.wait("scalar", s_T, nv[("E", hl)])
        ops.wait("scalar", s_V, g2("ZTadd"))
        nv[("Eev", hl)] = ops.sig(
            "scalar", s_A,
            lambda a, mA=mA, D_=D_: a.activation(out=Esb[:, mA],
                                                 in_=D_[:, 128:256], func=AF.Copy))
        ops.wait("tensor", s_A, max(nv[("Aev", hl)], g2("Atev")))
        nv[("At", hl)] = ops.sig(
            "tensor", s_T,
            lambda t, mA=mA, pT=pT: t.transpose(pT[:, 0:128], Asb[:, mA], ident[:]))
        ops.wait("scalar", s_T, nv[("At", hl)])
        ops.wait("scalar", s_V, g2("Btm"))
        ops.wait("scalar", s_T, g2("FT"))
        nv[("Atev", hl)] = ops.sig(
            "scalar", s_A,
            lambda a, mA=mA, pT=pT: a.activation(out=Atsb[:, mA], in_=pT[:, 0:128],
                                                 func=AF.Copy))
        ops.wait("vector", s_A, nv[("Atev", hl)])
        ops.wait("vector", s_T, max(g2("FT"), g2("Wm")))

        def v_bt(v, mA=mA):
            v.tensor_tensor(out=Btsb[:, mA], in0=Atsb[:, mA], in1=m_tiu[:],
                            op=ALU.mult)
            return v.tensor_tensor(out=Awmsb[:, mA], in0=Asb[:, mA], in1=m_upp[:],
                                   op=ALU.mult)
        nv[("Btm", hl)] = nv[("Awm", hl)] = ops.sig("vector", s_V, v_bt)
        ops.wait("tensor", s_V, max(nv[("Btm", hl)], g2("ZTadd")))
        nv[("FT", hl)] = ops.sig(
            "tensor", s_T,
            lambda t, mA=mA, P_=P_: t.matmul(P_[:, 0:128], Btsb[:, mA],
                                             Atsb[:, mA], start=True, stop=True))
        ops.wait("tensor", s_A, g2("Wmev"))
        nv[("Wm", hl)] = ops.sig(
            "tensor", s_T,
            lambda t, mA=mA, P_=P_, vg_=vg_: t.matmul(
                P_[:, 192:256], Awmsb[:, mA], vg_, start=True, stop=True))
        ops.wait("scalar", s_T, nv[("Wm", hl)])
        ops.wait("scalar", s_T, g2("KtWm"))
        nv[("Wmev", hl)] = ops.sig(
            "scalar", s_A,
            lambda a, m64=m64, P_=P_: a.activation(out=Wmsb[:, m64],
                                                   in_=P_[:, 192:256], func=AF.Copy))
        ops.wait("tensor", s_A, max(g2("KKev"), 0))
        ops.wait("tensor", s_V, max(g2("SKup"), g2("CQup")))
        nv[("KK", hl)] = ops.sig(
            "tensor", s_T,
            lambda t, kg_=kg_, pC0=pC0: t.matmul(psC[0:64, pC0:pC0 + 64], kg_, kg_,
                                                 start=True, stop=True))
        nv[("QtV", hl)] = ops.sig(
            "tensor", s_T,
            lambda t, qg_=qg_, vg_=vg_, pC0=pC0: t.matmul(
                psC[0:64, pC0 + 64:pC0 + 128], qg_, vg_, start=True, stop=True))
        ops.wait("scalar", s_T, nv[("KK", hl)])
        ops.wait("scalar", s_T, g2("KKC"))
        nv[("KKev", hl)] = ops.sig(
            "scalar", s_A,
            lambda a, m64=m64, pC0=pC0: a.activation(out=KKsb[0:64, m64],
                                                     in_=psC[0:64, pC0:pC0 + 64],
                                                     func=AF.Copy))
        ops.wait("tensor", s_V, max(g2("D0ev"), pv.get(("Gup", hl), 0)))
        nv[("SC", hl)] = ops.sig(
            "tensor", s_T,
            lambda t, sk_=sk_, cq_=cq_, m=m: t.matmul(
                psF[0:64, m * 256 + 128:m * 256 + 192], sk_, cq_,
                start=True, stop=True))
        ops.wait("vector", s_T, nv[("SC", hl)])
        ops.wait("vector", s_T, g2("QD0"))
        nv[("D0ev", hl)] = ops.sig(
            "vector", s_V,
            lambda v, m=m, m64=m64, g_=g_: v.tensor_tensor(
                out=D0sb[0:64, m64], in0=psF[0:64, m * 256 + 128:m * 256 + 192],
                in1=g_, op=ALU.subtract))
        ops.wait("tensor", s_V, nv[("D0ev", hl)])
        ops.wait("tensor", s_V, g2("Oadd"))
        nv[("QD0", hl)] = ops.sig(
            "tensor", s_T,
            lambda t, qT_=qT_, m64=m64, D_=D_: t.matmul(
                D_[:, 256:320], qT_, D0sb[0:64, m64], start=True, stop=True))
        ops.wait("vector", s_A, nv[("Eev", hl)])
        ops.wait("vector", s_T, max(nv[("FT", hl)], g2("ZV")))
        def v_zt(v, mA=mA, P_=P_):
            v.tensor_tensor(out=ZTsb[:, mA], in0=P_[:, 0:128], in1=Esb[:, mA],
                            op=ALU.add)
            v.drain()
            return v.tensor_tensor(out=ZTsb[:, mA], in0=ZTsb[:, mA],
                                   in1=m_tiu[:], op=ALU.mult)
        nv[("ZTadd", hl)] = nv[("ZTm", hl)] = ops.sig("vector", s_V, v_zt)
        ops.wait("tensor", s_V, nv[("ZTm", hl)])
        ops.wait("tensor", s_A, g2("ZVev"))
        nv[("ZV", hl)] = ops.sig(
            "tensor", s_T,
            lambda t, mA=mA, P_=P_, vg_=vg_: t.matmul(
                P_[:, 128:192], ZTsb[:, mA], vg_, start=True, stop=True))
        ops.wait("scalar", s_T, nv[("ZV", hl)])
        ops.wait("scalar", s_T, nv.get(("oT", cp - 1), pv.get(("oT", 3), 0)))
        nv[("ZVev", hl)] = ops.sig(
            "scalar", s_A,
            lambda a, m64=m64, P_=P_: a.activation(out=osb[:, m64],
                                                   in_=P_[:, 128:192], func=AF.Copy))
        ops.wait("vector", s_A, nv[("ZVev", hl)])
        ops.wait("vector", s_T, nv[("QD0", hl)])
        nv[("Oadd", hl)] = ops.sig(
            "vector", s_V,
            lambda v, m64=m64, D_=D_: v.tensor_tensor(
                out=osb[:, m64], in0=D_[:, 256:320], in1=osb[:, m64],
                op=ALU.add))
        ops.wait("tensor", s_A, nv[("Wmev", hl)])
        nv[("KtWm", hl)] = ops.sig(
            "tensor", s_T,
            lambda t, kg_=kg_, m64=m64, pC0=pC0: t.matmul(
                psC[0:64, pC0 + 128:pC0 + 192], kg_, Wmsb[:, m64],
                start=True, stop=True))
        ops.wait("tensor", s_A, nv[("KKev", hl)])
        nv[("KKC", hl)] = ops.sig(
            "tensor", s_T,
            lambda t, m64=m64, cq_=cq_, m=m: t.matmul(
                psF[0:64, m * 256 + 192:m * 256 + 256], KKsb[0:64, m64], cq_,
                start=True, stop=True))
        ops.wait("vector", s_T, max(nv[("KK", hl)], nv[("QtV", hl)],
                                    nv[("KtWm", hl)], nv[("KKC", hl)]))

        def v_upd(v, sk_=sk_, cq_=cq_, g_=g_, m=m, pC0=pC0):
            v.tensor_tensor(out=sk_, in0=sk_, in1=psC[0:64, pC0:pC0 + 64],
                            op=ALU.add)
            v.tensor_tensor(out=g_, in0=g_,
                            in1=psF[0:64, m * 256 + 192:m * 256 + 256], op=ALU.add)
            v.drain()
            v.tensor_tensor(out=g_, in0=g_, in1=psC[0:64, pC0 + 128:pC0 + 192],
                            op=ALU.add)
            return v.tensor_tensor(out=cq_, in0=cq_,
                                   in1=psC[0:64, pC0 + 64:pC0 + 128], op=ALU.add)
        last_upd = ops.sig("vector", s_V, v_upd)
        nv[("SKup", hl)] = nv[("CQup", hl)] = nv[("Gup", hl)] = last_upd

        if m == 1:
            ops.wait("tensor", s_V, max(nv[("Oadd", hl - 1)], nv[("Oadd", hl)]))
            ops.wait("tensor", s_A,
                     nv.get(("oTev", cp - 2), pv.get(("oTev", cp + 2), 0)))
            pTo = psG if cp % 2 == 0 else psI
            nv[("oT", cp)] = ops.sig(
                "tensor", s_T,
                lambda t, pTo=pTo: t.transpose(pTo[:, 128:256], osb[:], ident[:]))
            ops.wait("scalar", s_T, nv[("oT", cp)])
            ops.wait("scalar", s_T, pv.get(("Wo", 0), 0) if cp == 0 else 0)
            nv[("oTev", cp)] = ops.sig(
                "scalar", s_A,
                lambda a, cp=cp, pTo=pTo: a.activation(
                    out=oTs[:, cp * 128:cp * 128 + 128], in_=pTo[:, 128:256],
                    func=AF.Copy))

    ops.wait("tensor", s_A, nv[("oTev", 3)])
    ops.wait("tensor", s_V, max(nv[("Oadd", 6)], nv[("Oadd", 7)],
                                nv[("ZTadd", 6)], nv[("ZTadd", 7)]))
    ops.wait("tensor", s_V, pv.get(("Woev", 0), 0))

    def mm_wo(t):
        last = None
        for dch in range(8):
            bank, col = (psD, dch * 128) if dch < 4 else (psH, (dch - 4) * 128)
            for cp in range(4):
                last = t.matmul(
                    bank[:, col:col + 128],
                    wo_g[:, cp * 1024 + dch * 128:cp * 1024 + dch * 128 + 128],
                    oTs[:, cp * 128:cp * 128 + 128],
                    start=(cp == 0), stop=(cp == 3))
        return last
    nv[("Wo", 0)] = ops.sig("tensor", s_T, mm_wo)
    ops.wait("vector", s_T, nv[("Wo", 0)])

    def v_woev(v):
        last = None
        for dch in range(8):
            bank, col = (psD, dch * 128) if dch < 4 else (psH, (dch - 4) * 128)
            xs = x_sb[:, xcol(b, dch, cc * 128):xcol(b, dch, cc * 128) + 128]
            last = v.scalar_tensor_tensor(out=xs, in0=bank[:, col:col + 128],
                                          scalar=1.0, in1=xs,
                                          op0=ALU.mult, op1=ALU.add)
        return last
    nv[("Woev", 0)] = ops.sig("vector", s_V, v_woev)

    pv.clear()
    pv.update(nv)


def make_in_maps(inputs):
    t_cs, t_sn, t_scale = make_tables()
    in_maps = []
    for core in range(8):
        m = {
            "x_in": np.ascontiguousarray(inputs["L0_attn_x"][2 * core:2 * core + 2]),
            "W0": inputs["L0_attn_W"],
            "k_norm_w": inputs["L0_attn_k_norm_w"],
            "o_norm_w": inputs["L0_attn_o_norm_w"],
            "o_norm_b": inputs["L0_attn_o_norm_b"],
            "eta_raw": inputs["L0_attn_eta_raw"],
            "lam_logit": inputs["L0_attn_lam_logit"],
            "wqkv": inputs["L0_ffn_wqkv"],
            "q_scale": inputs["L0_ffn_q_scale"],
            "k_scale": inputs["L0_ffn_k_scale"],
            "wo": inputs["L0_ffn_wo"],
            "ln_g": inputs["L0_ffn_ln_g"],
            "ln_b": inputs["L0_ffn_ln_b"],
            "w1": inputs["L0_ffn_w1"],
            "b1": inputs["L0_ffn_b1"],
            "w2": inputs["L0_ffn_w2"],
            "b2": inputs["L0_ffn_b2"],
            "Wq1": inputs["L1_attn_W_q"],
            "Wk1": inputs["L1_attn_W_k"],
            "Wv1": inputs["L1_attn_W_v"],
            "Wo1": inputs["L1_attn_W_o"],
            "Wf1": inputs["L1_ffn_weight"],
            "bf1": inputs["L1_ffn_bias"],
            "t_cs": t_cs, "t_sn": t_sn, "t_scale": t_scale,
        }
        in_maps.append({k: np.ascontiguousarray(v) for k, v in m.items()})
    return in_maps


def kernel(**inputs):
    nc = build_program()
    in_maps = make_in_maps(inputs)
    res = run_bass_kernel_spmd(nc, in_maps, core_ids=list(range(8)))
    return np.concatenate([r["y"] for r in res.results], axis=0)


# version 08
# speedup vs baseline: 1.2773x; 1.0591x over previous
import math
import numpy as np
import concourse.bass as bass
import concourse.mybir as mybir
from concourse.bass_utils import run_bass_kernel_spmd

F32 = mybir.dt.float32
BF16 = mybir.dt.bfloat16
F8 = mybir.dt.float8e4
AF = mybir.ActivationFunctionType
ALU = mybir.AluOpType

NB = 2
T = 2048
D = 1024
NCH = 8
PAD = 8
TP = T + PAD
CHK = T // 128
H0 = 16
DH0 = 64
WIN = 512
HF = 8
DHF = 64
HID = 2730
HIDP = 2816
NHC = HIDP // 128
LN_EPS = 1e-5
RMS_EPS = 1.1920929e-07
ENGS = ("tensor", "vector", "scalar", "gpsimd", "sync")
L0A_LEVEL = 99
L0A_NSTEP = 99
L1_LEVEL = 99
L1_NGRP = 2
L1_NB = 2
L1_NCC = 99


def xcol(b, ch, t0=0):
    return (b * NCH + ch) * T + t0


def ncol(b, ch, t0=0):
    return (b * NCH + ch) * TP + PAD + t0


class Sem:
    def __init__(self, nc, name):
        self.h = nc.alloc_semaphore(name)
        self.v = 0
        self.name = name


class Ops:

    def __init__(self, nc):
        self.nc = nc
        self.q = {e: [] for e in ENGS}
        self.sems = {}

    def sem(self, name):
        if name not in self.sems:
            self.sems[name] = Sem(self.nc, name)
        return self.sems[name]

    def on(self, eng, fn):
        self.q[eng].append(fn)

    def wait(self, eng, sem, val):
        self.q[eng].append(lambda e: e.wait_ge(sem.h, val))

    def sig(self, eng, sem, fn, n=1):
        sem.v += n
        val = sem.v

        def run(e):
            ins = fn(e)
            assert ins is not None, "sig() closure must return the instruction"
            ins.then_inc(sem.h, n)
        self.q[eng].append(run)
        return val

    def flush(self, block):
        for eng in ENGS:
            if not self.q[eng]:
                continue
            ops = self.q[eng]
            self.q[eng] = []

            def section(e, ops=ops):
                for fn in ops:
                    fn(e)
            getattr(block, eng)(section)

    def barrier(self, tag):
        s = self.sem("bar_" + tag)
        engs = ("tensor", "vector", "scalar", "gpsimd")
        base = s.v
        for eng in engs:
            self.sig(eng, s, lambda e: e.drain())
        for eng in engs + ("sync",):
            self.wait(eng, s, base + len(engs))


class Ctx:
    def __init__(self, nc):
        self.nc = nc
        self.ops = Ops(nc)
        self.t = {}
        for nm in ("psA", "psB", "psC", "psD", "psF", "psH"):
            self.ps(nm, [128, 512], F32)
        self.ps("psG", [128, 1024], BF16)
        self.ps("psI", [128, 1024], BF16)

    def sb(self, name, shape, dtype):
        h = self.nc.alloc_sbuf_tensor(name, shape, dtype)
        self.t[name] = h
        return h

    def ps(self, name, shape, dtype=F32):
        h = self.nc.alloc_psum_tensor(name, shape, dtype)
        self.t[name] = h
        return h

    def addr_of(self, name):
        import concourse.mybir as mb
        for a in self.nc.m.functions[0].allocations:
            if isinstance(a, mb.MemoryLocationSet):
                for ml in a.memorylocations:
                    if ml.name == name or ml.name.startswith(name + "_"):
                        return ml.addr
        raise KeyError(name)

    def alias(self, name, shape, dtype, host, off_bytes=0):
        base = self.addr_of(self.t[host].name if host in self.t else host)
        h = self.nc.alloc_sbuf_tensor_at(name, shape, dtype,
                                         offset=base + off_bytes)
        self.t[name] = h
        return h


def emit_load(c, dram):
    nc, ops = c.nc, c.ops
    x_sb = c.sb("x_sb", [128, NB * NCH * T], BF16)
    xn_sb = c.sb("xn_sb", [128, NB * NCH * TP], BF16)
    s_in = ops.sem("s_in")

    ident = c.sb("ident", [128, 128], BF16)
    identf = c.sb("identf", [128, 128], F32)
    iota_pr = c.sb("iota_pr", [128, 128], F32)
    iota_p1 = c.sb("iota_p1", [128, 1], F32)
    iota_rev = c.sb("iota_rev", [128, 1], F32)
    m_low = c.sb("m_low", [128, 128], BF16)
    m_upp = c.sb("m_upp", [128, 128], BF16)
    m_til = c.sb("m_til", [128, 128], BF16)
    m_tiu = c.sb("m_tiu", [128, 128], BF16)
    ones_col = c.sb("ones_col", [128, 1], BF16)
    ones_row = c.sb("ones_row", [1, 128], BF16)
    ones_rowf = c.sb("ones_rowf", [1, 128], F32)

    def g_init(g):
        n = 0
        for tile, const in ((ones_col, 1.0), (ones_row, 1.0), (ones_rowf, 1.0)):
            g.memset(tile[:], const).then_inc(s_pre.h, 1)
            n += 1
        g.iota(iota_pr[:], pattern=[[-1, 128]], base=0, channel_multiplier=1,
               allow_small_or_imprecise_dtypes=True).then_inc(s_pre.h, 1)
        g.iota(iota_p1[:], pattern=[[0, 1]], base=1, channel_multiplier=1,
               allow_small_or_imprecise_dtypes=True).then_inc(s_pre.h, 1)
        g.iota(iota_rev[:], pattern=[[0, 1]], base=127, channel_multiplier=-1,
               allow_small_or_imprecise_dtypes=True).then_inc(s_pre.h, 1)
        g.memset(xn_sb[:], 0.0).then_inc(s_pre.h, 1)

    def v_masks(v):
        v.tensor_scalar(out=ident[:], in0=iota_pr[:], scalar1=0.0, scalar2=None,
                        op0=ALU.is_equal)
        v.tensor_scalar(out=identf[:], in0=iota_pr[:], scalar1=0.0, scalar2=None,
                        op0=ALU.is_equal)
        v.tensor_scalar(out=m_low[:], in0=iota_pr[:], scalar1=0.0, scalar2=None,
                        op0=ALU.is_gt)
        v.tensor_scalar(out=m_upp[:], in0=iota_pr[:], scalar1=0.0, scalar2=None,
                        op0=ALU.is_lt)
        v.tensor_scalar(out=m_til[:], in0=iota_pr[:], scalar1=0.0, scalar2=None,
                        op0=ALU.is_ge)
        return v.tensor_scalar(out=m_tiu[:], in0=iota_pr[:], scalar1=0.0,
                               scalar2=None, op0=ALU.is_le)

    s_pre = ops.sem("s_pre")
    s_pre.v += 7
    ops.on("gpsimd", g_init)
    ops.wait("vector", s_pre, s_pre.v)
    ops.sig("vector", s_pre, v_masks)
    for eng in ("tensor", "scalar", "sync", "gpsimd", "vector"):
        ops.wait(eng, s_pre, s_pre.v)

    xstg = c.sb("xstg", [128, 2 * 1024], F32)
    ps_t0, ps_t1 = c.t["psA"], c.t["psB"]
    s_tr = ops.sem("s_tr")
    s_tev = ops.sem("s_tev")
    s_ld = [ops.sem("s_ld0"), ops.sem("s_ld1")]
    x_in = dram["x_in"]
    items = [(b, tc) for b in range(NB) for tc in range(CHK)]
    for idx, (b, tc) in enumerate(items):
        buf = idx % 2
        if idx >= 2:
            ops.wait("sync", s_tr, idx - 1)
        ops.sig("sync", s_ld[buf],
                lambda e, b=b, tc=tc, buf=buf: e.dma_start(
                    out=xstg[:, buf * 1024:(buf + 1) * 1024],
                    in_=x_in[b, tc * 128:(tc + 1) * 128, :]), n=16)
        ops.wait("tensor", s_ld[buf], s_ld[buf].v)
        if idx >= 1:
            ops.wait("tensor", s_tev, idx - 1 + 1)

        def mm_tr(t, buf=buf):
            last = None
            for ch in range(NCH):
                ps = (ps_t0 if ch % 2 == 0 else ps_t1)
                last = t.transpose(ps[:, (ch // 2) * 128:(ch // 2) * 128 + 128],
                                   xstg[:, buf * 1024 + ch * 128:buf * 1024 + ch * 128 + 128],
                                   identf[:])
            return last
        ops.sig("tensor", s_tr, mm_tr)
        ops.wait("vector", s_tr, idx + 1)

        def v_ev(v, b=b, tc=tc):
            last = None
            for ch in range(NCH):
                ps = (ps_t0 if ch % 2 == 0 else ps_t1)
                last = v.tensor_copy(
                    x_sb[:, xcol(b, ch, tc * 128):xcol(b, ch, tc * 128) + 128],
                    ps[:, (ch // 2) * 128:(ch // 2) * 128 + 128])
            return last
        ops.sig("vector", s_tev, v_ev)
    for eng in ("tensor", "vector", "scalar", "gpsimd"):
        ops.wait(eng, s_tev, s_tev.v)


def emit_top_ln(c, tag):
    nc, ops = c.nc, c.ops
    x_sb, xn_sb = c.t["x_sb"], c.t["xn_sb"]
    ones_col, ones_rowf = c.t["ones_col"], c.t["ones_rowf"]
    ones_row = c.t["ones_row"]

    if "ln_xsq" not in c.t:
        c.sb("ln_xsq", [128, NCH * 512], BF16)
    xsq = c.t["ln_xsq"]
    if "ln_stats" not in c.t:
        c.sb("ln_stats", [1, 1024], BF16)
    stats = c.t["ln_stats"]
    ps_s, ps_q = c.t["psA"], c.t["psB"]
    ps_mu, ps_rs = c.t["psD"], c.t["psF"]

    s_sq = ops.sem(tag + "sq")
    s_mm = ops.sem(tag + "mm")
    s_st = ops.sem(tag + "st")
    s_rs = ops.sem(tag + "rs")
    s_ev = ops.sem(tag + "ev")

    TBL = min(512, T)
    items = [(b, tb) for b in range(NB) for tb in range(T // TBL)]
    for idx, (b, tb) in enumerate(items):
        if idx >= 1:
            ops.wait("scalar", s_mm, 2 * (idx - 1) + 1)
        xin3 = x_sb[:].rearrange("p (n t) -> p n t", t=T)[
            :, b * NCH:(b + 1) * NCH, tb * TBL:(tb + 1) * TBL]
        ops.sig("scalar", s_sq,
                lambda s, xin3=xin3: s.activation(
                    out=xsq[:].rearrange("p (n t) -> p n t", t=512)[:, :, 0:TBL],
                    in_=xin3, func=AF.Square))
        ops.wait("tensor", s_sq, idx + 1)
        if idx >= 1:
            ops.wait("tensor", s_st, idx)

        def mm_sums(t, b=b, tb=tb):
            for chx in range(NCH):
                xin = x_sb[:, xcol(b, chx, tb * TBL):xcol(b, chx, tb * TBL) + TBL]
                t.matmul(ps_s[0:1, 0:TBL], ones_col[:], xin,
                         start=(chx == 0), stop=(chx == NCH - 1))
                last = t.matmul(ps_q[0:1, 0:TBL], ones_col[:],
                                xsq[:, chx * 512:chx * 512 + TBL],
                                start=(chx == 0), stop=(chx == NCH - 1))
            return last
        ops.sig("tensor", s_mm, mm_sums)
        ops.wait("vector", s_mm, 2 * idx + 1)

        def v_stats(v):
            v.tensor_scalar(out=stats[0:1, 0:TBL], in0=ps_s[0:1, 0:TBL],
                            scalar1=-1.0 / D, scalar2=None, op0=ALU.mult)
            v.drain()
            v.tensor_tensor(out=stats[0:1, 512:512 + TBL], in0=stats[0:1, 0:TBL],
                            in1=stats[0:1, 0:TBL], op=ALU.mult)
            v.drain()
            v.scalar_tensor_tensor(out=stats[0:1, 512:512 + TBL], in0=ps_q[0:1, 0:TBL],
                                   scalar=1.0 / D, in1=stats[0:1, 512:512 + TBL],
                                   op0=ALU.mult, op1=ALU.subtract)
            v.drain()
            v.tensor_scalar(out=stats[0:1, 512:512 + TBL], in0=stats[0:1, 512:512 + TBL],
                            scalar1=float(LN_EPS), scalar2=None, op0=ALU.add)
            v.drain()
            with nc.allow_low_precision(reason="bf16 LN stats"):
                return v.reciprocal(stats[0:1, 512:512 + TBL],
                                    stats[0:1, 512:512 + TBL])
        ops.sig("vector", s_st, v_stats)
        ops.wait("scalar", s_st, idx + 1)
        def a_sqrt(s):
            with nc.allow_low_precision(reason="bf16 LN stats"):
                return s.sqrt(stats[0:1, 512:512 + TBL], stats[0:1, 512:512 + TBL])
        ops.sig("scalar", s_rs, a_sqrt)
        ops.wait("tensor", s_rs, idx + 1)
        if idx >= 1:
            ops.wait("tensor", s_ev, idx)
        def mm_bcast(t):
            t.matmul(ps_mu[:, 0:TBL], ones_row[:], stats[0:1, 0:TBL],
                     start=True, stop=True)
            return t.matmul(ps_rs[:, 0:TBL], ones_row[:],
                            stats[0:1, 512:512 + TBL], start=True, stop=True)
        ops.sig("tensor", s_mm, mm_bcast)
        ops.wait("vector", s_mm, 2 * idx + 2)

        def v_norm(v, b=b, tb=tb):
            xin3 = x_sb[:].rearrange("p (n t) -> p n t", t=T)[
                :, b * NCH:(b + 1) * NCH, tb * TBL:(tb + 1) * TBL]
            xout3 = xn_sb[:].rearrange("p (n t) -> p n t", t=TP)[
                :, b * NCH:(b + 1) * NCH, PAD + tb * TBL:PAD + (tb + 1) * TBL]
            mub = ps_mu[:, None, 0:TBL].broadcast_to([128, NCH, TBL])
            rsb = ps_rs[:, None, 0:TBL].broadcast_to([128, NCH, TBL])
            v.tensor_tensor(out=xout3, in0=xin3, in1=mub, op=ALU.add)
            v.drain()
            return v.tensor_tensor(out=xout3, in0=xout3, in1=rsb, op=ALU.mult)
        ops.sig("vector", s_ev, v_norm)


def make_tables():
    half = np.arange(0, DHF, 2, dtype=np.float64)
    inv = 1.0 / 10000.0 ** (half / DHF)
    t = np.arange(T, dtype=np.float64)
    fr = t[:, None] * inv[None, :]
    base = (half + 0.4 * DHF) / (1.4 * DHF)
    pw = (t - T // 2) / float(WIN // 2)
    sc = base[None, :] ** pw[:, None]
    return (np.cos(fr).astype(np.float32), np.sin(fr).astype(np.float32),
            sc.astype(np.float32))


def declare_inputs(nc):
    dram = {}

    def din(name, shape, dtype=F32):
        dram[name] = nc.dram_tensor(name, list(shape), dtype,
                                    kind="ExternalInput").ap()
    din("x_in", (NB, T, D))
    din("W0", (D, D))
    din("k_norm_w", (DH0,))
    din("o_norm_w", (D,))
    din("o_norm_b", (D,))
    din("eta_raw", (H0,))
    din("lam_logit", (H0,))
    din("wqkv", (2, D, 1536))
    din("q_scale", (2, DHF))
    din("k_scale", (2, DHF))
    din("wo", (2, 512, D))
    din("ln_g", (2, D))
    din("ln_b", (2, D))
    din("w1", (2, D, 5460))
    din("b1", (2, 5460))
    din("w2", (2, HID, D))
    din("b2", (2, D))
    din("Wq1", (D, D))
    din("Wk1", (D, D))
    din("Wv1", (D, D))
    din("Wo1", (D, D))
    din("Wf1", (D, D))
    din("bf1", (D,))
    din("t_cs", (T, 32))
    din("t_sn", (T, 32))
    din("t_scale", (T, 32))
    dram["s_w1"] = nc.dram_tensor("s_w1", [2, 44, 128, 1024], BF16,
                                  kind="Internal").ap()
    dram["s_w2"] = nc.dram_tensor("s_w2", [2, 22, 2, 128, 512], BF16,
                                  kind="Internal").ap()
    dram["y"] = nc.dram_tensor("y", [NB, T, D], F32, kind="ExternalOutput").ap()
    return dram


def build_program(stages=None):
    nc = bass.Bass("TRN2", target_bir_lowering=False)
    c = Ctx(nc)
    dram = declare_inputs(nc)
    mode = "full" if stages is None else stages[0]
    with nc.Block() as block:
        emit_load(c, dram)
        emit_top_ln(c, "ln0")
        if mode in ("l0a", "l0", "full"):
            emit_l0_attn(c, dram)
            if mode != "l0a":
                emit_w_scratch(c, dram)
        if mode in ("l0", "full"):
            emit_l0_ffn(c, dram)
        if mode == "full":
            emit_top_ln(c, "ln1")
            emit_l1(c, dram)
        emit_store(c, dram, debug=(mode == "ln"))
        c.ops.flush(block)
    return nc


def emit_store(c, dram, debug=False):
    ops = c.ops
    ident = c.t["ident"]
    y = dram["y"]
    ystg = c.t["xstg"]
    ps_g = c.t["psG"]
    ps_y0 = ps_g[:, 0:512]
    ps_y1 = ps_g[:, 512:1024]
    s_ytr = ops.sem("s_ytr")
    s_yev = ops.sem("s_yev")
    s_outs = [ops.sem("s_out0"), ops.sem("s_out1")]
    ops.barrier("store")
    src_t = c.t["xn_sb"] if debug else c.t["x_sb"]

    def scol(b, ch, t0):
        return ncol(b, ch, t0) if debug else xcol(b, ch, t0)

    items = [(b, tc) for b in range(NB) for tc in range(CHK)]
    for idx, (b, tc) in enumerate(items):
        buf = idx % 2
        if idx >= 1:
            ops.wait("tensor", s_yev, idx - 1 + 1)

        def mm_tr(t, b=b, tc=tc):
            last = None
            for ch in range(NCH):
                ps = ps_y0 if ch % 2 == 0 else ps_y1
                last = t.transpose(ps[:, (ch // 2) * 128:(ch // 2) * 128 + 128],
                                   src_t[:, scol(b, ch, tc * 128):scol(b, ch, tc * 128) + 128],
                                   ident[:])
            return last
        ops.sig("tensor", s_ytr, mm_tr)
        if idx >= 2:
            ops.wait("vector", s_outs[buf], s_outs[buf].v)
        ops.wait("vector", s_ytr, idx + 1)

        def v_ev(v, buf=buf):
            last = None
            for ch in range(NCH):
                ps = ps_y0 if ch % 2 == 0 else ps_y1
                last = v.tensor_copy(
                    ystg[:, buf * 1024 + ch * 128:buf * 1024 + (ch + 1) * 128],
                    ps[:, (ch // 2) * 128:(ch // 2) * 128 + 128])
            return last
        ops.sig("vector", s_yev, v_ev)
        ops.wait("sync", s_yev, idx + 1)
        ops.sig("sync", s_outs[buf],
                lambda e, b=b, tc=tc, buf=buf: e.dma_start(
                    out=y[b, tc * 128:(tc + 1) * 128, :],
                    in_=ystg[:, buf * 1024:(buf + 1) * 1024]), n=16)
    ops.wait("sync", s_outs[0], s_outs[0].v)
    ops.wait("sync", s_outs[1], s_outs[1].v)


def emit_l0_attn(c, dram):
    nc, ops = c.nc, c.ops
    x_sb, xn_sb = c.t["x_sb"], c.t["xn_sb"]
    ident, identf = c.t["ident"], c.t["identf"]
    iota_pr, iota_p1, iota_rev = c.t["iota_pr"], c.t["iota_p1"], c.t["iota_rev"]
    m_low = c.t["m_low"]
    ones_rowf = c.t["ones_rowf"]
    psA, psB, psC, psD = c.t["psA"], c.t["psB"], c.t["psC"], c.t["psD"]
    psF, psG, psH, psI = c.t["psF"], c.t["psG"], c.t["psH"], c.t["psI"]
    xsq = c.t["ln_xsq"]

    WT0 = c.sb("WT0", [128, NCH * 1024], BF16)
    knw2 = c.sb("knw2", [128, 1], F32)
    onw = c.sb("onw", [128, NCH], F32)
    onb = c.sb("onb", [128, NCH], F32)
    ones2 = c.sb("ones2", [128, 2], BF16)
    ident2 = c.sb("ident2", [2, 2], BF16)
    sel2 = c.sb("sel2", [2, 128], F32)
    iotaLM = c.sb("iotaLM", [128, 128], F32)
    iotaUM = c.sb("iotaUM", [128, 128], F32)
    prow = c.sb("prow", [1, 64], F32)
    pbc = c.sb("pbc", [128, 48], F32)
    negb_bc = c.sb("negb_bc", [128, 16], F32)
    lamC_bc = c.sb("lamC_bc", [128, 16], F32)
    gneg = c.sb("gneg", [128, 16], F32)
    sdec = c.sb("sdec", [128, 16], F32)
    dpair = c.sb("dpair", [128, 32 * 128], BF16)
    vneg = c.sb("vneg", [128, 1024], BF16)
    kt = c.sb("kt", [128, 1024], BF16)
    r2i = c.alias("r2i", [2, 1024], F32, "xstg", 0)
    rstd2 = c.alias("rstd2", [2, 1024], F32, "xstg", 4096)
    krow = c.sb("krow", [2, 1024], BF16)
    kbt = c.sb("kbt", [128, 16], F32)
    osum = c.sb("osum", [128, 1024], BF16)
    lnvec = c.sb("lnvec", [128, 4], F32)
    nh = c.sb("nh", [128, 2 * 256], BF16)
    nhT = c.sb("nhT", [128, 2 * 256], BF16)
    sqp = c.sb("sqp", [128, 2 * 5 * 512], BF16)
    rbuf = c.sb("rbuf", [128, 2 * 2 * 128], BF16)
    kdec = c.sb("kdec", [128, 2 * 128], BF16)
    S0 = c.sb("S0", [128, NB * 8 * 64], BF16)

    def sq_tile(px, j):
        o = (px * 9 + j) * 256
        return sqp[:, o:o + 256]

    s_pr = ops.sem("a_pr")
    s_T = ops.sem("a_T")
    s_V = ops.sem("a_V")
    s_A = ops.sem("a_A")
    s_ld = ops.sem("a_ld")
    s_gd = ops.sem("a_gd")

    s_pd = ops.sem("a_pd")

    def g_dma(g):
        g.dma_start(out=prow[0:1, 0:16], in_=dram["lam_logit"][None, :]).then_inc(s_pd.h, 16)
        g.dma_start(out=prow[0:1, 16:32], in_=dram["eta_raw"][None, :]).then_inc(s_pd.h, 16)
        g.dma_start(out=knw2[0:64, 0:1], in_=dram["k_norm_w"][:, None]).then_inc(s_pd.h, 16)
        g.dma_start(out=knw2[64:128, 0:1], in_=dram["k_norm_w"][:, None]).then_inc(s_pd.h, 16)
    s_pd.v += 64
    ops.on("gpsimd", g_dma)

    def g_tiles(g):
        g.iota(iotaLM[:], pattern=[[-1, 128]], base=0, channel_multiplier=1,
               allow_small_or_imprecise_dtypes=True).then_inc(s_pr.h, 1)
        g.iota(iotaUM[:], pattern=[[1, 128]], base=0, channel_multiplier=-1,
               allow_small_or_imprecise_dtypes=True).then_inc(s_pr.h, 1)
        g.memset(ones2[:], 0.0).then_inc(s_pr.h, 1)
        g.memset(sel2[:], 1.0).then_inc(s_pr.h, 1)
        g.memset(ident2[:], 1.0).then_inc(s_pr.h, 1)
    s_pr.v += 5
    ops.on("gpsimd", g_tiles)
    ops.wait("gpsimd", s_pr, s_pr.v)

    def g_tiles2(g):
        g.affine_select(out=iotaLM[:], in_=iotaLM[:], compare_op=ALU.is_gt,
                        fill=1e30, base=0, pattern=[[-1, 128]],
                        channel_multiplier=1).then_inc(s_pr.h, 1)
        g.affine_select(out=iotaUM[:], in_=iotaUM[:], compare_op=ALU.is_gt,
                        fill=1e30, base=0, pattern=[[1, 128]],
                        channel_multiplier=-1).then_inc(s_pr.h, 1)
        g.memset(ones2[0:64, 0:1], 1.0).then_inc(s_pr.h, 1)
        g.memset(ones2[64:128, 1:2], 1.0).then_inc(s_pr.h, 1)
        g.affine_select(out=sel2[:], in_=sel2[:], compare_op=ALU.is_ge,
                        fill=0.0, base=0, pattern=[[1, 128]],
                        channel_multiplier=-64).then_inc(s_pr.h, 1)
        g.affine_select(out=ident2[:], in_=ident2[:], compare_op=ALU.not_equal,
                        fill=0.0, base=0, pattern=[[-1, 2]],
                        channel_multiplier=1).then_inc(s_pr.h, 1)
        g.memset(S0[:], 0.0).then_inc(s_pr.h, 1)
    s_pr.v += 7
    ops.on("gpsimd", g_tiles2)
    ops.wait("gpsimd", s_pr, s_pr.v)

    def g_tiles3(g):
        g.affine_select(out=sel2[:], in_=sel2[:], compare_op=ALU.is_ge,
                        fill=0.0, base=63, pattern=[[-1, 128]],
                        channel_multiplier=64).then_inc(s_pr.h, 1)
    s_pr.v += 1
    ops.on("gpsimd", g_tiles3)

    ops.wait("scalar", s_pd, 64)

    def a_prep(a):
        a.activation(out=prow[0:1, 32:48], in_=prow[0:1, 0:16], func=AF.Sigmoid)
        a.drain()
        a.activation(out=prow[0:1, 48:64], in_=prow[0:1, 16:32], func=AF.Tanh)
        a.drain()
        return a.activation(out=prow[0:1, 0:16], in_=prow[0:1, 32:48], func=AF.Ln)
    ops.sig("scalar", s_A, a_prep)
    ops.wait("vector", s_A, s_A.v)

    def v_beta(v):
        v.tensor_scalar(out=prow[0:1, 32:48], in0=prow[0:1, 32:48], scalar1=-1.0,
                        scalar2=None, op0=ALU.add)
        v.drain()
        v.tensor_tensor(out=prow[0:1, 16:32], in0=prow[0:1, 48:64],
                        in1=prow[0:1, 32:48], op=ALU.mult)
        v.drain()
        return v.tensor_scalar(out=prow[0:1, 16:32], in0=prow[0:1, 16:32],
                               scalar1=1.0 / 64.0, scalar2=None, op0=ALU.mult)
    ops.sig("vector", s_V, v_beta)
    ops.wait("tensor", s_V, s_V.v)
    ops.sig("tensor", s_T,
            lambda t: t.matmul(psD[:, 0:32], ones_rowf[:], prow[0:1, 0:32],
                               start=True, stop=True))
    ops.wait("vector", s_T, s_T.v)

    def v_pbc(v):
        v.tensor_copy(pbc[:, 0:32], psD[:, 0:32])
        v.drain()
        return v.tensor_scalar(out=negb_bc[:], in0=pbc[:, 16:32], scalar1=-1.0,
                               scalar2=None, op0=ALU.mult)
    ops.sig("vector", s_V, v_pbc)
    ops.wait("scalar", s_V, s_V.v)

    def a_exp(a):
        a.activation(out=lamC_bc[:], in_=pbc[:, 0:16], func=AF.Exp, scale=128.0)
        for h in range(16):
            a.activation(out=gneg[:, h:h + 1], in_=iota_p1[:], func=AF.Exp,
                         scale=pbc[:, h:h + 1])
        last = None
        for h in range(16):
            last = a.activation(out=sdec[:, h:h + 1], in_=iota_rev[:], func=AF.Exp,
                                scale=pbc[:, h:h + 1])
        return last
    ops.sig("scalar", s_A, a_exp)
    ops.wait("vector", s_A, s_A.v)

    def v_gs(v):
        v.tensor_scalar(out=gneg[:], in0=gneg[:], scalar1=-1.0, scalar2=None,
                        op0=ALU.mult)
        return v.tensor_tensor(out=sdec[:], in0=sdec[:], in1=pbc[:, 16:32],
                               op=ALU.mult)
    ops.sig("vector", s_V, v_gs)
    ops.wait("scalar", s_pr, s_pr.v)

    dtmp = c.t["ln_xsq"]
    for h in range(16):
        for which, src in ((0, iotaLM), (1, iotaUM)):
            slot = (2 * h + which) % 4
            dsl = dtmp[:, 2048 + slot * 128:2048 + (slot + 1) * 128]
            if 2 * h + which >= 4:
                ops.wait("scalar", s_V, s_V.v - 3)
            ops.sig("scalar", s_A,
                    lambda a, src=src, h=h, dsl=dsl: a.activation(
                        out=dsl, in_=src[:], func=AF.Exp, scale=pbc[:, h:h + 1]))
            ops.wait("vector", s_A, s_A.v)
            idx = 4 * (h // 2) + (h % 2) + 2 * which
            ops.sig("vector", s_V,
                    lambda v, h=h, idx=idx, dsl=dsl: v.tensor_scalar(
                        out=dpair[:, idx * 128:(idx + 1) * 128],
                        in0=dsl, scalar1=negb_bc[:, h:h + 1], scalar2=None,
                        op0=ALU.mult))

    cstg = c.sb("cstg", [128, 128], F32)
    for name, dst in (("o_norm_w", onw), ("o_norm_b", onb)):
        ops.wait("sync", s_T, s_T.v)
        ops.sig("sync", s_ld,
                lambda e, name=name: e.dma_start(
                    out=cstg[0:8, 0:128],
                    in_=dram[name].rearrange("(a b) -> a b", b=128)), n=16)
        ops.wait("tensor", s_ld, s_ld.v)
        ops.wait("tensor", s_V, s_V.v)
        ops.sig("tensor", s_T,
                lambda t: t.transpose(psD[:, 0:8], cstg[0:8, 0:128], identf[0:8, 0:8]))
        ops.wait("vector", s_T, s_T.v)
        ops.sig("vector", s_V,
                lambda v, dst=dst: v.tensor_copy(dst[:], psD[:, 0:8]))

    ops.barrier("l0attn_wprep")
    wld = c.t["ln_xsq"]
    wtb_vals = []
    s_gw = [ops.sem("a_gw0"), ops.sem("a_gw1")]
    for oc in range(NCH):
        buf = oc % 2
        if oc >= 2:
            ops.wait("gpsimd", s_T, wtb_vals[oc - 2])
        ops.sig("gpsimd", s_gw[buf],
                lambda g, oc=oc, buf=buf: g.dma_start(
                    out=wld[:, buf * 1024:(buf + 1) * 1024],
                    in_=dram["W0"][oc * 128:(oc + 1) * 128, :]), n=16)
        ops.wait("tensor", s_gw[buf], s_gw[buf].v)

        def mm_wt_a(t, oc=oc, buf=buf):
            last = None
            for dc in range(4):
                last = t.transpose(psG[:, dc * 128:dc * 128 + 128],
                                   wld[:, buf * 1024 + dc * 128:buf * 1024 + dc * 128 + 128],
                                   ident[:])
            return last

        def mm_wt_b(t, oc=oc, buf=buf):
            last = None
            for dc in range(4, 8):
                last = t.transpose(psG[:, dc * 128:dc * 128 + 128],
                                   wld[:, buf * 1024 + dc * 128:buf * 1024 + dc * 128 + 128],
                                   ident[:])
            return last
        if oc >= 1:
            ops.wait("tensor", s_V, s_V.v)
        vta = ops.sig("tensor", s_T, mm_wt_a)
        ops.wait("vector", s_T, vta)

        def ev_a(v, oc=oc):
            last = None
            for dc in range(4):
                last = v.tensor_scalar(
                    out=WT0[:, dc * 1024 + oc * 128:dc * 1024 + oc * 128 + 128],
                    in0=psG[:, dc * 128:dc * 128 + 128], scalar1=-1.0, scalar2=None,
                    op0=ALU.mult)
            return last
        ops.sig("vector", s_V, ev_a)
        ops.wait("tensor", s_V, s_V.v)
        vtb = ops.sig("tensor", s_T, mm_wt_b)
        wtb_vals.append(vtb)
        ops.wait("vector", s_T, vtb)

        def ev_b(v, oc=oc):
            last = None
            for dc in range(4, 8):
                last = v.tensor_scalar(
                    out=WT0[:, dc * 1024 + oc * 128:dc * 1024 + oc * 128 + 128],
                    in0=psG[:, dc * 128:dc * 128 + 128], scalar1=-1.0, scalar2=None,
                    op0=ALU.mult)
            return last
        ops.sig("vector", s_V, ev_b)

    ops.barrier("l0attn_prep")

    if L0A_LEVEL >= 1:
        state_ev = {}
        for b in range(NB):
            for cc in range(CHK):
                emit_l0_chunk(c, b, cc, state_ev)

    ops.barrier("l0attn_done")


def emit_l0_chunk(c, b, cc, state_ev):
    nc, ops = c.nc, c.ops
    x_sb, xn_sb = c.t["x_sb"], c.t["xn_sb"]
    psA, psB, psC, psD = c.t["psA"], c.t["psB"], c.t["psC"], c.t["psD"]
    psF, psG, psH, psI = c.t["psF"], c.t["psG"], c.t["psH"], c.t["psI"]
    WT0, knw2, onw, onb = c.t["WT0"], c.t["knw2"], c.t["onw"], c.t["onb"]
    ones2, ident2, sel2 = c.t["ones2"], c.t["ident2"], c.t["sel2"]
    pbc, negb_bc, lamC_bc = c.t["pbc"], c.t["negb_bc"], c.t["lamC_bc"]
    gneg, sdec, dpair = c.t["gneg"], c.t["sdec"], c.t["dpair"]
    vneg, kt, r2i, rstd2 = c.t["vneg"], c.t["kt"], c.t["r2i"], c.t["rstd2"]
    krow, kbt, osum, lnvec = c.t["krow"], c.t["kbt"], c.t["osum"], c.t["lnvec"]
    nh, nhT, sqp, rbuf, kdec = c.t["nh"], c.t["nhT"], c.t["sqp"], c.t["rbuf"], c.t["kdec"]
    S0 = c.t["S0"]
    ident = c.t["ident"]
    xsq = c.t["ln_xsq"]
    s_T, s_V, s_A = ops.sem("a_T"), ops.sem("a_V"), ops.sem("a_A")

    def sq_tile(px, j):
        o = (px * 9 + j) * 256
        return c.t["sqp"][:, o:o + 256]

    xn3 = xn_sb[:].rearrange("p (n t) -> p n t", t=TP)[
        :, b * NCH:(b + 1) * NCH, PAD + cc * 128:PAD + cc * 128 + 128]

    ops.wait("tensor", s_V, s_V.v)
    ops.wait("tensor", s_A, s_A.v)

    def mm_static(t):
        last = None
        for half, ps in ((0, psA), (1, psB)):
            for dc in range(NCH):
                last = t.matmul(
                    ps[:, 0:512],
                    xn_sb[:, ncol(b, dc, cc * 128):ncol(b, dc, cc * 128) + 128],
                    WT0[:, dc * 1024 + half * 512:dc * 1024 + (half + 1) * 512],
                    start=(dc == 0), stop=(dc == NCH - 1))
        return last
    vs = ops.sig("tensor", s_T, mm_static)
    ops.wait("scalar", s_T, vs)

    def a_vneg(a):
        a.activation(out=vneg[:, 0:512], in_=psA[:, 0:512], func=AF.Copy)
        return a.activation(out=vneg[:, 512:1024], in_=psB[:, 0:512], func=AF.Copy)
    va_v = ops.sig("scalar", s_A, a_vneg)

    def a_sq(a):
        a.activation(out=xsq[:, 0:1024].rearrange("p (n t) -> p n t", t=128),
                     in_=xn3, func=AF.Square)
        return a.activation(out=xsq[:, 1024:2048].rearrange("p (n t) -> p n t", t=128),
                            in_=xn3, func=AF.Square, scale=knw2[:, 0:1])
    va_sq = ops.sig("scalar", s_A, a_sq)
    ops.wait("tensor", s_A, va_sq)
    ops.wait("tensor", s_V, s_V.v)

    def mm_hs(t):
        last = None
        for dc in range(NCH):
            ps = psC if dc < 4 else psD
            col = (dc % 4) * 128
            last = t.matmul(ps[0:2, col:col + 128], ones2[:],
                            xsq[:, dc * 128:dc * 128 + 128], start=True, stop=True)
        for dc in range(NCH):
            ps = psF if dc < 4 else psH
            col = (dc % 4) * 128
            last = t.matmul(ps[0:2, col:col + 128], ones2[:],
                            xsq[:, 1024 + dc * 128:1024 + dc * 128 + 128],
                            start=True, stop=True)
        return last
    vhs = ops.sig("tensor", s_T, mm_hs)
    ops.wait("vector", s_T, vhs)

    def v_rstd(v):
        v.tensor_scalar(out=r2i[0:2, 0:512], in0=psC[0:2, 0:512],
                        scalar1=1.0 / 64.0, scalar2=float(RMS_EPS),
                        op0=ALU.mult, op1=ALU.add)
        v.tensor_scalar(out=r2i[0:2, 512:1024], in0=psD[0:2, 0:512],
                        scalar1=1.0 / 64.0, scalar2=float(RMS_EPS),
                        op0=ALU.mult, op1=ALU.add)
        v.drain()
        v.reciprocal(r2i[0:2, 0:512], r2i[0:2, 0:512])
        v.reciprocal(r2i[0:2, 512:1024], r2i[0:2, 512:1024])
        v.drain()
        v.tensor_tensor(out=krow[0:2, 0:512], in0=psF[0:2, 0:512],
                        in1=r2i[0:2, 0:512], op=ALU.mult)
        return v.tensor_tensor(out=krow[0:2, 512:1024], in0=psH[0:2, 0:512],
                               in1=r2i[0:2, 512:1024], op=ALU.mult)
    vr = ops.sig("vector", s_V, v_rstd)
    ops.wait("scalar", s_V, vr)
    ops.sig("scalar", s_A,
            lambda a: a.activation(out=rstd2[0:2, 0:1024], in_=r2i[0:2, 0:1024],
                                   func=AF.Sqrt))
    ops.wait("tensor", s_A, s_A.v)

    def mm_rb(t):
        last = None
        for dc in range(NCH):
            ps = psA if dc < 4 else psB
            col = (dc % 4) * 128
            last = t.matmul(ps[:, col:col + 128], sel2[:],
                            rstd2[0:2, dc * 128:dc * 128 + 128],
                            start=True, stop=True)
        for hp in range(8):
            last = t.matmul(psC[:, 2 * hp:2 * hp + 2],
                            krow[0:2, hp * 128:hp * 128 + 128], ident2[:],
                            start=True, stop=True)
        return last
    vrb = ops.sig("tensor", s_T, mm_rb)
    ops.wait("vector", s_T, vrb)

    def v_kfin(v):
        rb3a = psA[:, None, 0:512].rearrange("p a (n t) -> p (a n) t", t=128)
        rb3b = psB[:, None, 0:512].rearrange("p a (n t) -> p (a n) t", t=128)
        xn3a = xn_sb[:].rearrange("p (n t) -> p n t", t=TP)[
            :, b * NCH:b * NCH + 4, PAD + cc * 128:PAD + cc * 128 + 128]
        xn3b = xn_sb[:].rearrange("p (n t) -> p n t", t=TP)[
            :, b * NCH + 4:b * NCH + 8, PAD + cc * 128:PAD + cc * 128 + 128]
        kt3a = kt[:, 0:512].rearrange("p (n t) -> p n t", t=128)
        kt3b = kt[:, 512:1024].rearrange("p (n t) -> p n t", t=128)
        v.scalar_tensor_tensor(out=kt3a, in0=xn3a, scalar=knw2[:, 0:1], in1=rb3a,
                               op0=ALU.mult, op1=ALU.mult)
        v.scalar_tensor_tensor(out=kt3b, in0=xn3b, scalar=knw2[:, 0:1], in1=rb3b,
                               op0=ALU.mult, op1=ALU.mult)
        v.drain()
        v.tensor_tensor(out=kbt[:], in0=psC[:, 0:16], in1=pbc[:, 16:32], op=ALU.mult)
        v.drain()
        return v.tensor_scalar(out=kbt[:], in0=kbt[:], scalar1=-1.0, scalar2=None,
                               op0=ALU.add)
    vkf = ops.sig("vector", s_V, v_kfin)

    if L0A_LEVEL >= 2:
        emit_scan_pairs(c, b, cc, state_ev, vkf, va_v)

    if L0A_LEVEL >= 3:
        emit_l0_outln(c, b, cc)





def dl_pair(c, pr):
    return c.t["dpair"][:, (4 * pr) * 128:(4 * pr + 2) * 128]


def du_pair(c, pr):
    return c.t["dpair"][:, (4 * pr + 2) * 128:(4 * pr + 4) * 128]


def emit_scan_pairs(c, b, cc, state_ev, vkf, va_v):
    ops = c.ops
    psA, psB, psC, psD = c.t["psA"], c.t["psB"], c.t["psC"], c.t["psD"]
    psF, psG, psH, psI = c.t["psF"], c.t["psG"], c.t["psH"], c.t["psI"]
    kt, kdec, rbuf, nh, nhT = c.t["kt"], c.t["kdec"], c.t["rbuf"], c.t["nh"], c.t["nhT"]
    sqp, S0, osum = c.t["sqp"], c.t["S0"], c.t["osum"]
    vneg, gneg, sdec, lamC_bc, kbt = (c.t["vneg"], c.t["gneg"], c.t["sdec"],
                                      c.t["lamC_bc"], c.t["kbt"])
    ident = c.t["ident"]
    s_T, s_V, s_A = ops.sem("a_T"), ops.sem("a_V"), ops.sem("a_A")
    BANKS = {0: (psA, psB, psC), 1: (psD, psF, psH)}

    def sql(px, lev):
        o = px * 2560 + (lev - 1) * 512
        return sqp[:, o:o + 512]

    val = {}

    def emit_step(pr, step):
        px = pr % 2
        bankE, bankO, alt = BANKS[px]
        he, ho = 2 * pr, 2 * pr + 1
        kte = kt[0:64, pr * 128:(pr + 1) * 128]
        kto = kt[64:128, pr * 128:(pr + 1) * 128]
        scol = (b * 8 + pr) * 64
        Spair = S0[:, scol:scol + 64]
        slot0 = rbuf[:, px * 256:px * 256 + 128]
        slot1 = rbuf[:, px * 256 + 128:px * 256 + 256]
        kd = kdec[:, px * 128:(px + 1) * 128]

        if step == "AP":
            ops.wait("tensor", s_V, max(vkf, state_ev.get((b, pr), 0),
                                        val.get((pr - 2, "SE"), 0)))
            ops.wait("tensor", s_A, max(va_v, val.get((pr - 2, "OS"), 0)))

            def mm(t):
                t.matmul(bankE[:, 0:128], kte, kte, start=True, stop=True)
                t.matmul(bankE[:, 256:320], kte, Spair[0:64, :], start=True, stop=True)
                t.matmul(bankO[:, 0:128], kto, kto, start=True, stop=True)
                return t.matmul(bankO[:, 256:320], kto, Spair[64:128, :],
                                start=True, stop=True)
            val[(pr, "AP")] = ops.sig("tensor", s_T, mm)
        elif step == "EV1":
            ops.wait("vector", s_T, val[(pr, "AP")])
            dp = c.t["dpair"]

            def ev(v):
                o = 4 * pr * 128
                v.tensor_tensor(out=nh[:, px * 256:px * 256 + 128],
                                in0=bankE[:, 0:128], in1=dp[:, o:o + 128],
                                op=ALU.mult)
                v.tensor_tensor(out=nh[:, px * 256 + 128:px * 256 + 256],
                                in0=bankO[:, 0:128], in1=dp[:, o + 128:o + 256],
                                op=ALU.mult)
                v.tensor_tensor(out=nhT[:, px * 256:px * 256 + 128],
                                in0=bankE[:, 0:128], in1=dp[:, o + 256:o + 384],
                                op=ALU.mult)
                v.tensor_tensor(out=nhT[:, px * 256 + 128:px * 256 + 256],
                                in0=bankO[:, 0:128], in1=dp[:, o + 384:o + 512],
                                op=ALU.mult)
                v.scalar_tensor_tensor(out=slot0[:, 0:64], in0=bankE[:, 256:320],
                                       scalar=gneg[:, he:he + 1],
                                       in1=vneg[:, he * 64:he * 64 + 64],
                                       op0=ALU.mult, op1=ALU.add)
                return v.scalar_tensor_tensor(out=slot0[:, 64:128],
                                              in0=bankO[:, 256:320],
                                              scalar=gneg[:, ho:ho + 1],
                                              in1=vneg[:, ho * 64:ho * 64 + 64],
                                              op0=ALU.mult, op1=ALU.add)
            val[(pr, "EV1")] = ops.sig("vector", s_V, ev)
        elif step.startswith("L"):
            lev = int(step[1:])
            psX = alt if lev % 2 == 1 else bankE
            if lev == 1:
                ops.wait("tensor", s_V, val[(pr, "EV1")])
                pe = nh[:, px * 256:px * 256 + 128]
                po = nh[:, px * 256 + 128:px * 256 + 256]
                be = nhT[:, px * 256:px * 256 + 128]
                bo = nhT[:, px * 256 + 128:px * 256 + 256]
            else:
                ops.wait("tensor", s_A, val[(pr, f"E{lev-1}")])
                prev = sql(px, lev - 1)
                pe, po = prev[:, 0:128], prev[:, 128:256]
                be, bo = prev[:, 256:384], prev[:, 384:512]

            def mm(t, pe=pe, po=po, be=be, bo=bo, lev=lev, psX=psX):
                if lev < 5:
                    t.matmul(psX[:, 0:128], be, pe, start=True, stop=True)
                    t.matmul(psX[:, 128:256], bo, po, start=True, stop=True)
                t.matmul(psX[:, 256:384], pe, be, start=True, stop=True)
                return t.matmul(psX[:, 384:512], po, bo, start=True, stop=True)
            val[(pr, step)] = ops.sig("tensor", s_T, mm)
        elif step.startswith("E"):
            lev = int(step[1:])
            psX = alt if lev % 2 == 1 else bankE
            ops.wait("scalar", s_T, val[(pr, f"L{lev}")])
            dst = sql(px, lev)
            if lev < 5:
                ops.sig("scalar", s_A,
                        lambda a, dst=dst, psX=psX: a.activation(
                            out=dst[:, 0:512], in_=psX[:, 0:512], func=AF.Copy))
            else:
                ops.sig("scalar", s_A,
                        lambda a, dst=dst, psX=psX: a.activation(
                            out=dst[:, 256:512], in_=psX[:, 256:512], func=AF.Copy))
            val[(pr, step)] = ops.sem("a_A").v
        elif step.startswith("AE"):
            ai = int(step[2:])
            psX = alt if ai % 2 == 0 else bankE
            ops.wait("scalar" if ai == 0 else "vector", s_T, val[(pr, f"A{ai}")])
            if ai == 0:
                ops.sig("scalar", s_A,
                        lambda a, psX=psX: a.activation(out=slot1[:, 0:128],
                                                        in_=psX[:, 0:128],
                                                        func=AF.Copy))
                val[(pr, "AE0")] = ops.sem("a_A").v
            else:
                def ev(v, psX=psX):
                    return v.scalar_tensor_tensor(out=slot0[:, 0:128],
                                                  in0=psX[:, 0:128], scalar=1.0,
                                                  in1=slot0[:, 0:128],
                                                  op0=ALU.mult, op1=ALU.add)
                val[(pr, f"AE{ai}")] = ops.sig("vector", s_V, ev)
        elif step.startswith("A") and step != "AP":
            ai = int(step[1:])
            psX = alt if ai % 2 == 0 else bankE
            if ai == 0 or ai == 1:
                bt = sql(px, 5)[:, 256:512]
            elif ai < 8:
                lev = 7 - ai
                lev = 7 - ai
                if lev == 0:
                    bt = nhT[:, px * 256:px * 256 + 256]
                else:
                    bt = sql(px, lev)[:, 256:512]
            rhs = slot0 if ai != 1 else slot1
            if ai == 0:
                ops.wait("tensor", s_A, val[(pr, "E5")])
            else:
                ops.wait("tensor", s_V if ai != 1 else s_A, val[(pr, f"AE{ai-1}")])

            def mm(t, bt=bt, rhs=rhs, psX=psX):
                t.matmul(psX[:, 0:64], bt[:, 0:128], rhs[:, 0:64],
                         start=True, stop=True)
                return t.matmul(psX[:, 64:128], bt[:, 128:256], rhs[:, 64:128],
                                start=True, stop=True)
            val[(pr, step)] = ops.sig("tensor", s_T, mm)
        elif step == "TR":
            ops.wait("tensor", s_V, max(vkf, val.get((pr - 2, "KD"), 0)))

            def mm(t):
                t.transpose(psG[:, px * 128:px * 128 + 64], kte, ident[0:64, 0:64])
                return t.transpose(psI[:, px * 128:px * 128 + 64], kto,
                                   ident[64:128, 64:128])
            val[(pr, "TR")] = ops.sig("tensor", s_T, mm)
        elif step == "KD":
            ops.wait("vector", s_T, val[(pr, "TR")])

            def ev(v):
                v.tensor_scalar(out=kd[:, 0:64], in0=psG[:, px * 128:px * 128 + 64],
                                scalar1=sdec[:, he:he + 1], scalar2=None, op0=ALU.mult)
                return v.tensor_scalar(out=kd[:, 64:128],
                                       in0=psI[:, px * 128:px * 128 + 64],
                                       scalar1=sdec[:, ho:ho + 1], scalar2=None,
                                       op0=ALU.mult)
            val[(pr, "KD")] = ops.sig("vector", s_V, ev)
        elif step == "ST":
            ops.wait("tensor", s_V, val[(pr, "KD")])

            def mm(t):
                t.matmul(bankE[0:64, 384:448], kd[:, 0:64], slot0[:, 0:64],
                         start=True, stop=True)
                return t.matmul(bankO[64:128, 384:448], kd[:, 64:128],
                                slot0[:, 64:128], start=True, stop=True,
                                tile_position=(0, 64))
            val[(pr, "ST")] = ops.sig("tensor", s_T, mm)
        elif step == "SE":
            ops.wait("vector", s_T, val[(pr, "ST")])

            def ev(v):
                v.scalar_tensor_tensor(out=Spair[0:64, :], in0=Spair[0:64, :],
                                       scalar=lamC_bc[0:64, he:he + 1],
                                       in1=bankE[0:64, 384:448],
                                       op0=ALU.mult, op1=ALU.add)
                return v.scalar_tensor_tensor(out=Spair[64:128, :],
                                              in0=Spair[64:128, :],
                                              scalar=lamC_bc[64:128, ho:ho + 1],
                                              in1=bankO[64:128, 384:448],
                                              op0=ALU.mult, op1=ALU.add)
            val[(pr, "SE")] = ops.sig("vector", s_V, ev)
            state_ev[(b, pr)] = val[(pr, "SE")]
        elif step == "OS":
            ops.wait("scalar", s_V, val[(pr, "AE7")])

            def ev(a):
                a.activation(out=osum[:, he * 64:he * 64 + 64], in_=slot0[:, 0:64],
                             func=AF.Copy, scale=kbt[:, he:he + 1])
                return a.activation(out=osum[:, ho * 64:ho * 64 + 64],
                                    in_=slot0[:, 64:128], func=AF.Copy,
                                    scale=kbt[:, ho:ho + 1])
            val[(pr, "OS")] = ops.sig("scalar", s_A, ev)

    STEPS = (["AP", "EV1", "L1", "E1", "L2", "E2", "L3", "E3", "L4", "E4",
              "L5", "E5", "A0", "AE0", "A1", "AE1", "A2", "AE2", "A3", "AE3",
              "A4", "AE4", "A5", "AE5", "A6", "AE6", "A7", "AE7",
              "TR", "KD", "ST", "SE", "OS"])[:L0A_NSTEP]
    for g in range(4):
        for step in STEPS:
            for pr in (2 * g, 2 * g + 1):
                emit_step(pr, step)


def emit_l0_outln(c, b, cc):
    ops = c.ops
    osum, lnvec = c.t["osum"], c.t["lnvec"]
    onw, onb = c.t["onw"], c.t["onb"]
    x_sb, ident, psG = c.t["x_sb"], c.t["ident"], c.t["psG"]
    xsq = c.t["ln_xsq"]
    afftmp = xsq[:, 2048:3072]
    s_T, s_V, s_A = ops.sem("a_T"), ops.sem("a_V"), ops.sem("a_A")
    vOS = ops.sem("a_A").v
    vSE = ops.sem("a_V").v

    ops.wait("vector", s_A, vOS)
    ops.sig("vector", ops.sem("a_V"),
            lambda v: v.reduce_sum(lnvec[:, 2:3], osum[:], axis=mybir.AxisListType.X))
    ops.wait("scalar", s_A, vOS)
    ops.sig("scalar", ops.sem("a_A"),
            lambda a: a.activation(out=xsq[:, 3072:4096], in_=osum[:],
                                   func=AF.Square, accum_out=lnvec[:, 3:4]))
    ops.wait("vector", ops.sem("a_A"), ops.sem("a_A").v)

    def v_stats(v):
        v.drain()
        v.tensor_scalar(out=lnvec[:, 0:1], in0=lnvec[:, 2:3], scalar1=-1.0 / D,
                        scalar2=None, op0=ALU.mult)
        v.drain()
        v.tensor_tensor(out=lnvec[:, 2:3], in0=lnvec[:, 0:1], in1=lnvec[:, 0:1],
                        op=ALU.mult)
        v.drain()
        v.scalar_tensor_tensor(out=lnvec[:, 1:2], in0=lnvec[:, 3:4],
                               scalar=1.0 / D, in1=lnvec[:, 2:3],
                               op0=ALU.mult, op1=ALU.subtract)
        v.drain()
        v.tensor_scalar(out=lnvec[:, 1:2], in0=lnvec[:, 1:2],
                        scalar1=float(LN_EPS), scalar2=None, op0=ALU.add)
        v.drain()
        return v.reciprocal(lnvec[:, 1:2], lnvec[:, 1:2])
    vst = ops.sig("vector", ops.sem("a_V"), v_stats)
    ops.wait("scalar", ops.sem("a_V"), vst)
    vsq = ops.sig("scalar", ops.sem("a_A"),
                  lambda a: a.sqrt(lnvec[:, 1:2], lnvec[:, 1:2]))
    ops.wait("vector", ops.sem("a_A"), vsq)
    def v_norm(v):
        v.drain()
        return v.tensor_scalar(out=osum[:], in0=osum[:], scalar1=lnvec[:, 0:1],
                               scalar2=lnvec[:, 1:2], op0=ALU.add, op1=ALU.mult)
    vn = ops.sig("vector", ops.sem("a_V"), v_norm)
    ops.wait("tensor", ops.sem("a_V"), vn)

    def mm_tr(t):
        last = None
        for dc in range(NCH):
            last = t.transpose(psG[:, dc * 128:dc * 128 + 128],
                               osum[:, dc * 128:dc * 128 + 128], ident[:])
        return last
    vtr = ops.sig("tensor", ops.sem("a_T"), mm_tr)
    ops.wait("scalar", ops.sem("a_T"), vtr)

    def a_aff(a):
        last = None
        for dc in range(NCH):
            last = a.activation(out=afftmp[:, dc * 128:dc * 128 + 128],
                                in_=psG[:, dc * 128:dc * 128 + 128],
                                func=AF.Identity,
                                scale=onw[:, dc:dc + 1], bias=onb[:, dc:dc + 1])
        return last
    vaf = ops.sig("scalar", ops.sem("a_A"), a_aff)
    ops.wait("vector", ops.sem("a_A"), vaf)

    def v_add(v):
        last = None
        for dc in range(NCH):
            xs = x_sb[:, xcol(b, dc, cc * 128):xcol(b, dc, cc * 128) + 128]
            last = v.tensor_tensor(out=xs, in0=xs,
                                   in1=afftmp[:, dc * 128:dc * 128 + 128],
                                   op=ALU.add)
        return last
    ops.sig("vector", ops.sem("a_V"), v_add)


def emit_w_scratch(c, dram):
    nc, ops = c.nc, c.ops
    cvt = c.sb("cvt", [128, 4 * 1024], BF16)
    s_ci = [ops.sem(f"cv_i{k}") for k in range(4)]
    s_co = [ops.sem(f"cv_o{k}") for k in range(4)]
    s_cm = ops.sem("cv_m")
    sw1, sw2 = dram["s_w1"], dram["s_w2"]
    items = []
    for l in range(2):
        for hc in range(22):
            for part in range(2):
                items.append(("w1", l, hc, part))
    for idx, (kind, l, hc, x) in enumerate(items):
        slot = idx % 4
        ops.wait("gpsimd", s_co[slot], s_co[slot].v)
        if kind == "w1":
            asz = min(128, HID - hc * 128)
            u = 2 * hc + x
            if asz < 128:
                ops.sig("gpsimd", s_cm, lambda g, slot=slot: g.memset(
                    cvt[:, slot * 1024:(slot + 1) * 1024], 0.0))
                ops.wait("gpsimd", s_cm, s_cm.v)

            def g_in(g, l=l, hc=hc, part=x, asz=asz, slot=slot):
                cb = part * HID + hc * 128
                src = dram["w1"][l, :, cb:cb + asz].rearrange(
                    "(n p) c -> p n c", p=128)
                dst3 = cvt[:, slot * 1024:(slot + 1) * 1024].rearrange(
                    "p (n c) -> p n c", c=128)
                return g.dma_start(out=dst3[:, :, 0:asz] if asz < 128 else dst3,
                                   in_=src)
            ops.sig("gpsimd", s_ci[slot], g_in, n=16)
            ops.wait("gpsimd", s_ci[slot], s_ci[slot].v)
            ops.sig("gpsimd", s_co[slot],
                    lambda g, l=l, u=u, slot=slot: g.dma_start(
                        out=sw1[l, u],
                        in_=cvt[:, slot * 1024:(slot + 1) * 1024]), n=16)
        else:
            dgrp = x
            rsz = min(128, HID - hc * 128)
            tile = cvt[:, slot * 1024:slot * 1024 + 512]
            if rsz < 128:
                ops.sig("gpsimd", s_cm,
                        lambda g, tile=tile: g.memset(tile, 0.0))
                ops.wait("gpsimd", s_cm, s_cm.v)

            def g_in(g, l=l, hc=hc, dgrp=dgrp, rsz=rsz, tile=tile):
                src = dram["w2"][l, hc * 128:hc * 128 + rsz,
                                 dgrp * 512:(dgrp + 1) * 512]
                return g.dma_start(out=tile[0:rsz, :], in_=src)
            ops.sig("gpsimd", s_ci[slot], g_in, n=16)
            ops.wait("gpsimd", s_ci[slot], s_ci[slot].v)
            ops.sig("gpsimd", s_co[slot],
                    lambda g, l=l, hc=hc, dgrp=dgrp, tile=tile: g.dma_start(
                        out=sw2[l, hc, dgrp], in_=tile), n=16)
    c.cv_done = [(sem, sem.v) for sem in s_co]


def emit_l0_ffn(c, dram):
    nc, ops = c.nc, c.ops
    ops.barrier("ffn_start")
    x_sb, xn_sb = c.t["x_sb"], c.t["xn_sb"]

    base = c.addr_of("WT0")
    end = c.addr_of("S0") + 2048
    cur = [base]

    def arena(name, shape, dtype):
        import concourse.mybir as mb
        per = 1
        for d in shape[1:]:
            per *= d
        per *= mb.dt.size(dtype)
        off = (cur[0] + 31) // 32 * 32
        assert off + per <= end, f"arena overflow {name}: {off+per-base} > {end-base}"
        cur[0] = off + per
        h = nc.alloc_sbuf_tensor_at(name, shape, dtype, offset=off)
        c.t[name] = h
        return h

    arena("wqkv_sb", [128, NCH * 1536], BF16)
    arena("TA", [128, 4 * CHK * 64], BF16)
    arena("wo_sb", [128, 4 * 1024], BF16)
    arena("qnb", [128, 1024], BF16)
    arena("ropet", [128, 1024], BF16)
    arena("attnT", [128, 1280], BF16)
    arena("rowq", [1, 128], F32)
    arena("qkbc", [128, 128], F32)
    arena("fstat", [128, 24], F32)
    arena("lgb", [128, 2 * NCH], F32)
    arena("b1p", [128, 48], F32)
    arena("b2p", [128, NCH], F32)
    arena("lcbuf", [128, NCH], F32)
    c.t["tbase"] = nc.alloc_sbuf_tensor_at(
        "tbase", [128, 4 * CHK * 32], F32, offset=c.addr_of("wo_sb"))

    for sem, vv in c.cv_done:
        for eng in ENGS:
            ops.wait(eng, sem, vv)
    cvb = c.addr_of("cvt")
    c.t["krT"] = nc.alloc_sbuf_tensor_at("krT", [128, 5 * 512], BF16,
                                         offset=cvb)
    c.t["ffnpad"] = nc.alloc_sbuf_tensor_at("ffnpad", [128, 1536], BF16,
                                            offset=cvb + 5120)
    c.t["cstg2"] = c.t["cstg"]

    lnx = c.addr_of("ln_xsq")
    xst = c.addr_of("xstg")

    def at(name, shape, dtype, addr):
        h = nc.alloc_sbuf_tensor_at(name, shape, dtype, offset=addr)
        c.t[name] = h
        return h
    at("qT", [128, 512], BF16, lnx + 0)
    at("sqscr", [128, 1024], BF16, lnx + 1024)
    at("vring", [128, 5 * 512], BF16, lnx + 3072)
    at("obuf", [128, 2048], BF16, xst + 0)
    at("attnb", [128, 1280], BF16, xst + 4096)

    krb = c.addr_of("krT")
    at("hid", [128, 8 * 512], BF16, krb + 0)
    at("vtmp", [128, 1024], F32, krb + 0)
    at("gelt", [128, 1024], BF16, c.addr_of("qnb"))
    at("lnout", [128, 8 * 512], BF16, lnx + 0)
    at("w1blk", [128, 3 * 1024], BF16, xst + 0)
    at("w2blk", [128, 1024], BF16, xst + 6144)

    for l in range(2):
        emit_ffn_layer_params(c, dram, l)
        for b in range(NB):
            emit_ffn_phaseA(c, dram, l, b)
            emit_ffn_phaseB(c, dram, l, b)

    ops.barrier("ffn_add")
    s_V = ops.sem("f_V")
    for b in range(NB):
        def v_add(v, b=b):
            last = None
            for dc in range(NCH):
                xs = x_sb[:, xcol(b, dc, 0):xcol(b, dc, 0) + T]
                hn = xn_sb[:, ncol(b, dc, 0):ncol(b, dc, 0) + T]
                last = v.tensor_tensor(out=xs, in0=xs, in1=hn, op=ALU.add)
            return last
        ops.sig("vector", s_V, v_add)
    ops.barrier("ffn_done")


def emit_ffn_layer_params(c, dram, l):
    ops, nc = c.ops, c.nc
    wqkv_sb, wo_sb, TA, tb = c.t["wqkv_sb"], c.t["wo_sb"], c.t["TA"], c.t["tbase"]
    lgb, b1p, b2p = c.t["lgb"], c.t["b1p"], c.t["b2p"]
    rowq, qkbc, cstg2 = c.t["rowq"], c.t["qkbc"], c.t["cstg2"]
    psD = c.t["psD"]
    identf, ones_rowf = c.t["identf"], c.t["ones_rowf"]
    s_V, s_A, s_T, s_G = (ops.sem("f_V"), ops.sem("f_A"), ops.sem("f_T"),
                          ops.sem("f_G"))
    s_pd = ops.sem(f"f_pd{l}")
    s_pg = ops.sem(f"f_pg{l}")
    K32 = CHK * 32
    ops.barrier(f"ffn_params{l}")

    s_dt = ops.sem(f"f_dt{l}")
    for i, name in enumerate(("t_cs", "t_sn", "t_scale")):
        src = dram[name].rearrange("(n p) c -> p n c", p=128)
        dst = tb[:].rearrange("p (k n c) -> p k n c", k=4, c=32)[:, i]
        ops.sig("sync", s_dt, lambda e, src=src, dst=dst: e.dma_start(out=dst,
                                                                     in_=src), n=16)
    ops.wait("vector", s_dt, s_dt.v)
    ops.sig("vector", s_V,
            lambda v: v.reciprocal(tb[:, 3 * K32:4 * K32], tb[:, 2 * K32:3 * K32]))
    ops.sig("sync", s_pd,
            lambda e: e.dma_start(out=rowq[0:1, 0:64],
                                  in_=dram["q_scale"][l][None, :]), n=16)
    ops.sig("sync", s_pd,
            lambda e: e.dma_start(out=rowq[0:1, 64:128],
                                  in_=dram["k_scale"][l][None, :]), n=16)
    ops.wait("tensor", s_pd, s_pd.v)
    ops.sig("tensor", s_T,
            lambda t: t.matmul(psD[:, 0:128], ones_rowf[:], rowq[0:1, 0:128],
                               start=True, stop=True))
    ops.wait("vector", s_T, s_T.v)
    ops.sig("vector", s_V, lambda v: v.tensor_copy(qkbc[:], psD[:, 0:128]))

    def v_tables(v):
        tmp = c.t["vtmp"]
        t3 = lambda a: a.rearrange("p (n c) -> p n c", c=32)
        qs3 = lambda sl: qkbc[:, sl, None].transpose([0, 2, 1]).broadcast_to(
            [128, CHK, 32]) if False else qkbc[:, None, sl].broadcast_to(
            [128, CHK, 32])
        v.drain()
        v.tensor_tensor(out=tmp[:, 0:K32], in0=tb[:, 0:K32],
                        in1=tb[:, 2 * K32:3 * K32], op=ALU.mult)
        v.tensor_tensor(out=tmp[:, K32:2 * K32], in0=tb[:, K32:2 * K32],
                        in1=tb[:, 2 * K32:3 * K32], op=ALU.mult)
        v.drain()
        TAv = TA[:].rearrange("p (k n c) -> p k n c", k=8, c=32)
        v.scalar_tensor_tensor(out=TAv[:, 0], in0=t3(tmp[:, 0:K32]), scalar=8.0,
                               in1=qs3(slice(0, 32)), op0=ALU.mult, op1=ALU.mult)
        v.scalar_tensor_tensor(out=TAv[:, 1], in0=t3(tmp[:, 0:K32]), scalar=8.0,
                               in1=qs3(slice(32, 64)), op0=ALU.mult, op1=ALU.mult)
        v.scalar_tensor_tensor(out=TAv[:, 2], in0=t3(tmp[:, K32:2 * K32]),
                               scalar=-8.0, in1=qs3(slice(32, 64)),
                               op0=ALU.mult, op1=ALU.mult)
        v.scalar_tensor_tensor(out=TAv[:, 3], in0=t3(tmp[:, K32:2 * K32]),
                               scalar=8.0, in1=qs3(slice(0, 32)),
                               op0=ALU.mult, op1=ALU.mult)
        v.drain()
        v.tensor_tensor(out=tmp[:, 0:K32], in0=tb[:, 0:K32],
                        in1=tb[:, 3 * K32:4 * K32], op=ALU.mult)
        v.tensor_tensor(out=tmp[:, K32:2 * K32], in0=tb[:, K32:2 * K32],
                        in1=tb[:, 3 * K32:4 * K32], op=ALU.mult)
        v.drain()
        v.scalar_tensor_tensor(out=TAv[:, 4], in0=t3(tmp[:, 0:K32]), scalar=1.0,
                               in1=qs3(slice(64, 96)), op0=ALU.mult, op1=ALU.mult)
        v.scalar_tensor_tensor(out=TAv[:, 5], in0=t3(tmp[:, 0:K32]), scalar=1.0,
                               in1=qs3(slice(96, 128)), op0=ALU.mult, op1=ALU.mult)
        v.scalar_tensor_tensor(out=TAv[:, 6], in0=t3(tmp[:, K32:2 * K32]),
                               scalar=-1.0, in1=qs3(slice(96, 128)),
                               op0=ALU.mult, op1=ALU.mult)
        return v.scalar_tensor_tensor(out=TAv[:, 7], in0=t3(tmp[:, K32:2 * K32]),
                                      scalar=1.0, in1=qs3(slice(64, 96)),
                                      op0=ALU.mult, op1=ALU.mult)
    vtb = ops.sig("vector", s_V, v_tables)
    ops.wait("gpsimd", s_V, vtb)

    def g_w(g):
        src = dram["wqkv"][l].rearrange("(n p) c -> p n c", p=128)
        g.dma_start(out=wqkv_sb[:].rearrange("p (n c) -> p n c", c=1536),
                    in_=src).then_inc(s_pg.h, 16)
        src2 = dram["wo"][l].rearrange("(n p) c -> p n c", p=128)
        g.dma_start(out=wo_sb[:].rearrange("p (n c) -> p n c", c=1024),
                    in_=src2).then_inc(s_pg.h, 16)
    s_pg.v += 32
    ops.on("gpsimd", g_w)
    for name, dst in (("ln_g", lgb[:, 0:NCH]), ("ln_b", lgb[:, NCH:2 * NCH]),
                      ("b2", b2p[:, 0:NCH])):
        ops.wait("sync", s_T, s_T.v)
        ops.sig("sync", s_pd,
                lambda e, name=name: e.dma_start(
                    out=cstg2[0:8, 0:128],
                    in_=dram[name][l].rearrange("(a b) -> a b", b=128)), n=16)
        ops.wait("tensor", s_pd, s_pd.v)
        ops.wait("tensor", s_V, s_V.v)
        ops.sig("tensor", s_T,
                lambda t: t.transpose(psD[:, 0:8], cstg2[0:8, 0:128],
                                      identf[0:8, 0:8]))
        ops.wait("vector", s_T, s_T.v)
        ops.sig("vector", s_V, lambda v, dst=dst: v.tensor_copy(dst, psD[:, 0:8]))
    s_gz = ops.sem(f"f_gz{l}")
    ops.wait("gpsimd", s_V, s_V.v)
    ops.sig("gpsimd", s_gz, lambda g: g.memset(cstg2[:], 0.0))
    ops.wait("sync", s_gz, s_gz.v)
    for part, coff in ((0, 0), (1, 24)):
        ops.wait("sync", s_T, s_T.v)
        ops.sig("sync", s_pd,
                lambda e, part=part: e.dma_start(
                    out=cstg2[0:21, 0:128],
                    in_=dram["b1"][l, part * 2730:part * 2730 + 2688]
                    .rearrange("(a b) -> a b", b=128)), n=16)
        ops.sig("sync", s_pd,
                lambda e, part=part: e.dma_start(
                    out=cstg2[21:22, 0:42],
                    in_=dram["b1"][l, part * 2730 + 2688:part * 2730 + 2730][None, :]),
                n=16)
        ops.wait("tensor", s_pd, s_pd.v)
        ops.wait("tensor", s_V, s_V.v)
        ops.sig("tensor", s_T,
                lambda t: t.transpose(psD[:, 0:22], cstg2[0:22, 0:128],
                                      identf[0:22, 0:22]))
        ops.wait("vector", s_T, s_T.v)
        vv = ops.sig("vector", s_V,
                     lambda v, coff=coff: v.tensor_copy(b1p[:, coff:coff + 22],
                                                        psD[:, 0:22]))
        ops.wait("sync", s_V, vv)
    ops.wait("tensor", s_pg, s_pg.v)
    ops.barrier(f"ffn_params_done{l}")


def emit_ffn_phaseA(c, dram, l, b):
    ops, nc = c.ops, c.nc
    xn_sb = c.t["xn_sb"]
    wqkv_sb, wo_sb, TA = c.t["wqkv_sb"], c.t["wo_sb"], c.t["TA"]
    krT, vring, qT = c.t["krT"], c.t["vring"], c.t["qT"]
    sqscr, attnb, attnT = c.t["sqscr"], c.t["attnb"], c.t["attnT"]
    qnb, ropet = c.t["qnb"], c.t["ropet"]
    obuf, lcbuf, fstat = c.t["obuf"], c.t["lcbuf"], c.t["fstat"]
    m_til, m_tiu, ident = c.t["m_til"], c.t["m_tiu"], c.t["ident"]
    psA, psB, psC, psD = c.t["psA"], c.t["psB"], c.t["psC"], c.t["psD"]
    psF, psG, psH, psI = c.t["psF"], c.t["psG"], c.t["psH"], c.t["psI"]
    s_V, s_A, s_T, s_G = (ops.sem("f_V"), ops.sem("f_A"), ops.sem("f_T"),
                          ops.sem("f_G"))
    val = {}
    TAv = TA[:].rearrange("p (k n c) -> p k n c", k=8, c=32)

    def hsl(dc, t0, n):
        o = ncol(b, dc, t0) - (1 if dc >= 4 else 0)
        return xn_sb[:, o:o + n]

    for i in range(CHK):
        ops.wait("tensor", s_V, val.get(("qkn", i - 1), 0))
        ops.wait("tensor", s_A, max(val.get(("vev", i - 1), 0),
                                    val.get(("oev", i - 1, 3), 0),
                                    val.get(("sev", i - 1, 3), 0)))

        def mm_qkv(t, i=i):
            last = None
            for part, ps in ((0, psA), (1, psB), (2, psC)):
                for dc in range(NCH):
                    last = t.matmul(
                        ps[:, 0:512], hsl(dc, i * 128, 128),
                        wqkv_sb[:, dc * 1536 + part * 512:dc * 1536 + (part + 1) * 512],
                        start=(dc == 0), stop=(dc == NCH - 1))
            return last
        val[("qkv", i)] = ops.sig("tensor", s_T, mm_qkv)
        if i % 4 == 0 and i > 0:
            ops.wait("vector", s_T, val[("qkv", i)])

            def v_lc(v, i=i):
                last = None
                for dc in range(NCH):
                    col = ncol(b, dc, i * 128 - 1)
                    last = v.tensor_tensor(out=xn_sb[:, col:col + 1],
                                           in0=xn_sb[:, col:col + 1],
                                           in1=lcbuf[:, dc:dc + 1], op=ALU.add)
                return last
            ops.sig("vector", s_V, v_lc)
        ops.wait("scalar", s_T, val[("qkv", i)])

        def a_sq(a, i=i):
            a.activation(out=sqscr[:, 0:512], in_=psA[:, 0:512], func=AF.Square)
            a.activation(out=sqscr[:, 512:1024], in_=psB[:, 0:512], func=AF.Square)
            return a.activation(out=vring[:, (i % 5) * 512:(i % 5) * 512 + 512],
                                in_=psC[:, 0:512], func=AF.Copy)
        val[("vev", i)] = ops.sig("scalar", s_A, a_sq)
        ops.wait("vector", s_A, val[("vev", i)])

        def v_l2(v):
            v.tensor_reduce(out=fstat[:, 0:8, None],
                            in_=sqscr[:, 0:512].rearrange("p (n c) -> p n c", c=64),
                            axis=mybir.AxisListType.X, op=ALU.add)
            return v.tensor_reduce(
                out=fstat[:, 8:16, None],
                in_=sqscr[:, 512:1024].rearrange("p (n c) -> p n c", c=64),
                axis=mybir.AxisListType.X, op=ALU.add)
        vl2 = ops.sig("vector", s_V, v_l2)
        ops.wait("scalar", s_V, vl2)
        va_n = ops.sig("scalar", s_A,
                       lambda a: a.sqrt(fstat[:, 0:16], fstat[:, 0:16]))
        ops.wait("vector", s_A, va_n)

        def v_rstd(v):
            v.tensor_scalar(out=fstat[:, 0:16], in0=fstat[:, 0:16],
                            scalar1=1e-12, scalar2=None, op0=ALU.max)
            v.drain()
            v.reciprocal(fstat[:, 0:16], fstat[:, 0:16])
            v.drain()
            v.tensor_tensor(
                out=qnb[:, 0:512].rearrange("p (n c) -> p n c", c=64),
                in0=psA[:, 0:512].rearrange("p (n c) -> p n c", c=64),
                in1=fstat[:, 0:8, None].broadcast_to([128, 8, 64]), op=ALU.mult)
            return v.tensor_tensor(
                out=qnb[:, 512:1024].rearrange("p (n c) -> p n c", c=64),
                in0=psB[:, 0:512].rearrange("p (n c) -> p n c", c=64),
                in1=fstat[:, 8:16, None].broadcast_to([128, 8, 64]), op=ALU.mult)
        val[("qkn", i)] = ops.sig("vector", s_V, v_rstd)
        def v_rope(v, i=i):
            qn3 = qnb[:, 0:512].rearrange("p (n c) -> p n c", c=64)
            kn3 = qnb[:, 512:1024].rearrange("p (n c) -> p n c", c=64)
            t13 = ropet[:, 0:512].rearrange("p (n c) -> p n c", c=64)
            t23 = ropet[:, 512:1024].rearrange("p (n c) -> p n c", c=64)
            last = None
            for (src3, tA, tB, dst) in ((qn3, 0, 2, sqscr[:, 0:512]),
                                        (kn3, 4, 6, sqscr[:, 512:1024])):
                def ts(k):
                    return TAv[:, k, i, None, :].broadcast_to([128, 8, 32])
                v.drain()
                v.tensor_tensor(out=t13[:, :, 0:32], in0=src3[:, :, 0:32],
                                in1=ts(tA), op=ALU.mult)
                v.tensor_tensor(out=t13[:, :, 32:64], in0=src3[:, :, 32:64],
                                in1=ts(tA + 1), op=ALU.mult)
                v.tensor_tensor(out=t23[:, :, 0:32], in0=src3[:, :, 32:64],
                                in1=ts(tB), op=ALU.mult)
                v.tensor_tensor(out=t23[:, :, 32:64], in0=src3[:, :, 0:32],
                                in1=ts(tB + 1), op=ALU.mult)
                v.drain()
                last = v.tensor_tensor(out=dst, in0=ropet[:, 0:512],
                                       in1=ropet[:, 512:1024], op=ALU.add)
            return last
        val[("rope", i)] = ops.sig("vector", s_V, v_rope)
        ops.wait("tensor", s_V, val[("rope", i)])
        ops.wait("tensor", s_A, val.get(("atev", i - 1, 3), 0))

        def mm_tr(t):
            last = None
            for j in range(8):
                last = t.transpose(psG[:, j * 128:j * 128 + 128],
                                   sqscr[:, j * 128:j * 128 + 128], ident[:])
            return last
        vtr = ops.sig("tensor", s_T, mm_tr)
        ops.wait("scalar", s_T, vtr)

        def a_qkT(a, i=i):
            a.activation(out=qT[:], in_=psG[:, 0:512], func=AF.Copy)
            return a.activation(out=krT[:, (i % 5) * 512:(i % 5) * 512 + 512],
                                in_=psG[:, 512:1024], func=AF.Copy)
        val[("qkT", i)] = ops.sig("scalar", s_A, a_qkT)

        nk = min(i, 4) + 1
        j0 = i - (nk - 1)
        for p in range(4):
            prev_sev = (val.get(("sev", i, p - 1), 0) if p >= 1
                        else val.get(("sev", i - 1, 3), 0))
            ops.wait("tensor", s_A, max(val[("qkT", i)], prev_sev))
            if i % 4 == 0 and i > 0:
                ops.wait("tensor", s_V, val.get(("woev", i // 4 - 1), 0))

            def mm_sc2(t, i=i, p=p, nk=nk, j0=j0):
                last = None
                for par in range(2):
                    main = psD if par == 0 else psH
                    diag = psA if par == 0 else psB
                    pslo = par * 64
                    for jj in range(nk):
                        sl = ((j0 + jj) % 5) * 512 + p * 128
                        out = (main[:, jj * 128:jj * 128 + 128] if jj < nk - 1
                               else diag[:, 0:128])
                        last = t.matmul(
                            out, qT[pslo:pslo + 64, p * 128:p * 128 + 128],
                            krT[pslo:pslo + 64, sl:sl + 128],
                            start=True, stop=True)
                return last
            val[("sc", i, p)] = ops.sig("tensor", s_T, mm_sc2)
            ops.wait("vector", s_T, val[("sc", i, p)])

            def v_mx2(v, nk=nk):
                last = None
                for par in range(2):
                    main = psD if par == 0 else psH
                    diag = psA if par == 0 else psB
                    if nk == 1:
                        last = v.tensor_reduce(
                            out=fstat[:, 16 + par:17 + par], in_=diag[:, 0:128],
                            axis=mybir.AxisListType.X, op=ALU.max, negate=True)
                        continue
                    v.tensor_reduce(out=fstat[:, 16 + par:17 + par],
                                    in_=main[:, 0:(nk - 1) * 128],
                                    axis=mybir.AxisListType.X,
                                    op=ALU.max, negate=True)
                    v.tensor_reduce(out=fstat[:, 20 + par:21 + par],
                                    in_=diag[:, 0:128], axis=mybir.AxisListType.X,
                                    op=ALU.max, negate=True)
                if nk > 1:
                    v.drain()
                    for par in range(2):
                        last = v.tensor_tensor(
                            out=fstat[:, 16 + par:17 + par],
                            in0=fstat[:, 16 + par:17 + par],
                            in1=fstat[:, 20 + par:21 + par], op=ALU.min)
                return last
            vmx = ops.sig("vector", s_V, v_mx2)
            ops.wait("scalar", s_V, vmx)

            def a_vex2(a, nk=nk):
                last = None
                for par in range(2):
                    main = psD if par == 0 else psH
                    diag = psA if par == 0 else psB
                    ab = attnb[:, par * 640:par * 640 + 640]
                    if nk > 1:
                        a.activation(out=ab[:, 0:(nk - 1) * 128],
                                     in_=main[:, 0:(nk - 1) * 128], func=AF.Exp,
                                     bias=fstat[:, 16 + par:17 + par])
                    last = a.activation(out=ab[:, (nk - 1) * 128:nk * 128],
                                        in_=diag[:, 0:128], func=AF.Exp,
                                        bias=fstat[:, 16 + par:17 + par])
                return last
            vex = ops.sig("scalar", s_A, a_vex2)
            val[("sev", i, p)] = vex
            ops.wait("vector", s_A, vex)

            def v_sm2(v, nk=nk):
                v.drain()
                for par in range(2):
                    ab = attnb[:, par * 640:par * 640 + 640]
                    v.tensor_tensor(out=ab[:, (nk - 1) * 128:nk * 128],
                                    in0=ab[:, (nk - 1) * 128:nk * 128],
                                    in1=m_til[:], op=ALU.mult)
                    if nk == 5:
                        v.tensor_tensor(out=ab[:, 0:128], in0=ab[:, 0:128],
                                        in1=m_tiu[:], op=ALU.mult)
                v.drain()
                for par in range(2):
                    ab = attnb[:, par * 640:par * 640 + 640]
                    v.tensor_reduce(out=fstat[:, 18 + par:19 + par],
                                    in_=ab[:, 0:nk * 128],
                                    axis=mybir.AxisListType.X, op=ALU.add)
                v.drain()
                v.reciprocal(fstat[:, 18:20], fstat[:, 18:20])
                v.drain()
                last = None
                for par in range(2):
                    ab = attnb[:, par * 640:par * 640 + 640]
                    last = v.tensor_scalar(out=ab[:, 0:nk * 128],
                                           in0=ab[:, 0:nk * 128],
                                           scalar1=fstat[:, 18 + par:19 + par],
                                           scalar2=None, op0=ALU.mult)
                return last
            vsm = ops.sig("vector", s_V, v_sm2)
            ops.wait("tensor", s_V, vsm)
            prev_at = (val.get(("atev", i, p - 1), 0) if p >= 1
                       else val.get(("atev", i - 1, 3), 0))
            if p == 0:
                prev_at = max(prev_at, val[("qkT", i)])
            ops.wait("tensor", s_A, prev_at)

            def mm_at2(t, nk=nk):
                last = None
                for par in range(2):
                    pst = psG if par == 0 else psI
                    ab = attnb[:, par * 640:par * 640 + 640]
                    for jj in range(nk):
                        last = t.transpose(pst[:, jj * 128:jj * 128 + 128],
                                           ab[:, jj * 128:jj * 128 + 128],
                                           ident[:])
                return last
            vat = ops.sig("tensor", s_T, mm_at2)
            ops.wait("scalar", s_T, vat)

            def a_atev2(a, nk=nk):
                last = None
                for par in range(2):
                    pst = psG if par == 0 else psI
                    at_ = attnT[:, par * 640:par * 640 + 640]
                    last = a.activation(out=at_[:, 0:nk * 128],
                                        in_=pst[:, 0:nk * 128], func=AF.Copy)
                return last
            val[("atev", i, p)] = ops.sig("scalar", s_A, a_atev2)
            prev_oev = (val.get(("oev", i, p - 1), 0) if p >= 1
                        else val.get(("oev", i - 1, 3), 0))
            ops.wait("tensor", s_A, max(val[("atev", i, p)], prev_oev))

            def mm_av2(t, p=p, nk=nk, j0=j0):
                last = None
                for par in range(2):
                    h = 2 * p + par
                    at_ = attnT[:, par * 640:par * 640 + 640]
                    avb = psF if par == 0 else psC
                    for jj in range(nk):
                        sl = ((j0 + jj) % 5) * 512 + h * 64
                        if par == 0:
                            last = t.matmul(avb[0:64, 256:384],
                                            vring[:, sl:sl + 64],
                                            at_[:, jj * 128:jj * 128 + 128],
                                            start=(jj == 0), stop=(jj == nk - 1))
                        else:
                            last = t.matmul(avb[64:128, 384:512],
                                            vring[:, sl:sl + 64],
                                            at_[:, jj * 128:jj * 128 + 128],
                                            start=(jj == 0), stop=(jj == nk - 1),
                                            tile_position=(0, 64))
                return last
            vav = ops.sig("tensor", s_T, mm_av2)
            ops.wait("scalar", s_T, vav)
            if i % 4 == 0 and i > 0:
                ops.wait("scalar", s_T, val.get(("woMM", i // 4 - 1), 0))

            def a_oev2(a, i=i, p=p):
                last = None
                for par in range(2):
                    pslo = par * 64
                    osl = obuf[pslo:pslo + 64,
                               p * 512 + (i % 4) * 128:
                               p * 512 + (i % 4) * 128 + 128]
                    src = (psF[0:64, 256:384] if par == 0
                           else psC[64:128, 384:512])
                    last = a.activation(out=osl, in_=src, func=AF.Copy)
                return last
            val[("oev", i, p)] = ops.sig("scalar", s_A, a_oev2)
        if i % 4 == 3:
            blk = i // 4
            ops.wait("tensor", s_A, val[("oev", i, 3)])
            lastwo = 0
            for dc in range(NCH):
                bank = psD if dc % 2 == 0 else psH

                def mm_wo(t, dc=dc, bank=bank):
                    last = None
                    for pc in range(4):
                        last = t.matmul(
                            bank[:, 0:512],
                            wo_sb[:, pc * 1024 + dc * 128:pc * 1024 + dc * 128 + 128],
                            obuf[:, pc * 512:pc * 512 + 512],
                            start=(pc == 0), stop=(pc == 3))
                    return last
                if dc >= 2:
                    ops.wait("tensor", s_V, lastwo)
                vwo = ops.sig("tensor", s_T, mm_wo)
                val[("woMM", blk)] = vwo
                ops.wait("vector", s_T, vwo)

                def v_woev(v, dc=dc, bank=bank, blk=blk):
                    col = ncol(b, dc, blk * 512)
                    v.scalar_tensor_tensor(out=xn_sb[:, col:col + 511],
                                           in0=bank[:, 0:511], scalar=1.0,
                                           in1=xn_sb[:, col:col + 511],
                                           op0=ALU.mult, op1=ALU.add)
                    return v.tensor_copy(lcbuf[:, dc:dc + 1], bank[:, 511:512])
                lastwo = ops.sig("vector", s_V, v_woev)
            val[("woev", blk)] = lastwo
    ops.wait("vector", s_V, s_V.v)

    def v_lcf(v):
        last = None
        for dc in range(NCH):
            col = ncol(b, dc, T - 1)
            last = v.tensor_tensor(out=xn_sb[:, col:col + 1],
                                   in0=xn_sb[:, col:col + 1],
                                   in1=lcbuf[:, dc:dc + 1], op=ALU.add)
        return last
    ops.sig("vector", s_V, v_lcf)
    ops.barrier(f"ffnA_{l}_{b}")


def emit_ffn_phaseB(c, dram, l, b):
    ops, nc = c.ops, c.nc
    xn_sb = c.t["xn_sb"]
    hid, lnout, w1blk, w2blk, gelt = (c.t["hid"], c.t["lnout"], c.t["w1blk"],
                                      c.t["w2blk"], c.t["gelt"])
    lgb, b1p, b2p, lcbuf = c.t["lgb"], c.t["b1p"], c.t["b2p"], c.t["lcbuf"]
    ones_col, ones_row = c.t["ones_col"], c.t["ones_row"]
    stats = c.t["ln_stats"]
    psA, psB, psC, psD = c.t["psA"], c.t["psB"], c.t["psC"], c.t["psD"]
    psF, psH = c.t["psF"], c.t["psH"]
    s_V, s_A, s_T, s_G = (ops.sem("f_V"), ops.sem("f_A"), ops.sem("f_T"),
                          ops.sem("f_G"))
    s_w = [ops.sem("f_w0"), ops.sem("f_w1"), ops.sem("f_w2")]
    s_u = [ops.sem("f_u0"), ops.sem("f_u1")]
    val = {}
    w1g = {}
    w2slot = {0: 0, 1: 0}

    def hsl(dc, t0, n):
        o = ncol(b, dc, t0) - (1 if dc >= 4 else 0)
        return xn_sb[:, o:o + n]

    for tau in range(T // 512):
        t0 = tau * 512
        ops.wait("scalar", s_T, val.get(("w2last", tau - 1), 0))
        ops.wait("scalar", s_V, s_V.v)

        def a_sq(a, t0=t0):
            last = None
            for dc in range(NCH):
                last = a.activation(out=lnout[:, dc * 512:dc * 512 + 512],
                                    in_=hsl(dc, t0, 512), func=AF.Square)
            return last
        vsq = ops.sig("scalar", s_A, a_sq)
        ops.wait("tensor", s_A, vsq)
        ops.wait("tensor", s_V, s_V.v)

        def mm_s(t, t0=t0):
            last = None
            for dc in range(NCH):
                t.matmul(psA[0:1, 0:512], ones_col[:], hsl(dc, t0, 512),
                         start=(dc == 0), stop=(dc == NCH - 1))
                last = t.matmul(psB[0:1, 0:512], ones_col[:],
                                lnout[:, dc * 512:dc * 512 + 512],
                                start=(dc == 0), stop=(dc == NCH - 1))
            return last
        vms = ops.sig("tensor", s_T, mm_s)
        if tau >= 1:
            ops.wait("vector", s_T, vms)

            def v_lc(v, tau=tau):
                last = None
                for dc in range(NCH):
                    col = ncol(b, dc, tau * 512 - 1)
                    last = v.tensor_tensor(out=xn_sb[:, col:col + 1],
                                           in0=xn_sb[:, col:col + 1],
                                           in1=lcbuf[:, dc:dc + 1], op=ALU.add)
                return last
            ops.sig("vector", s_V, v_lc)
        ops.wait("vector", s_T, vms)

        def v_st(v):
            v.drain()
            v.tensor_scalar(out=stats[0:1, 0:512], in0=psA[0:1, 0:512],
                            scalar1=-1.0 / D, scalar2=None, op0=ALU.mult)
            v.drain()
            v.tensor_tensor(out=stats[0:1, 512:1024], in0=stats[0:1, 0:512],
                            in1=stats[0:1, 0:512], op=ALU.mult)
            v.drain()
            v.scalar_tensor_tensor(out=stats[0:1, 512:1024], in0=psB[0:1, 0:512],
                                   scalar=1.0 / D, in1=stats[0:1, 512:1024],
                                   op0=ALU.mult, op1=ALU.subtract)
            v.drain()
            v.tensor_scalar(out=stats[0:1, 512:1024], in0=stats[0:1, 512:1024],
                            scalar1=float(LN_EPS), scalar2=None, op0=ALU.add)
            v.drain()
            with nc.allow_low_precision(reason="bf16 LN stats"):
                return v.reciprocal(stats[0:1, 512:1024], stats[0:1, 512:1024])
        vst = ops.sig("vector", s_V, v_st)
        ops.wait("scalar", s_V, vst)
        def a_bsqrt(a):
            with nc.allow_low_precision(reason="bf16 LN stats"):
                return a.sqrt(stats[0:1, 512:1024], stats[0:1, 512:1024])
        vsr = ops.sig("scalar", s_A, a_bsqrt)
        ops.wait("tensor", s_A, vsr)

        def mm_bc(t):
            t.matmul(psC[:, 0:512], ones_row[:], stats[0:1, 0:512],
                     start=True, stop=True)
            return t.matmul(psF[:, 0:512], ones_row[:], stats[0:1, 512:1024],
                            start=True, stop=True)
        vbc = ops.sig("tensor", s_T, mm_bc)
        ops.wait("vector", s_T, vbc)

        def v_ln(v, t0=t0):
            last = None
            for dc in range(NCH):
                sl = lnout[:, dc * 512:dc * 512 + 512]
                v.drain()
                v.tensor_tensor(out=sl, in0=hsl(dc, t0, 512), in1=psC[:, 0:512],
                                op=ALU.add)
                v.drain()
                v.tensor_tensor(out=sl, in0=sl, in1=psF[:, 0:512], op=ALU.mult)
                v.drain()
                last = v.tensor_scalar(out=sl, in0=sl, scalar1=lgb[:, dc:dc + 1],
                                       scalar2=lgb[:, NCH + dc:NCH + dc + 1],
                                       op0=ALU.mult, op1=ALU.add)
            return last
        val[("ln", tau)] = ops.sig("vector", s_V, v_ln)

        PASSES = ((0, 8), (8, 15), (15, 22))
        for half, (hlo, hhi) in enumerate(PASSES):
            npc = hhi - hlo
            for k in range(npc):
                hc = hlo + k
                u = 2 * hc
                par1 = hc % 2
                bank_a = psA if par1 == 0 else psC
                bank_g = psB if par1 == 0 else psD
                gsl = gelt[:, par1 * 512:par1 * 512 + 512]
                for part in range(2):
                    slot = (u + part) % 3
                    sw = s_w[slot]
                    gu = tau * 44 + u + part
                    ops.wait("gpsimd", s_T, w1g.get(gu - 3, 0))
                    ops.sig("gpsimd", sw,
                            lambda g, l=l, u2=u + part, slot=slot: g.dma_start(
                                out=w1blk[:, slot * 1024:(slot + 1) * 1024],
                                in_=dram["s_w1"][l, u2]), n=16)
                    ops.wait("tensor", sw, sw.v)
                    ops.wait("tensor", s_V, val[("ln", tau)])
                    bank = bank_a if part == 0 else bank_g
                    if hc >= 2:
                        ops.wait("tensor", s_V, val.get(("gg", tau, hc - 2), 0))
                        ops.wait("tensor", s_A, val.get(("gl", tau, hc - 2), 0))
                    if k <= 1 and par1 == 1:
                        ops.wait("tensor", s_V,
                                 val.get(("w2ev", tau, half - 1, 1),
                                         val.get(("w2ev", tau - 1, 2, 1), 0)))

                    def mm_w1(t, slot=slot, bank=bank):
                        last = None
                        d3 = w1blk[:, slot * 1024:(slot + 1) * 1024].rearrange(
                            "p (n c) -> p n c", c=128)
                        for dc in range(NCH):
                            last = t.matmul(bank[:, 0:512], d3[:, dc],
                                            lnout[:, dc * 512:dc * 512 + 512],
                                            start=(dc == 0), stop=(dc == NCH - 1))
                        return last
                    val[("w1mm", tau, u + part)] = ops.sig("tensor", s_T, mm_w1)
                    w1g[gu] = val[("w1mm", tau, u + part)]
                ops.wait("scalar", s_T, val[("w1mm", tau, u + 1)])
                val[("gl", tau, hc)] = ops.sig(
                    "scalar", s_A,
                    lambda a, hc=hc, bank_g=bank_g, gsl=gsl: a.activation(
                        out=gsl, in_=bank_g[:, 0:512], func=AF.Gelu,
                        bias=b1p[:, 24 + hc:25 + hc]))
                ops.wait("vector", s_A, val[("gl", tau, hc)])
                val[("gg", tau, hc)] = ops.sig(
                    "vector", s_V,
                    lambda v, k=k, hc=hc, bank_a=bank_a, gsl=gsl:
                    v.scalar_tensor_tensor(
                        out=hid[:, k * 512:(k + 1) * 512], in0=bank_a[:, 0:512],
                        scalar=b1p[:, hc:hc + 1], in1=gsl,
                        op0=ALU.add, op1=ALU.mult))
            lastodd = max(h2 for h2 in range(hlo, hhi) if h2 % 2 == 1)
            for dgrp in range(2):
                for k in range(npc):
                    hc = hlo + k
                    rsz = min(128, 2730 - hc * 128)
                    slot = k % 2
                    su = s_u[slot]
                    if w2slot[slot]:
                        ops.wait("gpsimd", s_T, w2slot[slot])
                    if rsz < 128:
                        ops.sig("gpsimd", s_G,
                                lambda g, slot=slot: g.memset(
                                    w2blk[:, slot * 512:slot * 512 + 512], 0.0))
                        ops.wait("gpsimd", s_G, s_G.v)

                    def g_w2(g, hc=hc, dgrp=dgrp, slot=slot, rsz=rsz, su=su):
                        src = dram["w2"][l, hc * 128:hc * 128 + rsz,
                                         dgrp * 512:(dgrp + 1) * 512]
                        dst = w2blk[0:rsz, slot * 512:slot * 512 + 512]
                        g.dma_start(out=dst, in_=src).then_inc(su.h, 16)
                    ops.on("gpsimd", g_w2)
                    su.v += 16
                    ops.wait("tensor", su, su.v)
                    ops.wait("tensor", s_V, val[("gg", tau, hc)])
                    if dgrp == 0 and k == 0:
                        ops.wait("tensor", s_V, val[("gg", tau, lastodd)])
                        ops.wait("tensor", s_A, val[("gl", tau, lastodd)])
                    if k == 0:
                        ops.wait("tensor", s_V,
                                 val.get(("w2ev", tau, half, dgrp - 1),
                                         val.get(("w2ev", tau, half - 1, 1), 0)))

                    def mm_w2(t, k=k, npc=npc, slot=slot):
                        last = None
                        d3 = w2blk[:, slot * 512:slot * 512 + 512].rearrange(
                            "p (n c) -> p n c", c=128)
                        banks = (psC, psD, psF, psH)
                        for j, bank in enumerate(banks):
                            last = t.matmul(bank[:, 0:512], d3[:, j],
                                            hid[:, k * 512:(k + 1) * 512],
                                            start=(k == 0), stop=(k == npc - 1))
                        return last
                    val[("w2mm", tau, half, dgrp, k)] = ops.sig("tensor", s_T, mm_w2)
                    w2slot[slot] = val[("w2mm", tau, half, dgrp, k)]
                val[("w2last", tau)] = val[("w2mm", tau, half, dgrp, npc - 1)]
                ops.wait("vector", s_T, val[("w2mm", tau, half, dgrp, npc - 1)])

                def v_w2ev(v, tau=tau, half=half, dgrp=dgrp):
                    banks = (psC, psD, psF, psH)
                    last = None
                    for j, bank in enumerate(banks):
                        dc = dgrp * 4 + j
                        col = ncol(b, dc, tau * 512)
                        if half == 0:
                            v.scalar_tensor_tensor(out=xn_sb[:, col:col + 511],
                                                   in0=bank[:, 0:511],
                                                   scalar=b2p[:, dc:dc + 1],
                                                   in1=xn_sb[:, col:col + 511],
                                                   op0=ALU.add, op1=ALU.add)
                            last = v.tensor_scalar(out=lcbuf[:, dc:dc + 1],
                                                   in0=bank[:, 511:512],
                                                   scalar1=b2p[:, dc:dc + 1],
                                                   scalar2=None, op0=ALU.add)
                        else:
                            v.scalar_tensor_tensor(out=xn_sb[:, col:col + 511],
                                                   in0=bank[:, 0:511], scalar=1.0,
                                                   in1=xn_sb[:, col:col + 511],
                                                   op0=ALU.mult, op1=ALU.add)
                            last = v.scalar_tensor_tensor(
                                out=lcbuf[:, dc:dc + 1], in0=bank[:, 511:512],
                                scalar=1.0, in1=lcbuf[:, dc:dc + 1],
                                op0=ALU.mult, op1=ALU.add)
                    return last
                val[("w2ev", tau, half, dgrp)] = ops.sig("vector", s_V, v_w2ev)
    ops.wait("vector", s_V, s_V.v)

    def v_lcf(v):
        last = None
        for dc in range(NCH):
            col = ncol(b, dc, T - 1)
            last = v.tensor_tensor(out=xn_sb[:, col:col + 1],
                                   in0=xn_sb[:, col:col + 1],
                                   in1=lcbuf[:, dc:dc + 1], op=ALU.add)
        return last
    ops.sig("vector", s_V, v_lcf)
    ops.barrier(f"ffnB_{l}_{b}")


def emit_l1(c, dram):
    nc, ops = c.nc, c.ops
    x_sb, xn_sb = c.t["x_sb"], c.t["xn_sb"]
    ident, identf = c.t["ident"], c.t["identf"]
    m_tiu, m_upp = c.t["m_tiu"], c.t["m_upp"]
    psA, psB, psC, psD = c.t["psA"], c.t["psB"], c.t["psC"], c.t["psD"]
    psF, psG, psH, psI = c.t["psF"], c.t["psG"], c.t["psH"], c.t["psI"]
    s_T, s_A, s_V, s_G = (ops.sem("l1_T"), ops.sem("l1_A"), ops.sem("l1_V"),
                          ops.sem("l1_G"))

    base = c.addr_of("WT0")
    end = c.addr_of("S0") + 2048
    cur = [base]

    def arena(name, shape, dtype):
        import concourse.mybir as mb
        per = 1
        for d in shape[1:]:
            per *= d
        per *= mb.dt.size(dtype)
        off = (cur[0] + 31) // 32 * 32
        assert off + per <= end, f"arena2 overflow {name}: {off+per-base} > {end-base}"
        cur[0] = off + per
        h = nc.alloc_sbuf_tensor_at(name, shape, dtype, offset=off)
        c.t[name] = h
        return h

    wq_g = arena("wq_g", [128, 8 * 512], BF16)
    wk_g = arena("wk_g", [128, 8 * 512], BF16)
    wv_g = arena("wv_g", [128, 8 * 512], BF16)
    wo_g = arena("wo_g", [128, 4 * 1024], BF16)
    w8T = arena("w8T", [128, 8 * 1024], F8)
    bias8 = arena("bias8", [128, 8], F32)
    stSK = arena("stSK", [64, 8 * 64], BF16)
    stCQ = arena("stCQ", [64, 8 * 64], BF16)
    stG = arena("stG", [64, 8 * 64], BF16)

    krb = c.addr_of("krT")
    lnx = c.addr_of("ln_xsq")

    def at(name, shape, dtype, addr):
        h = nc.alloc_sbuf_tensor_at(name, shape, dtype, offset=addr)
        c.t[name] = h
        return h
    qg = at("qg", [128, 512], BF16, krb + 0)
    kg = at("kg", [128, 512], BF16, krb + 1024)
    vg = at("vg", [128, 512], BF16, krb + 2048)
    qTs = at("qTs", [64, 1024], BF16, krb + 3072)
    kTs = at("kTs", [64, 1024], BF16, krb + 5120)
    oTs = at("oTs", [128, 512], BF16, krb + 7168)
    Asb = at("Asb", [128, 256], BF16, lnx + 0)
    Atsb = at("Atsb", [128, 256], BF16, lnx + 512)
    Btsb = at("Btsb", [128, 256], BF16, lnx + 1024)
    Awmsb = at("Awmsb", [128, 256], BF16, lnx + 1536)
    ZTsb = at("ZTsb", [128, 256], BF16, lnx + 2048)
    Wmsb = at("Wmsb", [128, 128], BF16, lnx + 2560)
    Hmsb = at("Hmsb", [64, 256], BF16, lnx + 2816)
    D0sb = at("D0sb", [64, 128], BF16, lnx + 3328)
    KKsb = at("KKsb", [64, 128], BF16, lnx + 3584)
    osb = at("osb", [128, 128], BF16, lnx + 3840)
    xn8 = at("xn8", [128, 1024], F8, lnx + 4096)
    Esb = at("Esb", [128, 256], BF16, lnx + 5120)
    wstg = c.t["xstg"]

    s_lw = [ops.sem("l1_lw0"), ops.sem("l1_lw1")]

    def queue_group_weights(grp):
        sw = s_lw[grp]

        def g_w(g, grp=grp, sw=sw):
            for wbuf, nm in ((wq_g, "Wq1"), (wk_g, "Wk1"), (wv_g, "Wv1")):
                src = dram[nm][:, grp * 512:(grp + 1) * 512].rearrange(
                    "(n p) j -> p n j", p=128)
                g.dma_start(out=wbuf[:].rearrange("p (n j) -> p n j", j=512),
                            in_=src).then_inc(sw.h, 16)
            src = dram["Wo1"][grp * 512:(grp + 1) * 512, :].rearrange(
                "(n p) d -> p n d", p=128)
            g.dma_start(out=wo_g[:].rearrange("p (n d) -> p n d", d=1024),
                        in_=src).then_inc(sw.h, 16)
        sw.v += 64
        ops.on("gpsimd", g_w)
    queue_group_weights(0)

    ops.barrier("l1_start")

    s_ws = [ops.sem("l1_ws0"), ops.sem("l1_ws1")]
    s_bd = ops.sem("l1_bd")
    ops.sig("sync", s_bd,
            lambda e: e.dma_start(out=wstg[0:8, 0:128],
                                  in_=dram["bf1"].rearrange("(a b) -> a b", b=128)),
            n=16)
    ops.wait("tensor", s_bd, s_bd.v)
    vb = ops.sig("tensor", s_T,
                 lambda t: t.transpose(psD[:, 0:8], wstg[0:8, 0:128],
                                       identf[0:8, 0:8]))
    ops.wait("vector", s_T, vb)
    vbe = ops.sig("vector", s_V, lambda v: v.tensor_copy(bias8[:], psD[:, 0:8]))
    ops.wait("sync", s_V, vbe)
    w8tr = {}
    w8cast = {}
    for jb in range(8):
        slot = jb % 2
        if jb >= 2:
            ops.wait("sync", s_T, w8tr[jb - 2])
        ops.sig("sync", s_ws[slot],
                lambda e, jb=jb, slot=slot: e.dma_start(
                    out=wstg[:, slot * 1024:(slot + 1) * 1024],
                    in_=dram["Wf1"][jb * 128:(jb + 1) * 128, :]), n=16)
        ops.wait("tensor", s_ws[slot], s_ws[slot].v)
        if jb >= 1:
            ops.wait("tensor", s_V, w8cast[jb - 1])

        def mm_w8(t, slot=slot):
            last = None
            for ci in range(8):
                bank = psA if ci < 4 else psB
                last = t.transpose(bank[:, (ci % 4) * 128:(ci % 4) * 128 + 128],
                                   wstg[:, slot * 1024 + ci * 128:
                                        slot * 1024 + ci * 128 + 128], identf[:])
            return last
        w8tr[jb] = ops.sig("tensor", s_T, mm_w8)
        ops.wait("vector", s_T, w8tr[jb])

        def v_w8(v, jb=jb):
            last = None
            for ci in range(8):
                bank = psA if ci < 4 else psB
                last = v.tensor_scalar(
                    out=w8T[:, ci * 1024 + jb * 128:ci * 1024 + jb * 128 + 128],
                    in0=bank[:, (ci % 4) * 128:(ci % 4) * 128 + 128],
                    scalar1=448.0, scalar2=-448.0, op0=ALU.min, op1=ALU.max)
            return last
        w8cast[jb] = ops.sig("vector", s_V, v_w8)
    ops.wait("tensor", s_V, w8cast[7])
    ops.barrier("l1_wb")

    for grp in range(min(2, L1_NGRP) if L1_LEVEL >= 1 else 0):
        if grp == 1:
            ops.barrier("l1_g1")
            queue_group_weights(1)
        ops.wait("tensor", s_lw[grp], s_lw[grp].v)
        ops.wait("scalar", s_lw[grp], s_lw[grp].v)
        for b in range(min(NB, L1_NB)):
            zv = s_G.v
            for st_ in (stSK, stCQ, stG):
                ops.sig("gpsimd", s_G,
                        lambda g, st_=st_: g.memset(st_[:], 0.0))
            ops.wait("tensor", s_G, s_G.v)
            ops.wait("vector", s_G, s_G.v)
            pv = {}
            for cc in range(min(CHK, L1_NCC)):
                emit_l1_chunk(c, grp, b, cc, pv)
            ops.barrier(f"l1_{grp}_{b}")

    f8p = {}
    for b in range(NB if L1_LEVEL >= 2 else 0):
        for cc in range(CHK):
            idx = b * CHK + cc
            ops.wait("vector", s_T, f8p.get(("mm", idx - 1), 0))

            def v_c8(v, b=b, cc=cc):
                xin3 = xn_sb[:].rearrange("p (n t) -> p n t", t=TP)[
                    :, b * NCH:(b + 1) * NCH,
                    PAD + cc * 128:PAD + cc * 128 + 128]
                return v.tensor_scalar(
                    out=xn8[:].rearrange("p (n t) -> p n t", t=128),
                    in0=xin3, scalar1=448.0, scalar2=-448.0,
                    op0=ALU.min, op1=ALU.max)
            f8p[("c8", idx)] = ops.sig("vector", s_V, v_c8)
            ops.wait("tensor", s_V, f8p[("c8", idx)])
            if idx >= 1:
                ops.wait("tensor", s_V, f8p[("ev", idx - 1)])

            def mm_f8(t):
                last = None
                for jb in range(8):
                    bank, col = (psD, jb * 128) if jb < 4 else (psH, (jb - 4) * 128)
                    for ci in range(8):
                        last = t.matmul(
                            bank[:, col:col + 128],
                            w8T[:, ci * 1024 + jb * 128:ci * 1024 + jb * 128 + 128],
                            xn8[:, ci * 128:(ci + 1) * 128],
                            start=(ci == 0), stop=(ci == 7))
                return last
            f8p[("mm", idx)] = ops.sig("tensor", s_T, mm_f8)
            ops.wait("vector", s_T, f8p[("mm", idx)])

            def v_f8ev(v, b=b, cc=cc):
                last = None
                for jb in range(8):
                    bank, col = (psD, jb * 128) if jb < 4 else (psH, (jb - 4) * 128)
                    xs = x_sb[:, xcol(b, jb, cc * 128):xcol(b, jb, cc * 128) + 128]
                    last = v.scalar_tensor_tensor(
                        out=xs, in0=bank[:, col:col + 128],
                        scalar=bias8[:, jb:jb + 1], in1=xs,
                        op0=ALU.add, op1=ALU.add)
                return last
            f8p[("ev", idx)] = ops.sig("vector", s_V, v_f8ev)
    ops.barrier("l1_done")


def emit_l1_chunk(c, grp, b, cc, pv):
    nc, ops = c.nc, c.ops
    xn_sb, x_sb = c.t["xn_sb"], c.t["x_sb"]
    ident = c.t["ident"]
    m_tiu, m_upp = c.t["m_tiu"], c.t["m_upp"]
    psA, psB, psC, psD = c.t["psA"], c.t["psB"], c.t["psC"], c.t["psD"]
    psF, psG, psH, psI = c.t["psF"], c.t["psG"], c.t["psH"], c.t["psI"]
    wq_g, wk_g, wv_g, wo_g = c.t["wq_g"], c.t["wk_g"], c.t["wv_g"], c.t["wo_g"]
    qg, kg, vg = c.t["qg"], c.t["kg"], c.t["vg"]
    qTs, kTs, oTs = c.t["qTs"], c.t["kTs"], c.t["oTs"]
    Asb, Atsb, Btsb = c.t["Asb"], c.t["Atsb"], c.t["Btsb"]
    Awmsb, ZTsb, Wmsb = c.t["Awmsb"], c.t["ZTsb"], c.t["Wmsb"]
    Hmsb, D0sb, KKsb, osb = c.t["Hmsb"], c.t["D0sb"], c.t["KKsb"], c.t["osb"]
    Esb = c.t["Esb"]
    stSK, stCQ, stG = c.t["stSK"], c.t["stCQ"], c.t["stG"]
    s_T, s_A, s_V, s_G = (ops.sem("l1_T"), ops.sem("l1_A"), ops.sem("l1_V"),
                          ops.sem("l1_G"))

    ops.wait("tensor", s_V, max(pv.get(("ZT", 7), 0), pv.get(("Oadd", 7), 0),
                                pv.get(("upd", 7), 0)))
    ops.wait("tensor", s_A, max(pv.get(("Wmev", 7), 0), pv.get(("KKev", 7), 0)))
    ops.wait("tensor", s_V, pv.get(("Woev", 0), 0))

    def mm_proj(t):
        last = None
        for wbuf, bank in ((wq_g, psA), (wk_g, psB), (wv_g, psC)):
            for fc in range(NCH):
                last = t.matmul(
                    bank[:, 0:512],
                    xn_sb[:, ncol(b, fc, cc * 128):ncol(b, fc, cc * 128) + 128],
                    wbuf[:, fc * 512:(fc + 1) * 512],
                    start=(fc == 0), stop=(fc == NCH - 1))
        return last
    vproj = ops.sig("tensor", s_T, mm_proj)
    ops.wait("scalar", s_T, vproj)
    ops.wait("scalar", s_T, max(pv.get(("KtWm", 7), 0), pv.get(("ZV", 7), 0),
                                pv.get(("Wm", 7), 0)))

    def a_pev(a):
        a.activation(out=qg[:], in_=psA[:, 0:512], func=AF.Copy)
        a.activation(out=kg[:], in_=psB[:, 0:512], func=AF.Copy)
        return a.activation(out=vg[:], in_=psC[:, 0:512], func=AF.Copy)
    pev = ops.sig("scalar", s_A, a_pev)

    ops.wait("tensor", s_A, pev)
    ops.wait("tensor", s_A, max(pv.get(("Atev", 7), 0),
                                pv.get(("oTev", 6), 0), pv.get(("oTev", 7), 0)))

    def mm_qkT(t):
        last = None
        for hl in range(8):
            t.transpose(psG[0:64, hl * 128:hl * 128 + 128],
                        qg[:, hl * 64:hl * 64 + 64], ident[:])
            last = t.transpose(psI[0:64, hl * 128:hl * 128 + 128],
                               kg[:, hl * 64:hl * 64 + 64], ident[:])
        return last
    vqkT = ops.sig("tensor", s_T, mm_qkT)
    ops.wait("scalar", s_T, vqkT)
    ops.wait("scalar", s_T, max(pv.get(("QD0", 7), 0), pv.get(("E", 7), 0)))

    def a_qkTev(a):
        last = None
        for hl in range(8):
            a.activation(out=qTs[0:64, hl * 128:hl * 128 + 128],
                         in_=psG[0:64, hl * 128:hl * 128 + 128], func=AF.Copy)
            last = a.activation(out=kTs[0:64, hl * 128:hl * 128 + 128],
                                in_=psI[0:64, hl * 128:hl * 128 + 128],
                                func=AF.Copy)
        return last
    qkTev = ops.sig("scalar", s_A, a_qkTev)

    nv = {}
    for cp in range(4):
        h0, h1 = 2 * cp, 2 * cp + 1
        qT0 = qTs[0:64, h0 * 128:h0 * 128 + 128]
        qT1 = qTs[0:64, h1 * 128:h1 * 128 + 128]
        kT0 = kTs[0:64, h0 * 128:h0 * 128 + 128]
        kT1 = kTs[0:64, h1 * 128:h1 * 128 + 128]
        qg0, qg1 = qg[:, h0 * 64:h0 * 64 + 64], qg[:, h1 * 64:h1 * 64 + 64]
        kg0, kg1 = kg[:, h0 * 64:h0 * 64 + 64], kg[:, h1 * 64:h1 * 64 + 64]
        vg0, vg1 = vg[:, h0 * 64:h0 * 64 + 64], vg[:, h1 * 64:h1 * 64 + 64]
        sk0 = stSK[0:64, h0 * 64:h0 * 64 + 64]
        sk1 = stSK[0:64, h1 * 64:h1 * 64 + 64]
        cq0 = stCQ[0:64, h0 * 64:h0 * 64 + 64]
        cq1 = stCQ[0:64, h1 * 64:h1 * 64 + 64]
        g0 = stG[0:64, h0 * 64:h0 * 64 + 64]
        g1 = stG[0:64, h1 * 64:h1 * 64 + 64]

        def gp(key, d=1):
            if cp - d >= 0:
                return nv.get((key, cp - d), 0)
            return pv.get((key, cp - d + 8), 0)

        ops.wait("tensor", s_A, max(qkTev, gp("AHmev")))
        nv[("A", cp)] = ops.sig(
            "tensor", s_T,
            lambda t, qT0=qT0, kT0=kT0, qT1=qT1, kT1=kT1: (
                t.matmul(psD[:, 0:128], qT0, kT0, start=True, stop=True),
                t.matmul(psH[:, 0:128], qT1, kT1, start=True, stop=True))[-1])
        ops.wait("tensor", s_V, max(pv.get(("upd", cp), 0), gp("D0ev")))
        nv[("Hm", cp)] = ops.sig(
            "tensor", s_T,
            lambda t, sk0=sk0, qT0=qT0, sk1=sk1, qT1=qT1: (
                t.matmul(psF[0:64, 0:128], sk0, qT0, start=True, stop=True),
                t.matmul(psF[0:64, 256:384], sk1, qT1, start=True, stop=True))[-1])
        ops.wait("scalar", s_T, max(nv[("Hm", cp)], gp("At"), gp("E")))
        ops.wait("scalar", s_V, gp("bt"))

        def a_ahm(a):
            a.activation(out=Asb[:, 0:128], in_=psD[:, 0:128], func=AF.Copy)
            a.activation(out=Asb[:, 128:256], in_=psH[:, 0:128], func=AF.Copy)
            a.activation(out=Hmsb[0:64, 0:128], in_=psF[0:64, 0:128],
                         func=AF.Copy)
            return a.activation(out=Hmsb[0:64, 128:256],
                                in_=psF[0:64, 256:384], func=AF.Copy)
        nv[("AHmev", cp)] = ops.sig("scalar", s_A, a_ahm)
        ops.wait("tensor", s_A, max(nv[("AHmev", cp)], gp("Eev")))
        nv[("E", cp)] = ops.sig(
            "tensor", s_T,
            lambda t, qT0=qT0, qT1=qT1: (
                t.matmul(psD[:, 128:256], qT0, Hmsb[0:64, 0:128],
                         start=True, stop=True),
                t.matmul(psH[:, 128:256], qT1, Hmsb[0:64, 128:256],
                         start=True, stop=True))[-1])
        ops.wait("scalar", s_T, nv[("E", cp)])
        ops.wait("scalar", s_V, gp("ZT"))

        def a_eev(a):
            a.activation(out=Esb[:, 0:128], in_=psD[:, 128:256], func=AF.Copy)
            return a.activation(out=Esb[:, 128:256], in_=psH[:, 128:256],
                                func=AF.Copy)
        nv[("Eev", cp)] = ops.sig("scalar", s_A, a_eev)
        ops.wait("tensor", s_A, max(nv[("AHmev", cp)], gp("Atev")))
        nv[("At", cp)] = ops.sig(
            "tensor", s_T,
            lambda t: (t.transpose(psG[:, 0:128], Asb[:, 0:128], ident[:]),
                       t.transpose(psI[:, 0:128], Asb[:, 128:256],
                                   ident[:]))[-1])
        ops.wait("scalar", s_T, max(nv[("At", cp)], gp("FT")))
        ops.wait("scalar", s_V, gp("bt"))
        nv[("Atev", cp)] = ops.sig(
            "scalar", s_A,
            lambda a: (a.activation(out=Atsb[:, 0:128], in_=psG[:, 0:128],
                                    func=AF.Copy),
                       a.activation(out=Atsb[:, 128:256], in_=psI[:, 0:128],
                                    func=AF.Copy))[-1])
        ops.wait("vector", s_A, nv[("Atev", cp)])
        ops.wait("vector", s_T, max(gp("FT"), gp("Wm")))

        def v_bt(v):
            v.tensor_tensor(out=Btsb[:, 0:128], in0=Atsb[:, 0:128],
                            in1=m_tiu[:], op=ALU.mult)
            v.tensor_tensor(out=Btsb[:, 128:256], in0=Atsb[:, 128:256],
                            in1=m_tiu[:], op=ALU.mult)
            v.tensor_tensor(out=Awmsb[:, 0:128], in0=Asb[:, 0:128],
                            in1=m_upp[:], op=ALU.mult)
            return v.tensor_tensor(out=Awmsb[:, 128:256], in0=Asb[:, 128:256],
                                   in1=m_upp[:], op=ALU.mult)
        nv[("bt", cp)] = ops.sig("vector", s_V, v_bt)
        ops.wait("tensor", s_V, max(nv[("bt", cp)], gp("ZT")))
        nv[("FT", cp)] = ops.sig(
            "tensor", s_T,
            lambda t: (t.matmul(psA[:, 0:128], Btsb[:, 0:128], Atsb[:, 0:128],
                                start=True, stop=True),
                       t.matmul(psB[:, 0:128], Btsb[:, 128:256],
                                Atsb[:, 128:256], start=True, stop=True))[-1])
        ops.wait("tensor", s_A, gp("Wmev"))
        nv[("Wm", cp)] = ops.sig(
            "tensor", s_T,
            lambda t, vg0=vg0, vg1=vg1: (
                t.matmul(psA[:, 192:256], Awmsb[:, 0:128], vg0,
                         start=True, stop=True),
                t.matmul(psB[:, 192:256], Awmsb[:, 128:256], vg1,
                         start=True, stop=True))[-1])
        ops.wait("scalar", s_T, max(nv[("Wm", cp)], gp("KtWm")))
        nv[("Wmev", cp)] = ops.sig(
            "scalar", s_A,
            lambda a: (a.activation(out=Wmsb[:, 0:64], in_=psA[:, 192:256],
                                    func=AF.Copy),
                       a.activation(out=Wmsb[:, 64:128], in_=psB[:, 192:256],
                                    func=AF.Copy))[-1])
        ops.wait("tensor", s_A, gp("KKev"))
        ops.wait("tensor", s_V, gp("upd"))
        nv[("KQ", cp)] = ops.sig(
            "tensor", s_T,
            lambda t, kg0=kg0, kg1=kg1, qg0=qg0, qg1=qg1, vg0=vg0, vg1=vg1: (
                t.matmul(psC[0:64, 0:64], kg0, kg0, start=True, stop=True),
                t.matmul(psC[0:64, 64:128], qg0, vg0, start=True, stop=True),
                t.matmul(psC[0:64, 192:256], kg1, kg1, start=True, stop=True),
                t.matmul(psC[0:64, 256:320], qg1, vg1,
                         start=True, stop=True))[-1])
        ops.wait("scalar", s_T, max(nv[("KQ", cp)], gp("KKC")))
        nv[("KKev", cp)] = ops.sig(
            "scalar", s_A,
            lambda a: (a.activation(out=KKsb[0:64, 0:64], in_=psC[0:64, 0:64],
                                    func=AF.Copy),
                       a.activation(out=KKsb[0:64, 64:128],
                                    in_=psC[0:64, 192:256], func=AF.Copy))[-1])
        ops.wait("tensor", s_V, gp("D0ev"))
        nv[("SC", cp)] = ops.sig(
            "tensor", s_T,
            lambda t, sk0=sk0, cq0=cq0, sk1=sk1, cq1=cq1: (
                t.matmul(psF[0:64, 128:192], sk0, cq0, start=True, stop=True),
                t.matmul(psF[0:64, 384:448], sk1, cq1,
                         start=True, stop=True))[-1])
        ops.wait("vector", s_T, max(nv[("SC", cp)], gp("QD0")))

        def v_d0(v, g0=g0, g1=g1):
            v.tensor_tensor(out=D0sb[0:64, 0:64], in0=psF[0:64, 128:192],
                            in1=g0, op=ALU.subtract)
            return v.tensor_tensor(out=D0sb[0:64, 64:128],
                                   in0=psF[0:64, 384:448], in1=g1,
                                   op=ALU.subtract)
        nv[("D0ev", cp)] = ops.sig("vector", s_V, v_d0)
        ops.wait("tensor", s_V, max(nv[("D0ev", cp)], gp("Oadd")))
        nv[("QD0", cp)] = ops.sig(
            "tensor", s_T,
            lambda t, qT0=qT0, qT1=qT1: (
                t.matmul(psD[:, 256:320], qT0, D0sb[0:64, 0:64],
                         start=True, stop=True),
                t.matmul(psH[:, 256:320], qT1, D0sb[0:64, 64:128],
                         start=True, stop=True))[-1])
        ops.wait("vector", s_A, nv[("Eev", cp)])
        ops.wait("vector", s_T, max(nv[("FT", cp)], gp("ZV")))

        def v_zt(v):
            v.tensor_tensor(out=ZTsb[:, 0:128], in0=psA[:, 0:128],
                            in1=Esb[:, 0:128], op=ALU.add)
            v.tensor_tensor(out=ZTsb[:, 128:256], in0=psB[:, 0:128],
                            in1=Esb[:, 128:256], op=ALU.add)
            v.drain()
            v.tensor_tensor(out=ZTsb[:, 0:128], in0=ZTsb[:, 0:128],
                            in1=m_tiu[:], op=ALU.mult)
            return v.tensor_tensor(out=ZTsb[:, 128:256], in0=ZTsb[:, 128:256],
                                   in1=m_tiu[:], op=ALU.mult)
        nv[("ZT", cp)] = ops.sig("vector", s_V, v_zt)
        ops.wait("tensor", s_V, nv[("ZT", cp)])
        ops.wait("tensor", s_A, gp("ZVev"))
        nv[("ZV", cp)] = ops.sig(
            "tensor", s_T,
            lambda t, vg0=vg0, vg1=vg1: (
                t.matmul(psA[:, 128:192], ZTsb[:, 0:128], vg0,
                         start=True, stop=True),
                t.matmul(psB[:, 128:192], ZTsb[:, 128:256], vg1,
                         start=True, stop=True))[-1])
        ops.wait("scalar", s_T, max(nv[("ZV", cp)], gp("oT")))
        nv[("ZVev", cp)] = ops.sig(
            "scalar", s_A,
            lambda a: (a.activation(out=osb[:, 0:64], in_=psA[:, 128:192],
                                    func=AF.Copy),
                       a.activation(out=osb[:, 64:128], in_=psB[:, 128:192],
                                    func=AF.Copy))[-1])
        ops.wait("vector", s_A, nv[("ZVev", cp)])
        ops.wait("vector", s_T, nv[("QD0", cp)])

        def v_oadd(v):
            v.tensor_tensor(out=osb[:, 0:64], in0=psD[:, 256:320],
                            in1=osb[:, 0:64], op=ALU.add)
            return v.tensor_tensor(out=osb[:, 64:128], in0=psH[:, 256:320],
                                   in1=osb[:, 64:128], op=ALU.add)
        nv[("Oadd", cp)] = ops.sig("vector", s_V, v_oadd)
        ops.wait("tensor", s_A, max(nv[("Wmev", cp)], nv[("KKev", cp)]))
        nv[("KtWm", cp)] = nv[("KKC", cp)] = ops.sig(
            "tensor", s_T,
            lambda t, kg0=kg0, kg1=kg1, cq0=cq0, cq1=cq1: (
                t.matmul(psC[0:64, 128:192], kg0, Wmsb[:, 0:64],
                         start=True, stop=True),
                t.matmul(psC[0:64, 320:384], kg1, Wmsb[:, 64:128],
                         start=True, stop=True),
                t.matmul(psF[0:64, 192:256], KKsb[0:64, 0:64], cq0,
                         start=True, stop=True),
                t.matmul(psF[0:64, 448:512], KKsb[0:64, 64:128], cq1,
                         start=True, stop=True))[-1])
        ops.wait("vector", s_T, nv[("KKC", cp)])

        def v_upd(v, sk0=sk0, sk1=sk1, cq0=cq0, cq1=cq1, g0=g0, g1=g1):
            v.tensor_tensor(out=sk0, in0=sk0, in1=psC[0:64, 0:64], op=ALU.add)
            v.tensor_tensor(out=sk1, in0=sk1, in1=psC[0:64, 192:256],
                            op=ALU.add)
            v.tensor_tensor(out=g0, in0=g0, in1=psF[0:64, 192:256], op=ALU.add)
            v.tensor_tensor(out=g1, in0=g1, in1=psF[0:64, 448:512], op=ALU.add)
            v.drain()
            v.tensor_tensor(out=g0, in0=g0, in1=psC[0:64, 128:192], op=ALU.add)
            v.tensor_tensor(out=g1, in0=g1, in1=psC[0:64, 320:384], op=ALU.add)
            v.tensor_tensor(out=cq0, in0=cq0, in1=psC[0:64, 64:128],
                            op=ALU.add)
            return v.tensor_tensor(out=cq1, in0=cq1, in1=psC[0:64, 256:320],
                                   op=ALU.add)
        nv[("upd", cp)] = ops.sig("vector", s_V, v_upd)
        ops.wait("tensor", s_V, nv[("Oadd", cp)])
        ops.wait("tensor", s_A, gp("oTev", 2))
        pTo = psG if cp % 2 == 0 else psI
        nv[("oT", cp)] = ops.sig(
            "tensor", s_T,
            lambda t, pTo=pTo: t.transpose(pTo[:, 128:256], osb[:], ident[:]))
        ops.wait("scalar", s_T, nv[("oT", cp)])
        nv[("oTev", cp)] = ops.sig(
            "scalar", s_A,
            lambda a, cp=cp, pTo=pTo: a.activation(
                out=oTs[:, cp * 128:cp * 128 + 128], in_=pTo[:, 128:256],
                func=AF.Copy))

    ops.wait("tensor", s_A, nv[("oTev", 3)])
    ops.wait("tensor", s_V, max(nv[("Oadd", 3)], nv[("ZT", 3)]))

    def mm_wo(t):
        last = None
        for dch in range(8):
            bank, col = (psD, dch * 128) if dch < 4 else (psH, (dch - 4) * 128)
            for cp in range(4):
                last = t.matmul(
                    bank[:, col:col + 128],
                    wo_g[:, cp * 1024 + dch * 128:cp * 1024 + dch * 128 + 128],
                    oTs[:, cp * 128:cp * 128 + 128],
                    start=(cp == 0), stop=(cp == 3))
        return last
    nv[("Wo", 0)] = ops.sig("tensor", s_T, mm_wo)
    ops.wait("vector", s_T, nv[("Wo", 0)])

    def v_woev(v):
        last = None
        for dch in range(8):
            bank, col = (psD, dch * 128) if dch < 4 else (psH, (dch - 4) * 128)
            xs = x_sb[:, xcol(b, dch, cc * 128):xcol(b, dch, cc * 128) + 128]
            last = v.scalar_tensor_tensor(out=xs, in0=bank[:, col:col + 128],
                                          scalar=1.0, in1=xs,
                                          op0=ALU.mult, op1=ALU.add)
        return last
    nv[("Woev", 0)] = ops.sig("vector", s_V, v_woev)

    pv.clear()
    for (k2, i2), v2 in nv.items():
        pv[(k2, i2 + 4 if isinstance(i2, int) and k2 not in ("Wo", "Woev")
            else i2)] = v2


def make_in_maps(inputs):
    t_cs, t_sn, t_scale = make_tables()
    in_maps = []
    for core in range(8):
        m = {
            "x_in": np.ascontiguousarray(inputs["L0_attn_x"][2 * core:2 * core + 2]),
            "W0": inputs["L0_attn_W"],
            "k_norm_w": inputs["L0_attn_k_norm_w"],
            "o_norm_w": inputs["L0_attn_o_norm_w"],
            "o_norm_b": inputs["L0_attn_o_norm_b"],
            "eta_raw": inputs["L0_attn_eta_raw"],
            "lam_logit": inputs["L0_attn_lam_logit"],
            "wqkv": inputs["L0_ffn_wqkv"],
            "q_scale": inputs["L0_ffn_q_scale"],
            "k_scale": inputs["L0_ffn_k_scale"],
            "wo": inputs["L0_ffn_wo"],
            "ln_g": inputs["L0_ffn_ln_g"],
            "ln_b": inputs["L0_ffn_ln_b"],
            "w1": inputs["L0_ffn_w1"],
            "b1": inputs["L0_ffn_b1"],
            "w2": inputs["L0_ffn_w2"],
            "b2": inputs["L0_ffn_b2"],
            "Wq1": inputs["L1_attn_W_q"],
            "Wk1": inputs["L1_attn_W_k"],
            "Wv1": inputs["L1_attn_W_v"],
            "Wo1": inputs["L1_attn_W_o"],
            "Wf1": inputs["L1_ffn_weight"],
            "bf1": inputs["L1_ffn_bias"],
            "t_cs": t_cs, "t_sn": t_sn, "t_scale": t_scale,
        }
        in_maps.append({k: np.ascontiguousarray(v) for k, v in m.items()})
    return in_maps


def kernel(**inputs):
    nc = build_program()
    in_maps = make_in_maps(inputs)
    res = run_bass_kernel_spmd(nc, in_maps, core_ids=list(range(8)))
    return np.concatenate([r["y"] for r in res.results], axis=0)
